# version 11
# baseline (speedup 1.0000x reference)
import numpy as np
import ml_dtypes
from contextlib import ExitStack
import concourse.bass as bass
import concourse.mybir as mybir
from concourse.bass_utils import run_bass_kernel_spmd
from concourse.aot_env import allow_ici_all_to_all

BFNP = ml_dtypes.bfloat16
F32 = mybir.dt.float32
I32 = mybir.dt.int32
BF16 = mybir.dt.bfloat16
AF = mybir.ActivationFunctionType
ALU = mybir.AluOpType
AX = mybir.AxisListType

T, C, NCORE, TLOC = 16384, 1024, 8, 2048
DI, DS, DTR = 2048, 16, 64
DSL = DI // NCORE
DNL = DSL * DS

ENGS = ("sync", "tensor", "vector", "scalar", "gpsimd")


class Plan:

    def __init__(self, nc):
        self.nc = nc
        self.q = {e: [] for e in ENGS}

    def add(self, eng, fn):
        self.q[eng].append(fn)

    def emit(self):
        if not any(self.q.values()):
            return
        with self.nc.Block() as blk:
            for eng in ENGS:
                fns = self.q[eng]
                if fns:
                    getattr(blk, eng)(lambda e, fns=fns: [f(e) for f in fns])
        self.q = {e: [] for e in ENGS}


class Sem:
    def __init__(self, handle):
        self.h = handle
        self.v = 0

    def bump(self, n):
        self.v += n
        return self.v


def f32c(a):
    return np.ascontiguousarray(np.asarray(a, dtype=np.float32))


def dram_in(nc, fu, name, shape, dt):
    if fu is not None and name in fu["ext"]:
        return fu["ext"][name]
    pfx = fu["pfx"] if fu is not None else ""
    return nc.dram_tensor(pfx + name, shape, dt, kind="ExternalInput").ap()


def dram_out(nc, fu, name, shape, dt):
    if fu is not None and name in fu["ext"]:
        return fu["ext"][name]
    pfx = fu["pfx"] if fu is not None else ""
    return nc.dram_tensor(pfx + name, shape, dt, kind="ExternalOutput").ap()


def fu_start(nc, fu, E):
    if fu is not None:
        return fu["p"], fu["sems"], fu["ctrs"]
    p = Plan(nc)
    hs = [E(nc.semaphore(f"s{i}")) for i in range(5)]
    cs = [Sem(h) for h in hs]
    return p, tuple(hs), tuple(cs)


def two(dram_ap, g=2, p=128):
    return dram_ap.rearrange("(g p) t -> p g t", p=p)



def build_l0a_pre(fu=None):
    nc = fu["nc"] if fu is not None else bass.Bass(
        "TRN2", target_bir_lowering=False, detect_race_conditions=False)
    pfx = fu["pfx"] if fu is not None else ""
    xT = dram_in(nc, fu, "xT", [C, T], F32)
    wT = dram_in(nc, fu, "wT", [C, 512], F32)
    cwp = dram_in(nc, fu, "cwp", [128, 8], F32)
    cbp = dram_in(nc, fu, "cbp", [128, 2], F32)
    wxT = dram_in(nc, fu, "wxT", [DSL, 97], F32)

    o_u = dram_out(nc, fu, "o_u", [DSL, T], BF16)
    o_z = dram_out(nc, fu, "o_z", [DSL, T], BF16)
    o_xdblp = dram_out(nc, fu, "o_xdblp", [97, T], F32)

    TCH = 2048
    NCH = T // TCH

    with ExitStack() as _st:
        E = _st.enter_context
        swT = E(nc.sbuf_tensor(pfx + "swT", [128, 8, 512], BF16))
        scw = E(nc.sbuf_tensor(pfx + "scw", [128, 8], F32))
        scb = E(nc.sbuf_tensor(pfx + "scb", [128, 2], F32))
        swxT = E(nc.sbuf_tensor(pfx + "swxT", [128, 2, 97], BF16))
        xb0 = E(nc.sbuf_tensor(pfx + "xb0", [128, 8, TCH], BF16))
        xb1 = E(nc.sbuf_tensor(pfx + "xb1", [128, 8, TCH], BF16))
        xib = E(nc.sbuf_tensor(pfx + "xib", [128, 2, TCH + 3], BF16))
        zb = E(nc.sbuf_tensor(pfx + "zb", [128, 2, TCH], BF16))
        upre = E(nc.sbuf_tensor(pfx + "upre", [128, 2, TCH], F32))
        ub = E(nc.sbuf_tensor(pfx + "ub", [128, 2, TCH], BF16))
        xdv = E(nc.sbuf_tensor(pfx + "xdv", [97, TCH], F32))
        psA = E(nc.psum_tensor(pfx + "psA", [128, 512], F32))
        psB = E(nc.psum_tensor(pfx + "psB", [128, 512], F32))
        psX = E(nc.psum_tensor(pfx + "psX", [97, 512], F32))
        p, (_sx, _sg, _smm, _sev, _so), (s_x, s_g, s_mm, s_ev, s_o) = fu_start(nc, fu, E)
        ps = (psA, psB)

        p.add("gpsimd", lambda g: (
            g.dma_start(out=swT[:, :, :], in_=wT.rearrange("(k p) j -> p k j", p=128))
            .then_inc(_sg, 16),
            g.dma_start(out=swxT[:, :, :], in_=two(wxT)).then_inc(_sg, 16)))
        p.add("sync", lambda s: (
            s.dma_start(out=scw[:, :], in_=cwp).then_inc(_sx, 16),
            s.dma_start(out=scb[:, :], in_=cbp).then_inc(_sx, 16)))
        s_x.bump(32)
        s_g.bump(32)
        p.add("gpsimd", lambda g, w=s_x.v, wg=s_g.v: (
            g.wait_ge(_sx, w), g.wait_ge(_sg, wg),
            g.memset(xib[:, :, 0:3], 0.0)))
        p.emit()

        gctr = [0]
        for ch in range(NCH):
            xb = (xb0, xb1)[ch % 2]
            p.add("gpsimd", lambda gp, ch=ch, xb=xb: (
                gp.dma_start(out=xb[:, :, :],
                             in_=xT[:, ch * TCH:(ch + 1) * TCH]
                             .rearrange("(k p) t -> p k t", p=128)).then_inc(_sg, 16)))
            s_g.bump(16)
            xgw = s_g.v
            p.emit()

            for oc in range(4):
                for tg in range(4):
                    gi = gctr[0]
                    pst = ps[gi % 2]

                    def mm(t, oc=oc, tg=tg, gi=gi, pst=pst, xb=xb, xgw=xgw):
                        t.wait_ge(_sg, xgw)
                        if gi >= 2:
                            t.wait_ge(_sev, gi - 1)
                        for k in range(7):
                            t.matmul(pst[:, :], swT[:, k, oc * 128:(oc + 1) * 128],
                                     xb[:, k, tg * 512:(tg + 1) * 512],
                                     start=(k == 0), stop=False)
                        t.matmul(pst[:, :], swT[:, 7, oc * 128:(oc + 1) * 128],
                                 xb[:, 7, tg * 512:(tg + 1) * 512],
                                 start=False, stop=True).then_inc(_smm, 1)
                    p.add("tensor", mm)
                    s_mm.bump(1)
                    mmv = s_mm.v

                    def ev(v, oc=oc, tg=tg, pst=pst, mmv=mmv):
                        v.wait_ge(_smm, mmv)
                        dst = (xib[:, oc, 3 + tg * 512:3 + (tg + 1) * 512] if oc < 2
                               else zb[:, oc - 2, tg * 512:(tg + 1) * 512])
                        v.tensor_copy(dst, pst[:, :]).then_inc(_sev, 1)
                    p.add("vector", ev)
                    s_ev.bump(1)
                    gctr[0] += 1
            p.emit()

            def conv(v):
                for gr in range(2):
                    v.tensor_scalar(upre[:, gr, :], xib[:, gr, 3:3 + TCH],
                                    scw[:, gr * 4 + 3:gr * 4 + 4], None, ALU.mult)
                    for j in range(3):
                        v.scalar_tensor_tensor(upre[:, gr, :], xib[:, gr, j:j + TCH],
                                               scw[:, gr * 4 + j:gr * 4 + j + 1],
                                               upre[:, gr, :], ALU.mult, ALU.add)
            p.add("vector", conv)
            p.emit()
            p.add("gpsimd", lambda g: g.tensor_copy(xib[:, :, 0:3], xib[:, :, TCH:TCH + 3]))
            p.emit()

            def act(s):
                for gr in range(2):
                    s.activation(ub[:, gr, :], upre[:, gr, :], AF.Silu,
                                 bias=scb[:, gr:gr + 1], scale=1.0)
            p.add("scalar", act)
            p.emit()

            for tg in range(4):
                def mmx(t, tg=tg):
                    t.matmul(psX[:, :], swxT[:, 0, :], ub[:, 0, tg * 512:(tg + 1) * 512],
                             start=True, stop=False)
                    t.matmul(psX[:, :], swxT[:, 1, :], ub[:, 1, tg * 512:(tg + 1) * 512],
                             start=False, stop=True).then_inc(_smm, 1)
                p.add("tensor", mmx)
                s_mm.bump(1)
                mmv = s_mm.v

                def evx(v, tg=tg, mmv=mmv):
                    v.wait_ge(_smm, mmv)
                    v.tensor_copy(xdv[:, tg * 512:(tg + 1) * 512], psX[:, :])
                p.add("vector", evx)
                p.emit()

            def outs(s, ch=ch, w=s_o.v + 48):
                s.dma_start(out=o_u[:, ch * TCH:(ch + 1) * TCH].rearrange("(g p) t -> p g t", p=128),
                            in_=ub[:, :, :]).then_inc(_so, 16)
                s.dma_start(out=o_z[:, ch * TCH:(ch + 1) * TCH].rearrange("(g p) t -> p g t", p=128),
                            in_=zb[:, :, :]).then_inc(_so, 16)
                s.dma_start(out=o_xdblp[:, ch * TCH:(ch + 1) * TCH], in_=xdv[:, :]).then_inc(_so, 16)
                s.wait_ge(_so, w)
            p.add("sync", outs)
            s_o.bump(48)
            p.emit()
    return nc


def inputs_l0a_pre(ins, core):
    W_in = ins["L0_attn_W_in"]
    sd = slice(DSL * core, DSL * (core + 1))
    wT = np.concatenate([W_in[sd].T, W_in[DI + DSL * core:DI + DSL * (core + 1)].T], axis=1)
    cw = ins["L0_attn_conv_w"][sd]
    cwp = np.concatenate([cw[:128], cw[128:]], axis=1)
    cb = ins["L0_attn_conv_b"][sd]
    cbp = np.stack([cb[:128], cb[128:]], axis=1)
    return {
        "xT": f32c(ins["L0_attn_x"][0].T),
        "wT": f32c(wT),
        "cwp": f32c(cwp),
        "cbp": f32c(cbp),
        "wxT": f32c(np.concatenate([ins["L0_attn_W_x"][:, sd].T,
                                    ins["L0_attn_Dp"][sd][:, None]], axis=1)),
    }



def build_l0a_scan(stage=3, parts=15, fu=None):
    nc = fu["nc"] if fu is not None else bass.Bass(
        "TRN2", target_bir_lowering=False, detect_race_conditions=False)
    pfx = fu["pfx"] if fu is not None else ""
    xdblp = dram_in(nc, fu, "xdblp", [NCORE, 97, T], F32)
    u_in = dram_in(nc, fu, "u_in", [DSL, T], BF16)
    z_in = dram_in(nc, fu, "z_in", [DSL, T], BF16)
    wdtT = dram_in(nc, fu, "wdtT", [DTR, DSL], F32)
    bdtp = dram_in(nc, fu, "bdtp", [128, 2], F32)
    alogc = dram_in(nc, fu, "alogc", [128, 32], F32)
    alogs = dram_in(nc, fu, "alogs", [DS, DSL], F32)
    dpp = dram_in(nc, fu, "dpp", [128, 2], F32)
    selS = dram_in(nc, fu, "selS", [128, 16 * 128], F32)
    selR = dram_in(nc, fu, "selR", [128, 16 * 128], F32)

    o_ycat = dram_out(nc, fu, "o_ycat", [512, T], BF16)
    o_svec = dram_out(nc, fu, "o_svec", [1, T], F32)

    d_fm = nc.dram_tensor(pfx + "d_fm", [DSL, T], BF16).ap()
    du_fm = nc.dram_tensor(pfx + "du_fm", [DSL, T], BF16).ap()
    y_fm = nc.dram_tensor(pfx + "y_fm", [DSL, T], BF16).ap()

    PW = 1024
    NP = T // PW
    SP = 2048
    NS = T // SP

    with ExitStack() as _st:
        E = _st.enter_context
        sxdb = E(nc.sbuf_tensor(pfx + "sxdb", [97, T], BF16))
        brep = E(nc.sbuf_tensor(pfx + "brep", [128, T], BF16))
        crep = E(nc.sbuf_tensor(pfx + "crep", [128, T], BF16))
        swdtT = E(nc.sbuf_tensor(pfx + "swdtT", [DTR, DSL], BF16))
        sbdt = E(nc.sbuf_tensor(pfx + "sbdt", [128, 2], F32))
        sdp = E(nc.sbuf_tensor(pfx + "sdp", [128, 2], F32))
        sA = E(nc.sbuf_tensor(pfx + "sA", [128, 32], F32))
        sAs = E(nc.sbuf_tensor(pfx + "sAs", [DS, DSL], BF16))
        sselS = E(nc.sbuf_tensor(pfx + "sselS", [128, 16, 128], BF16))
        sselR = E(nc.sbuf_tensor(pfx + "sselR", [128, 16, 128], BF16))
        dtmp = E(nc.sbuf_tensor(pfx + "dtmp", [128, 2, SP], F32))
        dtmp2 = E(nc.sbuf_tensor(pfx + "dtmp2", [128, 2, SP], F32))
        dtb = E(nc.sbuf_tensor(pfx + "dtb", [128, 2, SP], BF16))
        ubch = E(nc.sbuf_tensor(pfx + "ubch", [128, 2, SP], BF16))
        ybch = E(nc.sbuf_tensor(pfx + "ybch", [128, 2, SP], BF16))
        dsl = E(nc.sbuf_tensor(pfx + "dsl", [128, 2, PW], BF16))
        dusl = E(nc.sbuf_tensor(pfx + "dusl", [128, 2, PW], BF16))
        dab = E(nc.sbuf_tensor(pfx + "dab", [128, PW + 1], F32))
        d1b = E(nc.sbuf_tensor(pfx + "d1b", [128, PW + 1], F32))
        sst = E(nc.sbuf_tensor(pfx + "sst", [128, PW + 1], F32))
        scb2 = E(nc.sbuf_tensor(pfx + "scb2", [128, PW], BF16))
        stcol = E(nc.sbuf_tensor(pfx + "stcol", [128, 32], F32))
        yev = E(nc.sbuf_tensor(pfx + "yev", [128, 2, PW], BF16))
        yacc = E(nc.sbuf_tensor(pfx + "yacc", [128, 2, PW], F32))
        onesr = E(nc.sbuf_tensor(pfx + "onesr", [1, 128], BF16))
        psD = E(nc.psum_tensor(pfx + "psD", [128, PW], F32))
        psU = E(nc.psum_tensor(pfx + "psU", [128, PW], F32))
        psY0 = E(nc.psum_tensor(pfx + "psY0", [128, PW], F32))
        psY1 = E(nc.psum_tensor(pfx + "psY1", [128, PW], F32))
        p, (_sx, _sg, _smm, _sev, _so), (s_x, s_g, s_mm, s_ev, s_o) = fu_start(nc, fu, E)

        with (nc.sbuf_tensor(pfx + "sxd", [97, T // 16], F32) as sxd,
              nc.sbuf_tensor(pfx + "xdst", [97, T // 16], F32) as xdst):
            def loads(s):
                s.dma_start(out=sbdt[:, :], in_=bdtp).then_inc(_sx, 16)
                s.dma_start(out=sdp[:, :], in_=dpp).then_inc(_sx, 16)
                s.dma_start(out=sA[:, :], in_=alogc).then_inc(_sx, 16)
            p.add("sync", loads)

            def loads2(g):
                g.dma_start(out=swdtT[:, :], in_=wdtT).then_inc(_sg, 16)
                g.dma_start(out=sAs[:, :], in_=alogs).then_inc(_sg, 16)
                g.dma_start(out=sselS[:, :, :],
                            in_=selS.rearrange("p (s j) -> p s j", s=16)).then_inc(_sg, 16)
                g.dma_start(out=sselR[:, :, :],
                            in_=selR.rearrange("p (s j) -> p s j", s=16)).then_inc(_sg, 16)
            p.add("gpsimd", loads2)
            s_x.bump(16 * 3)
            s_g.bump(16 * 4)
            p.emit()

            for hh in range(16):
                tsp = slice(hh * (T // 16), (hh + 1) * (T // 16))
                p.add("sync", lambda s, tsp=tsp: s.dma_start(
                    out=sxd[:, :], in_=xdblp[0][:, tsp]).then_inc(_sx, 16))
                s_x.bump(16)
                p.add("vector", lambda v, w=s_x.v: v.wait_ge(_sx, w))
                p.emit()
                for r in range(1, NCORE):
                    p.add("sync", lambda s, r=r, tsp=tsp: s.dma_start(
                        out=xdst[:, :], in_=xdblp[r][:, tsp]).then_inc(_sx, 16))
                    s_x.bump(16)
                    p.add("vector", lambda v, w=s_x.v: (
                        v.wait_ge(_sx, w),
                        v.tensor_add(sxd[:, :], sxd[:, :], xdst[:, :])))
                    p.emit()
                p.add("vector", lambda v, tsp=tsp: v.tensor_copy(sxdb[:, tsp], sxd[:, :]))
                p.emit()

        def prep_s(s, w=s_x.v, wg=s_g.v):
            s.wait_ge(_sx, w)
            s.wait_ge(_sg, wg)
            s.activation(sA[:, :], sA[:, :], AF.Exp, bias=0.0, scale=1.0)
            s.activation(sAs[:, :], sAs[:, :], AF.Exp, bias=0.0, scale=1.0)
        p.add("scalar", prep_s)
        p.emit()

        def prep_v(v):
            v.tensor_scalar(sAs[:, :], sAs[:, :], -1.0, None, ALU.mult)
            v.memset(onesr[:, :], 1.0)
            v.memset(dab[:, :], 1.0)
        p.add("vector", prep_v)
        p.add("gpsimd", lambda g: (
            g.tensor_scalar(sA[:, :], sA[:, :], -1.0, None, ALU.mult),
            g.memset(stcol[:, :], 0.0)))
        p.emit()

        def repl_bc(s, w=s_x.v + 16 * 16):
            for j in range(8):
                s.dma_start(out=brep[j * 16:(j + 1) * 16, :], in_=sxdb[64:80, :]).then_inc(_sx, 16)
                s.dma_start(out=crep[j * 16:(j + 1) * 16, :], in_=sxdb[80:96, :]).then_inc(_sx, 16)
            s.wait_ge(_sx, w)
        p.add("sync", repl_bc)
        s_x.bump(16 * 16)
        p.emit()

        for sp in range(NS):
            tsp = slice(sp * SP, (sp + 1) * SP)
            p.add("sync", lambda s, tsp=tsp: s.dma_start(
                out=ubch[:, :, :], in_=two(u_in)[:, :, tsp]).then_inc(_sx, 16))
            s_x.bump(16)
            uw = s_x.v
            p.emit()

            for gr in range(2):
                for tg in range(SP // 512):
                    def mmd(t, gr=gr, tg=tg, sp=sp):
                        t.matmul(psD[:, 0:512], swdtT[:, gr * 128:(gr + 1) * 128],
                                 sxdb[0:64, sp * SP + tg * 512:sp * SP + (tg + 1) * 512],
                                 start=True, stop=True).then_inc(_smm, 1)
                    p.add("tensor", mmd)
                    s_mm.bump(1)
                    mmv = s_mm.v

                    def evd(s, gr=gr, tg=tg, mmv=mmv):
                        s.wait_ge(_smm, mmv)
                        s.activation(dtmp[:, gr, tg * 512:(tg + 1) * 512], psD[:, 0:512],
                                     AF.Abs, bias=sbdt[:, gr:gr + 1], scale=1.0)
                        s.activation(dtmp2[:, gr, tg * 512:(tg + 1) * 512], psD[:, 0:512],
                                     AF.Relu, bias=sbdt[:, gr:gr + 1], scale=1.0)
                    p.add("scalar", evd)
                    p.emit()

            def sp1(s):
                for gr in range(2):
                    s.activation(dtmp[:, gr, :], dtmp[:, gr, :], AF.Exp, bias=0.0, scale=-1.0)
                    s.activation(dtmp[:, gr, :], dtmp[:, gr, :], AF.Ln, bias=1.0, scale=1.0)
            p.add("scalar", sp1)
            p.emit()

            def sp2(v, uw=uw):
                v.wait_ge(_sx, uw)
                for gr in range(2):
                    v.tensor_add(dtmp[:, gr, :], dtmp[:, gr, :], dtmp2[:, gr, :])
                    v.tensor_copy(dtb[:, gr, :], dtmp[:, gr, :])
                    v.tensor_mul(ubch[:, gr, :], dtb[:, gr, :], ubch[:, gr, :])
            p.add("vector", sp2)
            p.emit()

            def dout(s, tsp=tsp, w=s_o.v + 32):
                s.dma_start(out=two(d_fm)[:, :, tsp], in_=dtb[:, :, :]).then_inc(_so, 16)
                s.dma_start(out=two(du_fm)[:, :, tsp], in_=ubch[:, :, :]).then_inc(_so, 16)
                s.wait_ge(_so, w)
            p.add("sync", dout)
            s_o.bump(32)
            p.emit()

        import os as _os
        NPeff = int(_os.environ.get("A3NP", str(NP)))
        for pc in range(NPeff if stage >= 2 else 0):
            tsp = slice(pc * PW, (pc + 1) * PW)
            p.add("sync", lambda s, tsp=tsp: (
                s.dma_start(out=dsl[:, :, :], in_=two(d_fm)[:, :, tsp]).then_inc(_sx, 16),
                s.dma_start(out=dusl[:, :, :], in_=two(du_fm)[:, :, tsp]).then_inc(_sx, 16)))
            s_x.bump(32)
            pw_w = s_x.v
            p.add("vector", lambda v: v.memset(yacc[:, :, :], 0.0))
            p.emit()

            for cc in range(32):
                half = cc // 16
                sel = cc % 16

                def mms(t, half=half, sel=sel, pw_w=pw_w):
                    t.wait_ge(_sx, pw_w)
                    for tg in range(PW // 512):
                        t.matmul(psD[:, tg * 512:(tg + 1) * 512], sselS[:, sel, :],
                                 dsl[:, half, tg * 512:(tg + 1) * 512], start=True, stop=True)
                        mmi = t.matmul(psU[:, tg * 512:(tg + 1) * 512], sselS[:, sel, :],
                                 dusl[:, half, tg * 512:(tg + 1) * 512], start=True, stop=True)
                    mmi.then_inc(_smm, 1)
                p.add("tensor", mms)
                s_mm.bump(1)
                mmv = s_mm.v

                def mkda(s, cc=cc, mmv=mmv):
                    s.wait_ge(_smm, mmv)
                    s.activation(dab[:, 1:PW + 1], psD[:, :], AF.Exp, bias=0.0,
                                 scale=sA[:, cc:cc + 1])
                    s.activation(d1b[:, 0:1], stcol[:, cc:cc + 1], AF.Copy,
                                 bias=0.0, scale=1.0).then_inc(_sev, 1)
                p.add("scalar", mkda)
                s_ev.bump(1)
                dav = s_ev.v

                def mkd1(v, tsp=tsp, mmv=mmv, cc=cc, dav=dav):
                    v.wait_ge(_smm, mmv)
                    v.tensor_tensor(d1b[:, 1:PW + 1], psU[:, :], brep[:, tsp], ALU.mult)
                    v.wait_ge(_sev, dav)
                    v.tensor_tensor_scan(sst[:, :], dab[:, :], d1b[:, :],
                                         0.0, ALU.mult, ALU.add).then_inc(_sev, 1)
                p.add("vector", mkd1)
                s_ev.bump(1)
                scanv = s_ev.v
                p.add("scalar", lambda s, cc=cc, scanv=scanv: (
                    s.wait_ge(_sev, scanv),
                    s.activation(stcol[:, cc:cc + 1], sst[:, PW:PW + 1], AF.Copy,
                                 bias=0.0, scale=1.0)))

                def mksc(g, tsp=tsp, scanv=scanv):
                    g.wait_ge(_sev, scanv)
                    g.tensor_tensor(scb2[:, :], sst[:, 1:PW + 1], crep[:, tsp], ALU.mult).then_inc(_sev, 1)
                p.add("gpsimd", mksc)
                s_ev.bump(1)
                scv = s_ev.v

                psY = (psY0, psY1)[half]

                def mmy(t, sel=sel, psY=psY, scv=scv):
                    t.wait_ge(_sev, scv)
                    for tg in range(PW // 512):
                        mmi = t.matmul(psY[:, tg * 512:(tg + 1) * 512], sselR[:, sel, :],
                                 scb2[:, tg * 512:(tg + 1) * 512],
                                 start=True, stop=True)
                    mmi.then_inc(_smm, 1)
                p.add("tensor", mmy)
                s_mm.bump(1)
                p.add("vector", lambda v, half=half, psY=psY, w=s_mm.v: (
                    v.wait_ge(_smm, w),
                    v.tensor_add(yacc[:, half, :], yacc[:, half, :], psY[:, :])))
                p.emit()

            def evy(s):
                s.activation(yev[:, 0, :], yacc[:, 0, :], AF.Identity, bias=0.0, scale=1.0)
                s.activation(yev[:, 1, :], yacc[:, 1, :], AF.Identity, bias=0.0, scale=1.0)
            p.add("scalar", evy)
            p.emit()
            p.add("sync", lambda s, tsp=tsp, w=s_o.v + 16: (
                s.dma_start(out=two(y_fm)[:, :, tsp], in_=yev[:, :, :]).then_inc(_so, 16),
                s.wait_ge(_so, w)))
            s_o.bump(16)
            p.emit()

        for sp in range(NS if stage >= 3 else 0):
            tsp = slice(sp * SP, (sp + 1) * SP)
            p.add("sync", lambda s, tsp=tsp: (
                s.dma_start(out=ubch[:, :, :], in_=two(u_in)[:, :, tsp]).then_inc(_sx, 16),
                s.dma_start(out=dtb[:, :, :], in_=two(z_in)[:, :, tsp]).then_inc(_sx, 16),
                s.dma_start(out=ybch[:, :, :], in_=two(y_fm)[:, :, tsp]).then_inc(_sx, 16)))
            s_x.bump(48)
            fw = s_x.v
            p.emit()

            def fin1(s, fw=fw):
                s.wait_ge(_sx, fw)
                for gr in range(2):
                    s.activation(dtmp2[:, gr, :], dtb[:, gr, :], AF.Silu, bias=0.0, scale=1.0)
            p.add("scalar", fin1)

            def fin2(v, fw=fw):
                v.wait_ge(_sx, fw)
                for gr in range(2):
                    v.scalar_tensor_tensor(dtmp[:, gr, :], ubch[:, gr, :], sdp[:, gr:gr + 1],
                                           ybch[:, gr, :], ALU.mult, ALU.add)
            p.add("vector", fin2)
            p.emit()

            def fin3(v):
                for gr in range(2):
                    v.tensor_mul(dtmp[:, gr, :], dtmp[:, gr, :], dtmp2[:, gr, :])
                    v.tensor_copy(ubch[:, gr, :], dtmp[:, gr, :])
            p.add("vector", fin3)
            p.emit()

            for gr in range(2):
                for tg in range(SP // 512):
                    def mmyt(t, gr=gr, tg=tg, sp=sp):
                        tt = slice(sp * SP + tg * 512, sp * SP + (tg + 1) * 512)
                        t.matmul(psY0[:, 0:512], sAs[:, gr * 128:(gr + 1) * 128],
                                 crep[0:16, tt], start=True, stop=True).then_inc(_smm, 1)
                    p.add("tensor", mmyt)
                    s_mm.bump(1)
                    mmv = s_mm.v

                    def evyt(s, gr=gr, tg=tg, mmv=mmv):
                        s.wait_ge(_smm, mmv)
                        s.activation(ybch[:, gr, tg * 512:(tg + 1) * 512], psY0[:, 0:512],
                                     AF.Identity, bias=0.0, scale=1.0)
                    p.add("scalar", evyt)
                    p.emit()

            def fout(s, tsp=tsp, w=s_o.v + 32):
                s.dma_start(out=two(o_ycat[0:256, :])[:, :, tsp], in_=ubch[:, :, :]).then_inc(_so, 16)
                s.dma_start(out=two(o_ycat[256:512, :])[:, :, tsp], in_=ybch[:, :, :]).then_inc(_so, 16)
                s.wait_ge(_so, w)
            p.add("sync", fout)
            s_o.bump(32)
            p.emit()

        p.add("gpsimd", lambda g, w=s_g.v + 16: (
            g.dma_start(out=o_svec, in_=sxdb[96:97, :]).then_inc(_sg, 16),
            g.wait_ge(_sg, w)))
        s_g.bump(16)
        p.emit()
    return nc

def inputs_l0a_scan(ins, core, xdblp_all, u_all, z_all):
    sd = slice(DSL * core, DSL * (core + 1))
    A_log = ins["L0_attn_A_log"][sd]
    alogc = np.ascontiguousarray(A_log.reshape(-1).reshape(32, 128).T)
    selS = np.zeros((128, 16, 128), np.float32)
    selR = np.zeros((128, 16, 128), np.float32)
    for s in range(16):
        for r in range(128):
            selS[8 * s + r // 16, s, r] = 1.0
        for pp in range(128):
            selR[pp, s, 8 * s + pp // 16] = 1.0
    bdt = ins["L0_attn_b_dt"][sd]
    dp = ins["L0_attn_Dp"][sd]
    return {
        "xdblp": f32c(xdblp_all),
        "u_in": np.ascontiguousarray(u_all[sd]).astype(BFNP, copy=False)
        if u_all.dtype != BFNP else np.ascontiguousarray(u_all[sd]),
        "z_in": np.ascontiguousarray(z_all[sd]).astype(BFNP, copy=False)
        if z_all.dtype != BFNP else np.ascontiguousarray(z_all[sd]),
        "wdtT": f32c(ins["L0_attn_W_dt"][sd].T),
        "bdtp": f32c(np.stack([bdt[:128], bdt[128:]], axis=1)),
        "alogc": f32c(alogc),
        "alogs": f32c(A_log.T),
        "dpp": f32c(np.stack([dp[:128], dp[128:]], axis=1)),
        "selS": f32c(selS.reshape(128, 16 * 128)),
        "selR": f32c(selR.reshape(128, 16 * 128)),
    }



def build_l0a_out(fu=None):
    nc = fu["nc"] if fu is not None else bass.Bass(
        "TRN2", target_bir_lowering=False, detect_race_conditions=False)
    pfx = fu["pfx"] if fu is not None else ""
    ycat = dram_in(nc, fu, "ycat", [4096, TLOC], BF16)
    svr = dram_in(nc, fu, "svr", [1, TLOC], F32)
    w1T = dram_in(nc, fu, "w1T", [4096, DI], F32)
    w2T = dram_in(nc, fu, "w2T", [DI, C], F32)
    xs = dram_in(nc, fu, "xs", [TLOC, C], F32)
    o_s = dram_out(nc, fu, "o_s", [TLOC, C], F32)

    TH = 1024

    with ExitStack() as _st:
        E = _st.enter_context
        syc = E(nc.sbuf_tensor(pfx + "syc", [128, 32, TH], BF16))
        sw1a = E(nc.sbuf_tensor(pfx + "sw1a", [128, 32, 128], BF16))
        sw1b = E(nc.sbuf_tensor(pfx + "sw1b", [128, 32, 128], BF16))
        shf = E(nc.sbuf_tensor(pfx + "shf", [128, 16, TH], BF16))
        ssv = E(nc.sbuf_tensor(pfx + "ssv", [1, TLOC], F32))
        ssvb = E(nc.sbuf_tensor(pfx + "ssvb", [1, TLOC], BF16))
        sob = E(nc.sbuf_tensor(pfx + "sob", [1, 128], BF16))
        sw2 = E(nc.sbuf_tensor(pfx + "sw2", [128, 16, C], BF16))
        sxt = E(nc.sbuf_tensor(pfx + "sxt", [128, C], F32))
        sot = E(nc.sbuf_tensor(pfx + "sot", [128, C], F32))
        stt2 = E(nc.sbuf_tensor(pfx + "stt", [128, C], F32))
        mcol = E(nc.sbuf_tensor(pfx + "mcol", [128, 1], F32))
        vcol = E(nc.sbuf_tensor(pfx + "vcol", [128, 1], F32))
        psA = E(nc.psum_tensor(pfx + "psA", [128, 512], F32))
        psB = E(nc.psum_tensor(pfx + "psB", [128, 512], F32))
        p, (_sx, _sg, _smm, _sev, _so), (s_x, s_g, s_mm, s_ev, s_o) = fu_start(nc, fu, E)
        ps = (psA, psB)
        p.add("gpsimd", lambda g: g.dma_start(
            out=sw2[:, :, :], in_=w2T.rearrange("(k p) c -> p k c", p=128)).then_inc(_sg, 16))
        s_g.bump(16)
        w2w = s_g.v
        p.add("sync", lambda s: s.dma_start(out=ssv[:, :], in_=svr).then_inc(_sx, 16))
        s_x.bump(16)
        p.add("vector", lambda v, w=s_x.v: (v.wait_ge(_sx, w),
                                            v.tensor_copy(ssvb[:, :], ssv[:, :]),
                                            v.memset(sob[:, :], 1.0)))
        p.emit()

        for th in range(2):
            tsp = slice(th * TH, (th + 1) * TH)
            p.add("sync", lambda s, tsp=tsp: s.dma_start(
                out=syc[:, :, :], in_=ycat.rearrange("(k p) t -> p k t", p=128)[:, :, tsp])
                .then_inc(_sx, 16))
            s_x.bump(16)
            ycw = s_x.v
            p.emit()

            for tg2 in range(TH // 512):
                def mmsv(t, th=th, tg2=tg2, ycw=ycw):
                    t.wait_ge(_sx, ycw)
                    t.matmul(psA[:, :], sob[0:1, :],
                             ssvb[0:1, th * TH + tg2 * 512:th * TH + (tg2 + 1) * 512],
                             start=True, stop=True).then_inc(_smm, 1)
                p.add("tensor", mmsv)
                s_mm.bump(1)
                mmv = s_mm.v

                ytks = ([4 * rr + 2 for rr in range(8)] + [4 * rr + 3 for rr in range(8)]
                        if fu is not None else list(range(16, 32)))

                def addsv(v, tg2=tg2, mmv=mmv, ytks=ytks):
                    v.wait_ge(_smm, mmv)
                    for k in ytks:
                        v.tensor_add(syc[:, k, tg2 * 512:(tg2 + 1) * 512],
                                     syc[:, k, tg2 * 512:(tg2 + 1) * 512], psA[:, :])
                p.add("vector", addsv)
                p.emit()

            for jc in range(16):
                swl = (sw1a, sw1b)[jc % 2]
                p.add("gpsimd", lambda g, jc=jc, swl=swl: g.dma_start(
                    out=swl[:, :, :],
                    in_=w1T[:, jc * 128:(jc + 1) * 128].rearrange("(k p) j -> p k j", p=128))
                    .then_inc(_sg, 16))
                s_g.bump(16)
                xw = s_x.v
                gw = s_g.v

                for tg in range(2):
                    gi = jc * 2 + tg
                    pst = ps[gi % 2]

                    def mmh(t, jc=jc, tg=tg, gi=gi, pst=pst, swl=swl, xw=xw, gw=gw):
                        t.wait_ge(_sx, xw)
                        t.wait_ge(_sg, gw)
                        for k in range(32):
                            mmi = t.matmul(pst[:, :], swl[:, k, :],
                                     syc[:, k, tg * 512:(tg + 1) * 512],
                                     start=(k == 0), stop=(k == 31))
                        mmi.then_inc(_smm, 1)
                    p.add("tensor", mmh)
                    s_mm.bump(1)
                    mmv = s_mm.v

                    def evh(s, jc=jc, tg=tg, pst=pst, mmv=mmv):
                        s.wait_ge(_smm, mmv)
                        s.activation(shf[:, jc, tg * 512:(tg + 1) * 512], pst[:, :],
                                     AF.Relu, bias=0.0, scale=1.0).then_inc(_sev, 1)
                    p.add("scalar", evh)
                    s_ev.bump(1)
                p.emit()

            for tt in range(TH // 128):
                p.add("sync", lambda s, th=th, tt=tt: s.dma_start(
                    out=sxt[:, :], in_=xs[th * TH + tt * 128:th * TH + (tt + 1) * 128, :])
                    .then_inc(_sx, 16))
                s_x.bump(16)
                xw = s_x.v
                for cg in range(2):
                    pst = ps[cg % 2]

                    def mmo(t, tt=tt, cg=cg, pst=pst, w2w=w2w):
                        t.wait_ge(_sg, w2w)
                        for k in range(16):
                            mmi = t.matmul(pst[:, :], shf[:, k, tt * 128:(tt + 1) * 128],
                                     sw2[:, k, cg * 512:(cg + 1) * 512],
                                     start=(k == 0), stop=(k == 15))
                        mmi.then_inc(_smm, 1)
                    p.add("tensor", mmo)
                    s_mm.bump(1)
                    mmv = s_mm.v

                    def evo(v, cg=cg, pst=pst, mmv=mmv, xw=xw):
                        v.wait_ge(_smm, mmv)
                        v.wait_ge(_sx, xw)
                        v.tensor_add(sot[:, cg * 512:(cg + 1) * 512], pst[:, :],
                                     sxt[:, cg * 512:(cg + 1) * 512])
                    p.add("vector", evo)
                p.emit()

                emit_ln_tm(p, nc, sot, stt2, mcol, vcol)
                p.add("sync", lambda s, th=th, tt=tt, w=s_o.v + 16: (
                    s.dma_start(out=o_s[th * TH + tt * 128:th * TH + (tt + 1) * 128, :],
                                in_=sot[:, :]).then_inc(_so, 16),
                    s.wait_ge(_so, w)))
                s_o.bump(16)
                p.emit()
    return nc


def inputs_l0a_out(ins, core, ycat_full, svec_row):
    ts = slice(TLOC * core, TLOC * (core + 1))
    return {
        "ycat": np.ascontiguousarray(ycat_full[:, ts]),
        "svr": f32c(svec_row[:, ts]),
        "w1T": f32c(ins["L0_attn_W_out1"].T),
        "w2T": f32c(ins["L0_attn_W_out2"].T),
        "xs": f32c(ins["L0_attn_x"][0][ts]),
    }



_PROG_CACHE = {}


def prog(name, builder, *args):
    key = (name,) + args
    if key not in _PROG_CACHE:
        _PROG_CACHE[key] = builder(*args)
    return _PROG_CACHE[key]


def runp(name, builder, maps, bargs=()):
    nc = prog(name, builder, *bargs)
    return run_bass_kernel_spmd(nc, maps, core_ids=list(range(NCORE))).results


def kernel_unfused(**inputs):
    ins = inputs
    r = runp("a1", build_l0a_pre, [inputs_l0a_pre(ins, c) for c in range(NCORE)])
    u_all = np.concatenate([np.asarray(x["o_u"]) for x in r], axis=0)
    z_all = np.concatenate([np.asarray(x["o_z"]) for x in r], axis=0)
    xdblp_all = np.stack([np.asarray(x["o_xdblp"]) for x in r], axis=0)

    r = runp("a3", build_l0a_scan,
             [inputs_l0a_scan(ins, c, xdblp_all, u_all, z_all) for c in range(NCORE)])
    ycat_full = np.concatenate(
        [np.concatenate([np.asarray(x["o_ycat"][0:256]) for x in r], axis=0),
         np.concatenate([np.asarray(x["o_ycat"][256:512]) for x in r], axis=0)], axis=0)
    svec_row = np.asarray(r[0]["o_svec"], np.float32)

    r = runp("a4", build_l0a_out,
             [inputs_l0a_out(ins, c, ycat_full, svec_row) for c in range(NCORE)])
    stream0 = np.concatenate([np.asarray(x["o_s"]) for x in r], axis=0)

    b1res = runp("b1", build_l0f_qkv, [inputs_l0f_qkv(ins, c, stream0) for c in range(NCORE)])
    kf_all = np.concatenate([np.asarray(x["o_kf"]) for x in b1res], axis=1)
    vt_all = np.concatenate([np.asarray(x["o_vt"]) for x in b1res], axis=0)
    r = runp("b2", build_l0f_attn,
             [inputs_l0f_attn(ins, c, kf_all, vt_all, b1res, stream0) for c in range(NCORE)])
    stream1 = np.concatenate([np.asarray(x["o_s"]) for x in r], axis=0)

    r = runp("c1", build_l1a_ret, [inputs_l1a_ret(ins, h, stream1) for h in range(NCORE)])
    yn_full = np.concatenate([np.asarray(x["o_yn"]) for x in r], axis=1)
    r = runp("c2", build_l1a_out,
             [inputs_l1a_out(ins, c, stream1, yn_full) for c in range(NCORE)])
    stream2 = np.concatenate([np.asarray(x["o_s"]) for x in r], axis=0)

    xcur = stream2
    for layer in range(6):
        r = runp("da", build_l1f_attn,
                 [inputs_l1f_attn(ins, c, xcur, layer) for c in range(NCORE)])
        xcur = np.concatenate([np.asarray(x["o_x"]) for x in r], axis=0)
        last = layer == 5
        r = runp("df", build_l1f_ffn,
                 [inputs_l1f_ffn(ins, c, xcur, layer, s2=stream2 if last else None)
                  for c in range(NCORE)], bargs=(last,))
        xcur = np.concatenate([np.asarray(x["o_x"]) for x in r], axis=0)

    return np.ascontiguousarray(xcur, dtype=np.float32).reshape(1, T, C)



def emit_ln_tm(p, nc, sot, stt2, mcol, vcol, width=C):
    p.add("vector", lambda v: v.reduce_sum(mcol[:, :], sot[:, :], axis=AX.X))
    p.emit()

    def s1(v):
        v.tensor_scalar(stt2[:, :], mcol[:, 0:1].broadcast_to([128, width]),
                        1.0 / width, None, ALU.mult)
        v.tensor_sub(stt2[:, :], sot[:, :], stt2[:, :])
    p.add("vector", s1)
    p.emit()
    p.add("scalar", lambda s: s.activation(sot[:, :], stt2[:, :], AF.Square,
                                           bias=0.0, scale=1.0))
    p.emit()
    p.add("vector", lambda v: v.reduce_sum(vcol[:, :], sot[:, :], axis=AX.X))
    p.emit()
    p.add("vector", lambda v: v.tensor_scalar(
        sot[:, :], vcol[:, 0:1].broadcast_to([128, width]), 1.0 / width, 1e-5,
        ALU.mult, ALU.add))
    p.emit()

    def s2(s):
        s.activation(sot[:, :], sot[:, :], AF.Ln, bias=0.0, scale=1.0)
        s.activation(sot[:, :], sot[:, :], AF.Exp, bias=0.0, scale=-0.5)
    p.add("scalar", s2)
    p.emit()
    p.add("vector", lambda v: v.tensor_mul(sot[:, :], stt2[:, :], sot[:, :]))
    p.emit()



def build_l0f_qkv(fu=None):
    nc = fu["nc"] if fu is not None else bass.Bass(
        "TRN2", target_bir_lowering=False, detect_race_conditions=False)
    pfx = fu["pfx"] if fu is not None else ""
    ss = dram_in(nc, fu, "ss", [TLOC, C], F32)
    simT = dram_in(nc, fu, "simT", [16, C], F32)
    gatesr = dram_in(nc, fu, "gatesr", [1, 16], F32)
    qkvp = dram_in(nc, fu, "qkvp", [C, 6144], F32)
    ident = dram_in(nc, fu, "ident", [128, 128], F32)

    o_qf = dram_out(nc, fu, "o_qf", [128, TLOC], BF16)
    o_kf = dram_out(nc, fu, "o_kf", [128, TLOC], BF16)
    o_vt = dram_out(nc, fu, "o_vt", [TLOC, 128], BF16)
    o_w = dram_out(nc, fu, "o_w", [TLOC, 16], F32)
    o_qn = dram_out(nc, fu, "o_qn", [16, 128], F32)
    o_km = dram_out(nc, fu, "o_km", [1, 1], F32)

    with ExitStack() as _st:
        E = _st.enter_context
        stm = E(nc.sbuf_tensor(pfx + "stm", [128, 16, C], F32))
        xfm = E(nc.sbuf_tensor(pfx + "xfm", [128, 8, TLOC], BF16))
        sbt = E(nc.sbuf_tensor(pfx + "sbt", [128, C], BF16))
        sid = E(nc.sbuf_tensor(pfx + "sid", [128, 128], BF16))
        ssim = E(nc.sbuf_tensor(pfx + "ssim", [16, C], F32))
        ssimb = E(nc.sbuf_tensor(pfx + "ssimb", [16, C], BF16))
        snf = E(nc.sbuf_tensor(pfx + "snf", [128, 8, 16], BF16))
        sgat = E(nc.sbuf_tensor(pfx + "sgat", [1, 16], F32))
        sgatb = E(nc.sbuf_tensor(pfx + "sgatb", [1, 16], BF16))
        sgrep = E(nc.sbuf_tensor(pfx + "sgrep", [128, 16], F32))
        rnx = E(nc.sbuf_tensor(pfx + "rnx", [128, 16], F32))
        wql = E(nc.sbuf_tensor(pfx + "wql", [128, 8, 512], BF16))
        wq2 = E(nc.sbuf_tensor(pfx + "wq2", [128, 8, 512], BF16))
        qtm = E(nc.sbuf_tensor(pfx + "qtm", [128, 16, 128], F32))
        ktm = E(nc.sbuf_tensor(pfx + "ktm", [128, 16, 128], F32))
        vtm = E(nc.sbuf_tensor(pfx + "vtm", [128, 16, 128], F32))
        swt = E(nc.sbuf_tensor(pfx + "swt", [128, 16, 16], F32))
        tA = E(nc.sbuf_tensor(pfx + "tA", [128, 64], F32))
        tB = E(nc.sbuf_tensor(pfx + "tB", [128, 64], F32))
        tC = E(nc.sbuf_tensor(pfx + "tC", [128, 64], F32))
        tD = E(nc.sbuf_tensor(pfx + "tD", [128, 64], F32))
        tE = E(nc.sbuf_tensor(pfx + "tE", [128, 64], F32))
        sgrep4 = E(nc.sbuf_tensor(pfx + "sgrep4", [128, 64], F32))
        c1 = E(nc.sbuf_tensor(pfx + "c1", [128, 1], F32))
        c2 = E(nc.sbuf_tensor(pfx + "c2", [128, 1], F32))
        qnc = E(nc.sbuf_tensor(pfx + "qnc", [128, 16], F32))
        knc = E(nc.sbuf_tensor(pfx + "knc", [128, 16], F32))
        ones1 = E(nc.sbuf_tensor(pfx + "ones1", [1, 128], BF16))
        bfq = E(nc.sbuf_tensor(pfx + "bfq", [128, 128], BF16))
        psA = E(nc.psum_tensor(pfx + "psA", [128, 512], F32))
        psB = E(nc.psum_tensor(pfx + "psB", [128, 512], F32))
        psT = E(nc.psum_tensor(pfx + "psT", [128, 128], BF16))
        psS = E(nc.psum_tensor(pfx + "psS", [128, 16], F32))
        psR = E(nc.psum_tensor(pfx + "psR", [16, 128], F32))
        p, (_sx, _sg, _smm, _sev, _so), (s_x, s_g, s_mm, s_ev, s_o) = fu_start(nc, fu, E)

        def loads(s):
            s.dma_start(out=stm[:, :, :], in_=ss.rearrange("(tt p) c -> p tt c", p=128)).then_inc(_sx, 16)
            s.dma_start(out=ssim[:, :], in_=simT).then_inc(_sx, 16)
            s.dma_start(out=sgat[:, :], in_=gatesr).then_inc(_sx, 16)
        p.add("sync", loads)
        p.add("gpsimd", lambda g: g.dma_start(out=sid[:, :], in_=ident).then_inc(_sg, 16))
        s_x.bump(48); s_g.bump(16)
        p.add("vector", lambda v, w=s_x.v, wg=s_g.v: (
            v.wait_ge(_sx, w), v.wait_ge(_sg, wg), v.memset(ones1[:, :], 1.0),
            v.memset(qtm[:, :, :], 0.0), v.memset(ktm[:, :, :], 0.0),
            v.memset(vtm[:, :, :], 0.0)))
        p.add("gpsimd", lambda g: g.memset(c2[:, :], 0.0))
        p.emit()

        def sn1(s):
            s.activation(ssimb[:, :], ssim[:, :], AF.Square, bias=0.0, scale=1.0)
        p.add("scalar", sn1)
        p.emit()

        def sn2(v):
            v.reduce_sum(c1[0:16, 0:1], ssimb[:, :], axis=AX.X)
        p.add("vector", sn2)
        p.emit()
        p.add("scalar", lambda s: s.activation(c1[0:16, 0:1], c1[0:16, 0:1], AF.Sqrt,
                                               bias=0.0, scale=1.0))
        p.emit()

        p.add("gpsimd", lambda g: g.tensor_scalar_max(c1[0:16, 0:1], c1[0:16, 0:1], 1e-12))
        p.emit()
        p.add("scalar", lambda s: (
            s.activation(c1[0:16, 0:1], c1[0:16, 0:1], AF.Ln, bias=0.0, scale=1.0),
            s.activation(c1[0:16, 0:1], c1[0:16, 0:1], AF.Exp, bias=0.0, scale=-1.0)))
        p.emit()
        p.add("vector", lambda v: v.tensor_tensor(
            ssimb[:, :], ssim[:, :], c1[0:16, 0:1].broadcast_to([16, C]), ALU.mult))
        p.emit()

        for kc in range(8):
            p.add("tensor", lambda t, kc=kc: (
                t.transpose(psT[:, 0:16], ssimb[0:16, kc * 128:(kc + 1) * 128],
                            sid[0:16, 0:16]).then_inc(_smm, 1)))
            s_mm.bump(1)
            p.add("vector", lambda v, kc=kc, w=s_mm.v: (
                v.wait_ge(_smm, w),
                v.tensor_copy(snf[:, kc, :], psT[:, 0:16])))
            p.emit()

        p.add("scalar", lambda s: s.activation(sgat[:, :], sgat[:, :], AF.Sigmoid,
                                               bias=0.0, scale=1.0))
        p.emit()
        p.add("gpsimd", lambda g: g.tensor_copy(sgatb[:, :], sgat[:, :]))
        p.emit()
        p.add("tensor", lambda t: t.matmul(psS[:, :], ones1[0:1, :], sgatb[:, :],
                                           start=True, stop=True).then_inc(_smm, 1))
        s_mm.bump(1)
        p.add("scalar", lambda s, w=s_mm.v: (
            s.wait_ge(_smm, w),
            s.activation(sgrep[:, :], psS[:, :], AF.Copy, bias=0.0, scale=1.0),
            s.activation(sgrep4[:, :].rearrange("p (r j) -> p r j", r=4),
                         psS[:, :].unsqueeze(1).broadcast_to([128, 4, 16]),
                         AF.Copy, bias=0.0, scale=1.0)))
        p.emit()

        sqs = E(nc.sbuf_tensor(pfx + "sqs", [128, C], F32))
        for tt in range(16):
            p.add("scalar", lambda s, tt=tt: s.activation(
                sqs[:, :], stm[:, tt, :], AF.Square, bias=0.0, scale=1.0))
            p.emit()
            p.add("vector", lambda v, tt=tt: v.reduce_sum(c1[:, :], sqs[:, :], axis=AX.X))
            p.emit()
            p.add("scalar", lambda s, tt=tt: s.activation(c1[:, :], c1[:, :], AF.Sqrt,
                                                          bias=0.0, scale=1.0))
            p.emit()
            p.add("gpsimd", lambda g: g.tensor_scalar_max(c1[:, :], c1[:, :], 1e-12))
            p.add("vector", lambda v, tt=tt: v.tensor_copy(sbt[:, :], stm[:, tt, :]))
            p.emit()
            p.add("scalar", lambda s, tt=tt: (
                s.activation(c1[:, :], c1[:, :], AF.Ln, bias=0.0, scale=1.0),
                s.activation(rnx[:, tt:tt + 1], c1[:, :], AF.Exp, bias=0.0, scale=-1.0)))
            p.emit()
            for kc in range(8):
                p.add("tensor", lambda t, tt=tt, kc=kc: (
                    t.transpose(psT[:, :], sbt[:, kc * 128:(kc + 1) * 128], sid[:, :])
                    .then_inc(_smm, 1)))
                s_mm.bump(1)
                p.add("vector", lambda v, tt=tt, kc=kc, w=s_mm.v: (
                    v.wait_ge(_smm, w),
                    v.tensor_copy(xfm[:, kc, tt * 128:(tt + 1) * 128], psT[:, :])))
                p.emit()

            def lgm(t, tt=tt):
                for kc in range(8):
                    mmi = t.matmul(psS[:, :], xfm[:, kc, tt * 128:(tt + 1) * 128],
                             snf[:, kc, :], start=(kc == 0), stop=(kc == 7))
                mmi.then_inc(_smm, 1)
            p.add("tensor", lgm)
            s_mm.bump(1)

            def rt0(s, tt=tt, w=s_mm.v):
                s.wait_ge(_smm, w)
                s.activation(tA[:, :].rearrange("p (r j) -> p r j", r=4),
                             psS[:, :].unsqueeze(1).broadcast_to([128, 4, 16]),
                             AF.Copy, bias=0.0, scale=1.0)
            p.add("scalar", rt0)
            p.emit()

            def rt1(v, tt=tt):
                v.tensor_tensor(tA[:, :], tA[:, :], rnx[:, tt:tt + 1].broadcast_to([128, 64]),
                                ALU.mult)
                v.tensor_sub(tA[:, :], tA[:, :], sgrep4[:, :])
                v.tensor_relu(tB[:, :], tA[:, :])
                v.tensor_single_scalar(tC[:, :], tA[:, :], 0.0, ALU.is_gt)
                v.reduce_sum(c1[:, :], tC[:, 0:16], axis=AX.X)
                v.reduce_max(c2[:, :], tA[:, 0:16], axis=AX.X)
            p.add("vector", rt1)
            p.emit()

            def rt2(v):
                v.tensor_scalar(tE[:, :], c1[:, 0:1].broadcast_to([128, 64]),
                                -1.0, 1.0, ALU.mult, ALU.add)
                v.tensor_relu(tE[:, :], tE[:, :])
                v.tensor_tensor(tD[:, :], tA[:, :], c2[:, 0:1].broadcast_to([128, 64]),
                                ALU.is_ge)
                v.tensor_scalar(tD[:, :], tD[:, :], -1e30, None, ALU.mult)
                v.tensor_add(tD[:, :], tD[:, :], tA[:, :])
                v.reduce_max(c2[:, :], tD[:, 0:16], axis=AX.X)
            p.add("vector", rt2)
            p.emit()

            def rt3(v):
                v.tensor_tensor(tD[:, :], tA[:, :], c2[:, 0:1].broadcast_to([128, 64]),
                                ALU.is_ge)
                v.tensor_mul(tD[:, :], tD[:, :], tE[:, :])
                v.tensor_mul(tE[:, :], tC[:, :], tD[:, :])
                v.tensor_sub(tC[:, :], tC[:, :], tE[:, :])
                v.tensor_add(tC[:, :], tC[:, :], tD[:, :])
                v.tensor_mul(tA[:, :], tB[:, :], tC[:, :])
                v.tensor_scalar(tD[:, :], tC[:, :], 1e30, -1e30, ALU.mult, ALU.add)
                v.tensor_add(tA[:, :], tA[:, :], tD[:, :])
                v.reduce_max(c2[:, :], tA[:, 0:16], axis=AX.X, negate=True)
            p.add("vector", rt3)
            p.emit()
            p.add("scalar", lambda s, tt=tt: s.activation(
                tA[:, :], tA[:, :], AF.Exp, bias=c2[:, :], scale=1.0))
            p.emit()
            p.add("vector", lambda v: v.reduce_sum(c2[:, :], tA[:, 0:16], axis=AX.X))
            p.emit()
            p.add("scalar", lambda s: (
                s.activation(c2[:, :], c2[:, :], AF.Ln, bias=0.0, scale=1.0),
                s.activation(c2[:, :], c2[:, :], AF.Exp, bias=0.0, scale=-1.0)))
            p.emit()
            p.add("vector", lambda v: v.tensor_tensor(
                tB[:, :], tA[:, :], c2[:, 0:1].broadcast_to([128, 64]), ALU.mult))
            p.emit()
            p.add("gpsimd", lambda g, tt=tt: g.tensor_copy(swt[:, tt, :], tB[:, 0:16]))
            p.emit()

        wbufs = (wql, wq2)
        nw = 0
        for proj in range(3):
            dst = (qtm, ktm, vtm)[proj]
            for eg in range(4):
                wb = wbufs[nw % 2]
                p.add("gpsimd", lambda g, proj=proj, eg=eg, wb=wb: g.dma_start(
                    out=wb[:, :, :],
                    in_=qkvp[:, proj * 2048 + eg * 512:proj * 2048 + (eg + 1) * 512]
                    .rearrange("(k p) j -> p k j", p=128)).then_inc(_sg, 16))
                s_g.bump(16)
                gw = s_g.v
                nw += 1
                p.emit()

                for tt in range(16):
                    pst = (psA, psB)[tt % 2]

                    def mmq(t, tt=tt, wb=wb, pst=pst, gw=gw):
                        t.wait_ge(_sg, gw)
                        for kc in range(8):
                            mmi = t.matmul(pst[:, :], xfm[:, kc, tt * 128:(tt + 1) * 128],
                                     wb[:, kc, :], start=(kc == 0), stop=(kc == 7))
                        mmi.then_inc(_smm, 1)
                    p.add("tensor", mmq)
                    s_mm.bump(1)

                    def accq(v, tt=tt, eg=eg, dst=dst, pst=pst, w=s_mm.v):
                        v.wait_ge(_smm, w)
                        for e in range(4):
                            v.scalar_tensor_tensor(
                                dst[:, tt, :], pst[:, e * 128:(e + 1) * 128],
                                swt[:, tt, eg * 4 + e:eg * 4 + e + 1],
                                dst[:, tt, :], ALU.mult, ALU.add)
                    p.add("vector", accq)
                    if tt % 2 == 1:
                        p.emit()
                p.emit()

        for tt in range(16):
            p.add("scalar", lambda s, tt=tt: s.activation(
                sqs[:, 0:128], qtm[:, tt, :], AF.Square, bias=0.0, scale=1.0))
            p.add("scalar", lambda s, tt=tt: s.activation(
                sqs[:, 128:256], ktm[:, tt, :], AF.Square, bias=0.0, scale=1.0))
            p.emit()

            def nq(v, tt=tt):
                v.reduce_sum(qnc[:, tt:tt + 1], sqs[:, 0:128], axis=AX.X)
                v.reduce_sum(knc[:, tt:tt + 1], sqs[:, 128:256], axis=AX.X)
            p.add("vector", nq)
            p.emit()
        p.add("scalar", lambda s: (
            s.activation(qnc[:, :], qnc[:, :], AF.Sqrt, bias=0.0, scale=1.0),
            s.activation(knc[:, :], knc[:, :], AF.Sqrt, bias=0.0, scale=1.0)))
        p.emit()

        def km1(v):
            v.reduce_max(c1[:, :], knc[:, :], axis=AX.X)
        p.add("vector", km1)
        p.emit()
        p.add("gpsimd", lambda g: g.tensor_copy(bfq[:, 0:1], c1[:, :]))
        p.emit()
        p.add("tensor", lambda t: t.transpose(psT[:, :], bfq[:, :], sid[:, :])
              .then_inc(_smm, 1))
        s_mm.bump(1)
        p.add("vector", lambda v, w=s_mm.v: (
            v.wait_ge(_smm, w),
            v.tensor_copy(sqs[0:1, 0:128], psT[0:1, 0:128]),
            v.reduce_max(c2[0:1, 0:1], sqs[0:1, 0:128], axis=AX.X)))
        p.emit()

        for tt in range(16):
            for which, src, dstf in ((0, qtm, o_qf), (1, ktm, o_kf)):
                p.add("vector", lambda v, tt=tt, src=src: v.tensor_copy(bfq[:, :], src[:, tt, :]))
                p.emit()
                p.add("tensor", lambda t, tt=tt: t.transpose(psT[:, :], bfq[:, :], sid[:, :])
                      .then_inc(_smm, 1))
                s_mm.bump(1)
                p.add("vector", lambda v, w=s_mm.v: (
                    v.wait_ge(_smm, w), v.tensor_copy(bfq[:, :], psT[:, :])))
                p.emit()
                p.add("sync", lambda s, tt=tt, dstf=dstf, w=s_o.v + 16: (
                    s.dma_start(out=dstf[:, tt * 128:(tt + 1) * 128], in_=bfq[:, :])
                    .then_inc(_so, 16), s.wait_ge(_so, w)))
                s_o.bump(16)
                p.emit()

        vtb = E(nc.sbuf_tensor(pfx + "vtb", [128, 16, 128], BF16))
        p.add("vector", lambda v: [v.tensor_copy(vtb[:, tt, :], vtm[:, tt, :])
                                   for tt in range(16)])
        p.emit()

        def outs(s, w=s_o.v + 64):
            s.dma_start(out=o_vt.rearrange("(tt p) h -> p tt h", p=128), in_=vtb[:, :, :]).then_inc(_so, 16)
            s.dma_start(out=o_w.rearrange("(tt p) e -> p tt e", p=128), in_=swt[:, :, :]).then_inc(_so, 16)
            s.dma_start(out=o_qn.rearrange("a b -> b a"), in_=qnc[:, :],
                        allow_slow_non_contiguous=True).then_inc(_so, 16)
            s.dma_start(out=o_km, in_=c2[0:1, 0:1]).then_inc(_so, 16)
            s.wait_ge(_so, w)
        p.add("sync", outs)
        s_o.bump(64)
        p.emit()
    return nc


def inputs_l0f_qkv(ins, core, stream0):
    ts = slice(TLOC * core, TLOC * (core + 1))
    qkvp = np.concatenate([ins["L0_ffn_q_proj"].transpose(1, 0, 2).reshape(C, 2048),
                           ins["L0_ffn_k_proj"].transpose(1, 0, 2).reshape(C, 2048),
                           ins["L0_ffn_v_proj"].transpose(1, 0, 2).reshape(C, 2048)], axis=1)
    return {
        "ss": f32c(stream0[ts]),
        "simT": f32c(ins["L0_ffn_sim_matrix"].T),
        "gatesr": f32c(ins["L0_ffn_gates"][None, :]),
        "qkvp": f32c(qkvp),
        "ident": f32c(np.eye(128)),
    }



def build_l0f_attn(fu=None):
    nc = fu["nc"] if fu is not None else bass.Bass(
        "TRN2", target_bir_lowering=False, detect_race_conditions=False)
    pfx = fu["pfx"] if fu is not None else ""
    kf = dram_in(nc, fu, "kf", [128, T], BF16)
    vt = dram_in(nc, fu, "vt", [T, 128], BF16)
    qf = dram_in(nc, fu, "qf", [128, TLOC], BF16)
    qn = dram_in(nc, fu, "qn", [1, TLOC], F32)
    kms = dram_in(nc, fu, "kms", [8, 1], F32)
    wrt = dram_in(nc, fu, "wrt", [TLOC, 16], F32)
    oproj = dram_in(nc, fu, "oproj", [2048, C], F32)
    ss = dram_in(nc, fu, "ss", [TLOC, C], F32)
    ident = dram_in(nc, fu, "ident", [128, 128], F32)
    eindi = dram_in(nc, fu, "eindi", [16, 2048], F32)
    o_s = dram_out(nc, fu, "o_s", [TLOC, C], F32)

    with ExitStack() as _st:
        E = _st.enter_context
        skf = E(nc.sbuf_tensor(pfx + "skf", [128, T], BF16))
        svt = E(nc.sbuf_tensor(pfx + "svt", [128, 128, 128], BF16))
        sqf = E(nc.sbuf_tensor(pfx + "sqf", [128, TLOC], BF16))
        smr = E(nc.sbuf_tensor(pfx + "smr", [1, TLOC], BF16))
        skm = E(nc.sbuf_tensor(pfx + "skm", [8, 1], F32))
        skm2 = E(nc.sbuf_tensor(pfx + "skm2", [1, 8], F32))
        snegs = E(nc.sbuf_tensor(pfx + "snegs", [1, 128], BF16))
        sonec = E(nc.sbuf_tensor(pfx + "sonec", [128, 1], BF16))
        sw = E(nc.sbuf_tensor(pfx + "sw", [128, 16, 16], F32))
        sop = E(nc.sbuf_tensor(pfx + "sop", [128, 16, C], BF16))
        sid = E(nc.sbuf_tensor(pfx + "sid", [128, 128], BF16))
        satt8 = E(nc.sbuf_tensor(pfx + "satt8", [128, 4, 1024], BF16))
        ofacc = E(nc.sbuf_tensor(pfx + "ofacc", [128, 2, 1024], F32))
        rsr = E(nc.sbuf_tensor(pfx + "rsr", [1, 2048], F32))
        rsc = E(nc.sbuf_tensor(pfx + "rsc", [128, 16], F32))
        swT = E(nc.sbuf_tensor(pfx + "swT", [16, 2048], BF16))
        bfw = E(nc.sbuf_tensor(pfx + "bfw", [128, 16], BF16))
        sidf = E(nc.sbuf_tensor(pfx + "sidf", [1, 1], F32))
        eind = E(nc.sbuf_tensor(pfx + "eind", [16, 2048], BF16))
        ohf = E(nc.sbuf_tensor(pfx + "ohf", [128, 16, 1024], BF16))
        sxt = E(nc.sbuf_tensor(pfx + "sxt", [128, C], F32))
        souo = E(nc.sbuf_tensor(pfx + "souo", [128, C], F32))
        sst2 = E(nc.sbuf_tensor(pfx + "sst2", [128, C], F32))
        mcol = E(nc.sbuf_tensor(pfx + "mcol", [128, 1], F32))
        vcol = E(nc.sbuf_tensor(pfx + "vcol", [128, 1], F32))
        p, (_sx, _sg, _smm, _sev, _so), (s_x, s_g, s_mm, s_ev, s_o) = fu_start(nc, fu, E)

        def loads(s):
            s.dma_start(out=skf[:, :], in_=kf).then_inc(_sx, 16)
            s.dma_start(out=svt[:, :, :], in_=vt.rearrange("(kt p) h -> p kt h", p=128)).then_inc(_sx, 16)
            s.dma_start(out=sqf[:, :], in_=qf).then_inc(_sx, 16)
            s.dma_start(out=rsr[:, :], in_=qn).then_inc(_sx, 16)
            s.dma_start(out=skm[:, :], in_=kms).then_inc(_sx, 16)
            s.dma_start(out=skm2[:, :], in_=kms.rearrange("a b -> b a"),
                        allow_slow_non_contiguous=True).then_inc(_sx, 16)
            s.dma_start(out=sw[:, :, :], in_=wrt.rearrange("(tt p) e -> p tt e", p=128)).then_inc(_sx, 16)
        p.add("sync", loads)
        p.add("gpsimd", lambda g: (
            g.dma_start(out=sop[:, :, :], in_=oproj.rearrange("(k p) c -> p k c", p=128)).then_inc(_sg, 16),
            g.dma_start(out=sid[:, :], in_=ident).then_inc(_sg, 16),
            g.dma_start(out=eind[:, :], in_=eindi).then_inc(_sg, 16)))
        s_x.bump(16 * 7); s_g.bump(48)
        p.add("vector", lambda v, w=s_x.v, wg=s_g.v: (
            v.wait_ge(_sx, w), v.wait_ge(_sg, wg),
            v.memset(snegs[:, :], -1.0)))
        p.add("gpsimd", lambda g: (g.memset(sonec[:, :], 1.0),
                                   g.memset(sidf[:, :], 1.0)))
        p.emit()

        p.add("vector", lambda v: v.reduce_max(mcol[0:1, 0:1], skm2[:, :], axis=AX.X))
        p.emit()

        def mrow(v):
            v.tensor_tensor(rsr[:, :], rsr[:, :],
                            mcol[0:1, 0:1].broadcast_to([1, 2048]), ALU.mult)
            v.tensor_copy(smr[:, :], rsr[:, :])
        p.add("vector", mrow)
        p.emit()

        with ExitStack() as _ps:
            psS0 = _ps.enter_context(nc.psum_tensor(pfx + "psS0", [128, 1024], F32))
            psS1 = _ps.enter_context(nc.psum_tensor(pfx + "psS1", [128, 1024], F32))
            psOf = _ps.enter_context(nc.psum_tensor(pfx + "psOf", [128, 1024], F32))
            psRr = _ps.enter_context(nc.psum_tensor(pfx + "psRr", [1, 1024], F32))
            p.add("vector", lambda v: (v.memset(ofacc[:, :, :], 0.0),
                                       v.memset(rsr[:, :], 0.0)))
            p.emit()
            for qh in range(2):
                pss = (psS0, psS1)
                for kb in range(32):
                    for ki in range(4):
                        kt = kb * 4 + ki
                        ps = pss[ki % 2]
                        evw = max(0, s_ev.v - 1)

                        def mmsc2(t, kt=kt, ki=ki, ps=ps, qh=qh, evw=evw):
                            if ki >= 2:
                                t.wait_ge(_sev, evw)
                            for g2 in range(2):
                                t.matmul(ps[:, g2 * 512:(g2 + 1) * 512],
                                         skf[:, kt * 128:(kt + 1) * 128],
                                         sqf[:, qh * 1024 + g2 * 512:qh * 1024 + (g2 + 1) * 512],
                                         start=True, stop=False)
                                mmi = t.matmul(ps[:, g2 * 512:(g2 + 1) * 512], snegs[0:1, :],
                                         smr[0:1, qh * 1024 + g2 * 512:qh * 1024 + (g2 + 1) * 512],
                                         start=False, stop=True)
                            mmi.then_inc(_smm, 1)
                        p.add("tensor", mmsc2)
                        s_mm.bump(1)
                        mmv = s_mm.v

                        def expev(s, ki=ki, ps=ps, mmv=mmv):
                            s.wait_ge(_smm, mmv)
                            s.activation(satt8[:, ki, :], ps[:, :], AF.Exp, bias=0.0,
                                         scale=0.08838834764831845).then_inc(_sev, 1)
                        p.add("scalar", expev)
                        s_ev.bump(1)
                    attv = s_ev.v

                    def mmav(t, kb=kb, attv=attv):
                        t.wait_ge(_sev, attv)
                        for g2 in range(2):
                            for ki in range(4):
                                t.matmul(psOf[:, g2 * 512:(g2 + 1) * 512],
                                         svt[:, kb * 4 + ki, :],
                                         satt8[:, ki, g2 * 512:(g2 + 1) * 512],
                                         start=(ki == 0), stop=(ki == 3))
                        for g2 in range(2):
                            for ki in range(4):
                                mmi = t.matmul(psRr[:, g2 * 512:(g2 + 1) * 512],
                                         sonec[:, :],
                                         satt8[:, ki, g2 * 512:(g2 + 1) * 512],
                                         start=(ki == 0), stop=(ki == 3))
                        mmi.then_inc(_smm, 1)
                    p.add("tensor", mmav)
                    s_mm.bump(1)

                    def accof(v, qh=qh, w=s_mm.v):
                        v.wait_ge(_smm, w)
                        v.tensor_add(ofacc[:, qh, :], ofacc[:, qh, :], psOf[:, :])
                        v.tensor_add(rsr[:, qh * 1024:(qh + 1) * 1024],
                                     rsr[:, qh * 1024:(qh + 1) * 1024], psRr[:, :]).then_inc(_sev, 1)
                    p.add("vector", accof)
                    s_ev.bump(1)
                p.emit()

        def rcp(v):
            v.reciprocal(rsr[:, :], rsr[:, :])
        p.add("vector", rcp)
        p.emit()

        with ExitStack() as _ps:
            psW = _ps.enter_context(nc.psum_tensor(pfx + "psW", [128, 512], F32))
            psC = _ps.enter_context(nc.psum_tensor(pfx + "psC", [128, 16], F32))
            psCb = _ps.enter_context(nc.psum_tensor(pfx + "psCb", [16, 128], BF16))
            psO2 = _ps.enter_context(nc.psum_tensor(pfx + "psO2", [128, 1024], F32))

            for tt in range(16):
                p.add("tensor", lambda t, tt=tt: t.transpose(
                    psC[:, tt:tt + 1], rsr[0:1, tt * 128:(tt + 1) * 128],
                    sidf[:, :]).then_inc(_smm, 1))
                s_mm.bump(1)
            p.add("scalar", lambda s, w=s_mm.v: (
                s.wait_ge(_smm, w),
                s.activation(rsc[:, :], psC[:, :], AF.Copy, bias=0.0, scale=1.0)))
            p.emit()

            for tt in range(16):
                p.add("gpsimd", lambda g, tt=tt: g.tensor_copy(bfw[:, :], sw[:, tt, :]))
                p.emit()
                p.add("tensor", lambda t, tt=tt: t.transpose(
                    psCb[:, :], bfw[:, :], sid[:, :]).then_inc(_smm, 1))
                s_mm.bump(1)
                p.add("vector", lambda v, tt=tt, w=s_mm.v: (
                    v.wait_ge(_smm, w),
                    v.tensor_copy(swT[0:16, tt * 128:(tt + 1) * 128], psCb[0:16, :])))
                p.emit()

            for th2 in range(2):
              for e in range(16):
                for qg in range(2):
                    def mmwb(t, th2=th2, e=e, qg=qg):
                        t.matmul(psW[:, :], eind[0:16, e * 128:(e + 1) * 128],
                                 swT[0:16, th2 * 1024 + qg * 512:th2 * 1024 + (qg + 1) * 512],
                                 start=True, stop=True).then_inc(_smm, 1)
                    p.add("tensor", mmwb)
                    s_mm.bump(1)

                    def mkoh(v, th2=th2, e=e, qg=qg, w=s_mm.v):
                        v.wait_ge(_smm, w)
                        v.tensor_tensor(ohf[:, e, qg * 512:(qg + 1) * 512],
                                        ofacc.rearrange("p h q -> p (h q)")
                                        [:, th2 * 1024 + qg * 512:th2 * 1024 + (qg + 1) * 512],
                                        psW[:, :], ALU.mult)
                    p.add("vector", mkoh)
                    p.emit()

              for tt in range(th2 * 8, th2 * 8 + 8):
                p.add("sync", lambda s, tt=tt: s.dma_start(
                    out=sxt[:, :], in_=ss[tt * 128:(tt + 1) * 128, :]).then_inc(_sx, 16))
                s_x.bump(16)
                xw = s_x.v

                def mmo(t, tt=tt, th2=th2):
                    for cg in range(2):
                        for kc in range(16):
                            mmi = t.matmul(psO2[:, cg * 512:(cg + 1) * 512],
                                     ohf[:, kc, (tt - th2 * 8) * 128:(tt - th2 * 8 + 1) * 128],
                                     sop[:, kc, cg * 512:(cg + 1) * 512],
                                     start=(kc == 0), stop=(kc == 15))
                    mmi.then_inc(_smm, 1)
                p.add("tensor", mmo)
                s_mm.bump(1)

                def evf(v, tt=tt, w=s_mm.v, xw=xw):
                    v.wait_ge(_smm, w)
                    v.wait_ge(_sx, xw)
                    v.tensor_tensor(souo[:, :], psO2[:, 0:1024],
                                    rsc[:, tt:tt + 1].broadcast_to([128, 1024]), ALU.mult)
                    v.tensor_add(souo[:, :], souo[:, :], sxt[:, :])
                p.add("vector", evf)
                p.emit()
                emit_ln_tm(p, nc, souo, sst2, mcol, vcol)
                p.add("sync", lambda s, tt=tt, w=s_o.v + 16: (
                    s.dma_start(out=o_s[tt * 128:(tt + 1) * 128, :], in_=souo[:, :]).then_inc(_so, 16),
                    s.wait_ge(_so, w)))
                s_o.bump(16)
                p.emit()
    return nc


def inputs_l0f_attn(ins, core, kf_all, vt_all, b1res, stream0):
    ts = slice(TLOC * core, TLOC * (core + 1))
    return {
        "kf": np.ascontiguousarray(kf_all),
        "vt": np.ascontiguousarray(vt_all),
        "qf": np.ascontiguousarray(b1res[core]["o_qf"]),
        "qn": f32c(np.asarray(b1res[core]["o_qn"], np.float32).reshape(1, TLOC)),
        "kms": f32c(np.concatenate([np.asarray(r["o_km"], np.float32) for r in b1res], axis=0)),
        "wrt": f32c(np.asarray(b1res[core]["o_w"], np.float32)),
        "oproj": f32c(ins["L0_ffn_o_proj"].reshape(2048, C)),
        "ss": f32c(stream0[ts]),
        "ident": f32c(np.eye(128)),
        "eindi": f32c(np.repeat(np.eye(16, dtype=np.float32), 128, axis=1)),
    }



def build_l1a_ret(fu=None):
    nc = fu["nc"] if fu is not None else bass.Bass(
        "TRN2", target_bir_lowering=False, detect_race_conditions=False)
    pfx = fu["pfx"] if fu is not None else ""
    sfm = dram_in(nc, fu, "sfm", [128, T], F32)
    wq = dram_in(nc, fu, "wq", [128, 128], F32)
    wk = dram_in(nc, fu, "wk", [128, 128], F32)
    wv = dram_in(nc, fu, "wv", [128, 128], F32)
    tq1 = dram_in(nc, fu, "tq1", [128, T], F32)
    tq2 = dram_in(nc, fu, "tq2", [128, T], F32)
    tk1 = dram_in(nc, fu, "tk1", [128, T], F32)
    tk2 = dram_in(nc, fu, "tk2", [128, T], F32)
    prm = dram_in(nc, fu, "prm", [128, 128], F32)
    dtc = dram_in(nc, fu, "dtc", [128, 128], F32)
    gcols = dram_in(nc, fu, "gcols", [128, 4], F32)
    gnp = dram_in(nc, fu, "gnp", [1, 256], F32)
    ident = dram_in(nc, fu, "ident", [128, 128], F32)
    o_yn = dram_out(nc, fu, "o_yn", [T, 128], BF16)

    SPAN = 4096
    with ExitStack() as _st:
        E = _st.enter_context
        sx = E(nc.sbuf_tensor(pfx + "sx", [128, T], BF16))
        sqQ = E(nc.sbuf_tensor(pfx + "sqQ", [128, T], BF16))
        sqK = E(nc.sbuf_tensor(pfx + "sqK", [128, T], BF16))
        sqV = E(nc.sbuf_tensor(pfx + "sqV", [128, 128, 128], BF16))
        stb = E(nc.sbuf_tensor(pfx + "stb", [128, SPAN], BF16))
        stb2 = E(nc.sbuf_tensor(pfx + "stb2", [128, SPAN], BF16))
        srot = E(nc.sbuf_tensor(pfx + "srot", [128, SPAN], BF16))
        swq = E(nc.sbuf_tensor(pfx + "swq", [128, 128], BF16))
        swk = E(nc.sbuf_tensor(pfx + "swk", [128, 128], BF16))
        swv = E(nc.sbuf_tensor(pfx + "swv", [128, 128], BF16))
        sprm = E(nc.sbuf_tensor(pfx + "sprm", [128, 128], BF16))
        sdt = E(nc.sbuf_tensor(pfx + "sdt", [128, 128], BF16))
        sid = E(nc.sbuf_tensor(pfx + "sid", [128, 128], BF16))
        sgc = E(nc.sbuf_tensor(pfx + "sgc", [128, 4], F32))
        sgnp = E(nc.sbuf_tensor(pfx + "sgnp", [1, 256], F32))
        sgw = E(nc.sbuf_tensor(pfx + "sgw", [128, 128], F32))
        sgb = E(nc.sbuf_tensor(pfx + "sgb", [128, 128], F32))
        sS = E(nc.sbuf_tensor(pfx + "sS", [128, 128], F32))
        sSb = E(nc.sbuf_tensor(pfx + "sSb", [128, 128], BF16))
        attb = E(nc.sbuf_tensor(pfx + "attb", [128, 128], BF16))
        y2b = E(nc.sbuf_tensor(pfx + "y2b", [128, 128], F32))
        ytl = E(nc.sbuf_tensor(pfx + "ytl", [128, 128], F32))
        ktb = E(nc.sbuf_tensor(pfx + "ktb", [128, 128], BF16))
        vsb = E(nc.sbuf_tensor(pfx + "vsb", [128, 128], BF16))
        ynb = E(nc.sbuf_tensor(pfx + "ynb", [128, 128], BF16))
        mc = E(nc.sbuf_tensor(pfx + "mc", [128, 1], F32))
        vc = E(nc.sbuf_tensor(pfx + "vc", [128, 1], F32))
        sqs = E(nc.sbuf_tensor(pfx + "sqs", [128, 128], F32))
        onesf = E(nc.sbuf_tensor(pfx + "onesf", [1, 128], F32))
        psA = E(nc.psum_tensor(pfx + "psA", [128, 512], F32))
        psB = E(nc.psum_tensor(pfx + "psB", [128, 512], F32))
        psT = E(nc.psum_tensor(pfx + "psT", [128, 128], BF16))
        p, (_sx, _sg, _smm, _sev, _so), (s_x, s_g, s_mm, s_ev, s_o) = fu_start(nc, fu, E)

        def loads(g):
            g.dma_start(out=sx[:, :], in_=sfm).then_inc(_sg, 16)
            g.dma_start(out=swq[:, :], in_=wq).then_inc(_sg, 16)
            g.dma_start(out=swk[:, :], in_=wk).then_inc(_sg, 16)
            g.dma_start(out=swv[:, :], in_=wv).then_inc(_sg, 16)
            g.dma_start(out=sprm[:, :], in_=prm).then_inc(_sg, 16)
            g.dma_start(out=sdt[:, :], in_=dtc).then_inc(_sg, 16)
            g.dma_start(out=sid[:, :], in_=ident).then_inc(_sg, 16)
        p.add("gpsimd", loads)
        p.add("sync", lambda s: (
            s.dma_start(out=sgc[:, :], in_=gcols).then_inc(_sx, 16),
            s.dma_start(out=sgnp[:, :], in_=gnp).then_inc(_sx, 16)))
        s_g.bump(16 * 7); s_x.bump(32)
        p.add("vector", lambda v, w=s_x.v, wg=s_g.v: (
            v.wait_ge(_sx, w), v.wait_ge(_sg, wg),
            v.memset(sS[:, :], 0.0), v.memset(sSb[:, :], 0.0),
            v.memset(onesf[:, :], 1.0)))
        p.emit()

        for which, dst in ((0, sgw), (1, sgb)):
            p.add("tensor", lambda t, which=which: t.matmul(
                psA[:, 0:128], onesf[:, :], sgnp[0:1, which * 128:(which + 1) * 128],
                start=True, stop=True).then_inc(_smm, 1))
            s_mm.bump(1)
            p.add("vector", lambda v, dst=dst, w=s_mm.v: (
                v.wait_ge(_smm, w), v.tensor_copy(dst[:, :], psA[:, 0:128])))
            p.emit()

        for tg in range(T // 512):
            def mmqk(t, tg=tg):
                t.matmul(psA[:, :], swq[:, :], sx[:, tg * 512:(tg + 1) * 512],
                         start=True, stop=True)
                t.matmul(psB[:, :], swk[:, :], sx[:, tg * 512:(tg + 1) * 512],
                         start=True, stop=True).then_inc(_smm, 1)
            p.add("tensor", mmqk)
            s_mm.bump(1)
            p.add("vector", lambda v, tg=tg, w=s_mm.v: (
                v.wait_ge(_smm, w),
                v.tensor_copy(sqQ[:, tg * 512:(tg + 1) * 512], psA[:, :]),
                v.tensor_copy(sqK[:, tg * 512:(tg + 1) * 512], psB[:, :])))
            p.emit()
        for tt in range(128):
            pst = (psA, psB)[tt % 2]
            p.add("tensor", lambda t, tt=tt, pst=pst: t.matmul(
                pst[:, 0:128], sx[:, tt * 128:(tt + 1) * 128], swv[:, :],
                start=True, stop=True).then_inc(_smm, 1))
            s_mm.bump(1)
            p.add("vector", lambda v, tt=tt, pst=pst, w=s_mm.v: (
                v.wait_ge(_smm, w),
                v.tensor_copy(sqV[:, tt, :], pst[:, 0:128])))
            if tt % 2 == 1:
                p.emit()
        p.emit()

        for which, qk, t1d, t2d in ((0, sqQ, tq1, tq2), (1, sqK, tk1, tk2)):
            for sp in range(T // SPAN):
                tsp = slice(sp * SPAN, (sp + 1) * SPAN)
                p.add("gpsimd", lambda g, t1d=t1d, t2d=t2d, tsp=tsp: (
                    g.dma_start(out=stb[:, :], in_=t1d[:, tsp]).then_inc(_sg, 16),
                    g.dma_start(out=stb2[:, :], in_=t2d[:, tsp]).then_inc(_sg, 16)))
                s_g.bump(32)
                tw = s_g.v
                for g2 in range(SPAN // 512):
                    pst = (psA, psB)[g2 % 2]
                    p.add("tensor", lambda t, qk=qk, sp=sp, g2=g2, pst=pst: t.matmul(
                        pst[:, :], sprm[:, :], qk[:, sp * SPAN + g2 * 512:sp * SPAN + (g2 + 1) * 512],
                        start=True, stop=True).then_inc(_smm, 1))
                    s_mm.bump(1)
                    p.add("vector", lambda v, g2=g2, pst=pst, w=s_mm.v: (
                        v.wait_ge(_smm, w),
                        v.tensor_copy(srot[:, g2 * 512:(g2 + 1) * 512], pst[:, :])))
                    if g2 % 2 == 1:
                        p.emit()
                p.emit()

                def xp(v, qk=qk, tsp=tsp, tw=tw):
                    v.wait_ge(_sg, tw)
                    v.tensor_mul(srot[:, :], srot[:, :], stb2[:, :])
                    v.tensor_mul(qk[:, tsp], qk[:, tsp], stb[:, :])
                    v.tensor_add(qk[:, tsp], qk[:, tsp], srot[:, :])
                p.add("vector", xp)
                p.emit()

        for tt in range(128):
            def mmr(t, tt=tt):
                sl = slice(tt * 128, (tt + 1) * 128)
                t.matmul(psA[:, 0:128], sqK[:, sl], sqQ[:, sl],
                         start=True, stop=True).then_inc(_smm, 1)
                t.matmul(psB[:, 0:128], sqQ[:, sl], sSb[:, :],
                         start=True, stop=True).then_inc(_smm, 1)
                t.transpose(psT[:, :], sqK[:, sl], sid[:, :]).then_inc(_smm, 1)
            p.add("tensor", mmr)
            s_mm.bump(3)
            w0 = s_mm.v

            def ev1(v, tt=tt, w0=w0):
                v.wait_ge(_smm, w0)
                v.tensor_tensor(attb[:, :], psA[:, 0:128], sdt[:, :], ALU.mult)
                v.tensor_copy(y2b[:, :], psB[:, 0:128])
                v.tensor_copy(ktb[:, :], psT[:, :])
                v.tensor_scalar(vsb[:, :], sqV[:, tt, :], sgc[:, 1:2], None, ALU.mult)
                v.then_inc_dummy if False else None
            p.add("vector", ev1)
            p.emit()

            def mmr2(t, tt=tt):
                t.matmul(psA[:, 0:128], attb[:, :], sqV[:, tt, :],
                         start=True, stop=True).then_inc(_smm, 1)
                t.matmul(psB[:, 0:128], ktb[:, :], vsb[:, :],
                         start=True, stop=True).then_inc(_smm, 1)
            p.add("tensor", mmr2)
            s_mm.bump(2)
            w1 = s_mm.v

            def ev2(v, tt=tt, w1=w1):
                v.wait_ge(_smm, w1)
                v.scalar_tensor_tensor(ytl[:, :], y2b[:, :], sgc[:, 0:1], psA[:, 0:128],
                                       ALU.mult, ALU.add)
                v.scalar_tensor_tensor(sS[:, :], sS[:, :], sgc[:, 2:3], psB[:, 0:128],
                                       ALU.mult, ALU.add)
                v.tensor_copy(sSb[:, :], sS[:, :])
            p.add("vector", ev2)
            p.emit()

            p.add("vector", lambda v: v.reduce_sum(mc[:, :], ytl[:, :], axis=AX.X))
            p.emit()

            def gn1(v):
                v.tensor_scalar(sqs[:, :], mc[:, 0:1].broadcast_to([128, 128]),
                                1.0 / 128, None, ALU.mult)
                v.tensor_sub(ytl[:, :], ytl[:, :], sqs[:, :])
            p.add("vector", gn1)
            p.emit()
            p.add("scalar", lambda s: s.activation(sqs[:, :], ytl[:, :], AF.Square,
                                                   bias=0.0, scale=1.0))
            p.emit()
            p.add("vector", lambda v: v.reduce_sum(vc[:, :], sqs[:, :], axis=AX.X))
            p.emit()
            p.add("vector", lambda v: v.tensor_scalar(
                sqs[:, :], vc[:, 0:1].broadcast_to([128, 128]), 1.0 / 128, 1e-5,
                ALU.mult, ALU.add))
            p.emit()
            p.add("scalar", lambda s: (
                s.activation(sqs[:, :], sqs[:, :], AF.Ln, bias=0.0, scale=1.0),
                s.activation(sqs[:, :], sqs[:, :], AF.Exp, bias=0.0, scale=-0.5)))
            p.emit()

            def gn3(v, tt=tt):
                v.tensor_mul(ytl[:, :], ytl[:, :], sqs[:, :])
                v.tensor_mul(ytl[:, :], ytl[:, :], sgw[:, :])
                v.tensor_add(ytl[:, :], ytl[:, :], sgb[:, :])
                v.tensor_copy(ynb[:, :], ytl[:, :])
            p.add("vector", gn3)
            p.emit()
            p.add("sync", lambda s, tt=tt, w=s_o.v + 16: (
                s.dma_start(out=o_yn[tt * 128:(tt + 1) * 128, :], in_=ynb[:, :]).then_inc(_so, 16),
                s.wait_ge(_so, w)))
            s_o.bump(16)
            p.emit()
    return nc


def inputs_l1a_ret(ins, core, stream1):
    import math
    h = core
    gam = 1.0 - np.exp(np.linspace(math.log(1 / 32), math.log(1 / 512), 8))
    g = np.float64(gam[h])
    d = 128
    half = d // 2
    scale = (np.arange(0, d, 2, dtype=np.float64) + 0.4 * d) / (1.4 * d)
    pos = np.arange(T, dtype=np.float64)
    sc = scale[None, :] ** (pos / 512.0)[:, None]
    inv_freq = 1.0 / 10000.0 ** (np.arange(half, dtype=np.float64) / half)
    sinus = pos[:, None] * inv_freq[None, :]
    sin, cos = np.sin(sinus), np.cos(sinus)
    cs = np.repeat(cos * sc, 2, axis=1)
    ss_ = np.repeat(sin * sc, 2, axis=1)
    ci = np.repeat(cos / sc, 2, axis=1)
    si = np.repeat(sin / sc, 2, axis=1)
    prm = np.zeros((128, 128), np.float32)
    for i in range(64):
        prm[2 * i + 1, 2 * i] = -1.0
        prm[2 * i, 2 * i + 1] = 1.0
    idx = np.arange(128)
    dec = g ** (idx[None, :] - idx[:, None])
    dtc = np.where(idx[None, :] >= idx[:, None], dec, 0.0)
    gcols = np.stack([g ** idx, g ** (128 - idx),
                      np.full(128, g ** 128), np.zeros(128)], axis=1)
    sl = slice(128 * h, 128 * (h + 1))
    return {
        "sfm": f32c(stream1.T[sl]),
        "wq": f32c(ins["L1_attn_W_Q"][h]),
        "wk": f32c(ins["L1_attn_W_K"][h]),
        "wv": f32c(ins["L1_attn_W_V"][h]),
        "tq1": f32c(cs.T), "tq2": f32c(ss_.T), "tk1": f32c(ci.T), "tk2": f32c(si.T),
        "prm": prm,
        "dtc": f32c(dtc),
        "gcols": f32c(gcols),
        "gnp": f32c(np.concatenate([ins["L1_attn_gn_weight"][sl],
                                    ins["L1_attn_gn_bias"][sl]])[None, :]),
        "ident": f32c(np.eye(128)),
    }



def build_l1a_out(fu=None):
    nc = fu["nc"] if fu is not None else bass.Bass(
        "TRN2", target_bir_lowering=False, detect_race_conditions=False)
    pfx = fu["pfx"] if fu is not None else ""
    sfT = dram_in(nc, fu, "sfT", [C, TLOC], F32)
    ss = dram_in(nc, fu, "ss", [TLOC, C], F32)
    yn = dram_in(nc, fu, "yn", [TLOC, C], BF16)
    wg = dram_in(nc, fu, "wg", [C, C], F32)
    wo = dram_in(nc, fu, "wo", [C, C], F32)
    ident = dram_in(nc, fu, "ident", [128, 128], F32)
    o_s = dram_out(nc, fu, "o_s", [TLOC, C], F32)

    with ExitStack() as _st:
        E = _st.enter_context
        sxf = E(nc.sbuf_tensor(pfx + "sxf", [128, 8, TLOC], BF16))
        swg = E(nc.sbuf_tensor(pfx + "swg", [128, 8, C], BF16))
        swo = E(nc.sbuf_tensor(pfx + "swo", [128, 8, C], BF16))
        syn = E(nc.sbuf_tensor(pfx + "syn", [128, 16, C], BF16))
        sid = E(nc.sbuf_tensor(pfx + "sid", [128, 128], BF16))
        sgt = E(nc.sbuf_tensor(pfx + "sgt", [128, C], BF16))
        spr = E(nc.sbuf_tensor(pfx + "spr", [128, C], BF16))
        sprf = E(nc.sbuf_tensor(pfx + "sprf", [128, 8, TLOC], BF16))
        sxt = E(nc.sbuf_tensor(pfx + "sxt", [128, C], F32))
        souo = E(nc.sbuf_tensor(pfx + "souo", [128, C], F32))
        sst2 = E(nc.sbuf_tensor(pfx + "sst2", [128, C], F32))
        mcol = E(nc.sbuf_tensor(pfx + "mcol", [128, 1], F32))
        vcol = E(nc.sbuf_tensor(pfx + "vcol", [128, 1], F32))
        psA = E(nc.psum_tensor(pfx + "psA", [128, 512], F32))
        psB = E(nc.psum_tensor(pfx + "psB", [128, 512], F32))
        psT = E(nc.psum_tensor(pfx + "psT", [128, 128], BF16))
        p, (_sx, _sg, _smm, _sev, _so), (s_x, s_g, s_mm, s_ev, s_o) = fu_start(nc, fu, E)

        def loads(g):
            g.dma_start(out=sxf[:, :, :], in_=sfT.rearrange("(k p) t -> p k t", p=128)).then_inc(_sg, 16)
            g.dma_start(out=swg[:, :, :], in_=wg.rearrange("(k p) c -> p k c", p=128)).then_inc(_sg, 16)
            g.dma_start(out=swo[:, :, :], in_=wo.rearrange("(k p) c -> p k c", p=128)).then_inc(_sg, 16)
            g.dma_start(out=syn[:, :, :], in_=yn.rearrange("(tt p) c -> p tt c", p=128)).then_inc(_sg, 16)
            g.dma_start(out=sid[:, :], in_=ident).then_inc(_sg, 16)
        p.add("gpsimd", loads)
        s_g.bump(16 * 5)
        p.add("vector", lambda v, wg_=s_g.v: v.wait_ge(_sg, wg_))
        p.emit()

        for tt in range(16):
            for cg in range(2):
                pst = (psA, psB)[cg]

                def mmg(t, tt=tt, cg=cg, pst=pst):
                    for k in range(8):
                        mmi = t.matmul(pst[:, :], sxf[:, k, tt * 128:(tt + 1) * 128],
                                 swg[:, k, cg * 512:(cg + 1) * 512],
                                 start=(k == 0), stop=(k == 7))
                    mmi.then_inc(_smm, 1)
                p.add("tensor", mmg)
                s_mm.bump(1)
                p.add("scalar", lambda s, tt=tt, cg=cg, pst=pst, w=s_mm.v: (
                    s.wait_ge(_smm, w),
                    s.activation(sgt[:, cg * 512:(cg + 1) * 512], pst[:, :],
                                 AF.Silu, bias=0.0, scale=1.0)))
            p.emit()

            def prod(v, tt=tt):
                v.tensor_mul(spr[:, :], sgt[:, :], syn[:, tt, :])
            p.add("vector", prod)
            p.emit()
            for kc in range(8):
                p.add("tensor", lambda t, tt=tt, kc=kc: t.transpose(
                    psT[:, :], spr[:, kc * 128:(kc + 1) * 128], sid[:, :]).then_inc(_smm, 1))
                s_mm.bump(1)
                p.add("vector", lambda v, tt=tt, kc=kc, w=s_mm.v: (
                    v.wait_ge(_smm, w),
                    v.tensor_copy(sprf[:, kc, tt * 128:(tt + 1) * 128], psT[:, :])))
                p.emit()

        for tt in range(16):
            p.add("sync", lambda s, tt=tt: s.dma_start(
                out=sxt[:, :], in_=ss[tt * 128:(tt + 1) * 128, :]).then_inc(_sx, 16))
            s_x.bump(16)
            xw = s_x.v
            for cg in range(2):
                pst = (psA, psB)[cg]

                def mmo(t, tt=tt, cg=cg, pst=pst):
                    for k in range(8):
                        mmi = t.matmul(pst[:, :], sprf[:, k, tt * 128:(tt + 1) * 128],
                                 swo[:, k, cg * 512:(cg + 1) * 512],
                                 start=(k == 0), stop=(k == 7))
                    mmi.then_inc(_smm, 1)
                p.add("tensor", mmo)
                s_mm.bump(1)
                p.add("vector", lambda v, tt=tt, cg=cg, pst=pst, w=s_mm.v, xw=xw: (
                    v.wait_ge(_smm, w), v.wait_ge(_sx, xw),
                    v.tensor_add(souo[:, cg * 512:(cg + 1) * 512], pst[:, :],
                                 sxt[:, cg * 512:(cg + 1) * 512])))
            p.emit()
            emit_ln_tm(p, nc, souo, sst2, mcol, vcol)
            p.add("sync", lambda s, tt=tt, w=s_o.v + 16: (
                s.dma_start(out=o_s[tt * 128:(tt + 1) * 128, :], in_=souo[:, :]).then_inc(_so, 16),
                s.wait_ge(_so, w)))
            s_o.bump(16)
            p.emit()
    return nc


def inputs_l1a_out(ins, core, stream1, yn_full):
    ts = slice(TLOC * core, TLOC * (core + 1))
    return {
        "sfT": f32c(stream1.T[:, ts]),
        "ss": f32c(stream1[ts]),
        "yn": np.ascontiguousarray(yn_full[ts]),
        "wg": f32c(ins["L1_attn_W_G"]),
        "wo": f32c(ins["L1_attn_W_O"]),
        "ident": f32c(np.eye(128)),
    }



EXT = 2560
H0 = 512

def build_l1f_attn(fu=None):
    nc = fu["nc"] if fu is not None else bass.Bass(
        "TRN2", target_bir_lowering=False, detect_race_conditions=False)
    pfx = fu["pfx"] if fu is not None else ""
    xh = dram_in(nc, fu, "xh", [EXT, C], F32)
    wqkv = dram_in(nc, fu, "wqkv", [C, 3072], F32)
    wo = dram_in(nc, fu, "wo", [C, C], F32)
    qsc = dram_in(nc, fu, "qsc", [128, 8], F32)
    ksc = dram_in(nc, fu, "ksc", [128, 8], F32)
    tqc = dram_in(nc, fu, "tqc", [128, EXT], F32)
    tqs = dram_in(nc, fu, "tqs", [128, EXT], F32)
    tkc = dram_in(nc, fu, "tkc", [128, EXT], F32)
    tks = dram_in(nc, fu, "tks", [128, EXT], F32)
    mk0 = dram_in(nc, fu, "mk0", [128, 1024], F32)
    mkN = dram_in(nc, fu, "mkN", [128, 1024], F32)
    mtb = dram_in(nc, fu, "mtb", [128, 1024], F32)
    mt0 = dram_in(nc, fu, "mt0", [128, 1024], F32)
    bind = dram_in(nc, fu, "bind", [128, 8 * 16], F32)
    bind2 = dram_in(nc, fu, "bind2", [16, 8 * 128], F32)
    erows = dram_in(nc, fu, "erows", [32, 32 * 128], F32)
    prm2 = dram_in(nc, fu, "prm2", [128, 128], F32)
    ident = dram_in(nc, fu, "ident", [128, 128], F32)
    o_x = dram_out(nc, fu, "o_x", [TLOC, C], F32)

    v_hbm = nc.dram_tensor(pfx + "v_hbm", [EXT, C], BF16).ap()
    y_hbm = nc.dram_tensor(pfx + "y_hbm", [128, 8 * TLOC], BF16).ap()

    NSP = EXT // 512
    with ExitStack() as _st:
        E = _st.enter_context
        sqf = E(nc.sbuf_tensor(pfx + "sqf", [128, 8, TLOC], BF16))
        skf = E(nc.sbuf_tensor(pfx + "skf", [128, 8, EXT], BF16))
        wbig = E(nc.sbuf_tensor(pfx + "wbig", [128, 8, 2048], BF16))
        xspan = E(nc.sbuf_tensor(pfx + "xspan", [128, 8, 513], BF16))
        svtw = E(nc.sbuf_tensor(pfx + "svtw", [128, 4, 1024], BF16))
        yfw = E(nc.sbuf_tensor(pfx + "yfw", [128, 8, 256], BF16))
        ybuf = E(nc.sbuf_tensor(pfx + "ybuf", [128, 8, 128], BF16))
        ttb1 = E(nc.sbuf_tensor(pfx + "ttb1", [128, 512], BF16))
        ttb2 = E(nc.sbuf_tensor(pfx + "ttb2", [128, 512], BF16))
        smk0 = E(nc.sbuf_tensor(pfx + "smk0", [128, 1024], BF16))
        smkN = E(nc.sbuf_tensor(pfx + "smkN", [128, 1024], BF16))
        smtb = E(nc.sbuf_tensor(pfx + "smtb", [128, 1024], BF16))
        smt0 = E(nc.sbuf_tensor(pfx + "smt0", [128, 1024], BF16))
        sbind = E(nc.sbuf_tensor(pfx + "sbind", [128, 8, 16], BF16))
        sbind2 = E(nc.sbuf_tensor(pfx + "sbind2", [16, 8, 128], BF16))
        serw = E(nc.sbuf_tensor(pfx + "serw", [32, 32, 128], BF16))
        sprm = E(nc.sbuf_tensor(pfx + "sprm", [128, 128], BF16))
        sid = E(nc.sbuf_tensor(pfx + "sid", [128, 128], BF16))
        sidf = E(nc.sbuf_tensor(pfx + "sidf", [128, 128], F32))
        sqsc = E(nc.sbuf_tensor(pfx + "sqsc", [128, 8], F32))
        sksc = E(nc.sbuf_tensor(pfx + "sksc", [128, 8], F32))
        sbt = E(nc.sbuf_tensor(pfx + "sbt", [128, C], BF16))
        sqn = E(nc.sbuf_tensor(pfx + "sqn", [16, EXT], F32))
        sqnb = E(nc.sbuf_tensor(pfx + "sqnb", [16, EXT], BF16))
        srb = E(nc.sbuf_tensor(pfx + "srb", [128, 512], BF16))
        ssc = E(nc.sbuf_tensor(pfx + "ssc", [128, 1024], F32))
        se1 = E(nc.sbuf_tensor(pfx + "se1", [128, 1024], F32))
        seT = E(nc.sbuf_tensor(pfx + "seT", [128, 1024], BF16))
        stw = E(nc.sbuf_tensor(pfx + "stw", [128, 32], F32))
        strb = E(nc.sbuf_tensor(pfx + "strb", [32, 128], BF16))
        nm = E(nc.sbuf_tensor(pfx + "nm", [128, 2], F32))
        ssum = E(nc.sbuf_tensor(pfx + "ssum", [128, 2], F32))
        sxt = E(nc.sbuf_tensor(pfx + "sxt", [128, C], F32))
        souo = E(nc.sbuf_tensor(pfx + "souo", [128, C], F32))
        psA = E(nc.psum_tensor(pfx + "psA", [128, 512], F32))
        psB = E(nc.psum_tensor(pfx + "psB", [128, 512], F32))
        psW = E(nc.psum_tensor(pfx + "psW", [128, 512], F32))
        psS = E(nc.psum_tensor(pfx + "psS", [128, 1024], F32))
        psY = E(nc.psum_tensor(pfx + "psY", [64, 256], F32))
        psT = E(nc.psum_tensor(pfx + "psT", [128, 128], BF16))
        psN = psW
        psT2 = psS
        p, (_sx, _sg, _smm, _sev, _so), (s_x, s_g, s_mm, s_ev, s_o) = fu_start(nc, fu, E)

        def loads(g):
            g.dma_start(out=smk0[:, :], in_=mk0).then_inc(_sg, 16)
            g.dma_start(out=smkN[:, :], in_=mkN).then_inc(_sg, 16)
            g.dma_start(out=smtb[:, :], in_=mtb).then_inc(_sg, 16)
            g.dma_start(out=smt0[:, :], in_=mt0).then_inc(_sg, 16)
            g.dma_start(out=sbind[:, :, :], in_=bind.rearrange("p (k j) -> p k j", k=8)).then_inc(_sg, 16)
            g.dma_start(out=sbind2[:, :, :], in_=bind2.rearrange("p (k j) -> p k j", k=8)).then_inc(_sg, 16)
            g.dma_start(out=serw[:, :, :], in_=erows.rearrange("p (k j) -> p k j", k=32)).then_inc(_sg, 16)
            g.dma_start(out=sprm[:, :], in_=prm2).then_inc(_sg, 16)
            g.dma_start(out=sid[:, :], in_=ident).then_inc(_sg, 16)
        p.add("gpsimd", loads)
        p.add("sync", lambda s: (
            s.dma_start(out=sqsc[:, :], in_=qsc).then_inc(_sx, 16),
            s.dma_start(out=sksc[:, :], in_=ksc).then_inc(_sx, 16),
            s.dma_start(out=sidf[:, :], in_=ident).then_inc(_sx, 16)))
        s_g.bump(16 * 9); s_x.bump(48)
        p.add("gpsimd", lambda g, w=s_x.v, wg=s_g.v: (
            g.wait_ge(_sx, w), g.wait_ge(_sg, wg),
            g.memset(xspan[:, :, 0:1], 0.0)))
        p.emit()

        def build_span(sp):
            for tl in range(4):
                tt = sp * 4 + tl
                p.add("sync", lambda s, tt=tt: s.dma_start(
                    out=sxt[:, :], in_=xh[tt * 128:(tt + 1) * 128, :]).then_inc(_sx, 16))
                s_x.bump(16)
                p.add("vector", lambda v, w=s_x.v: (v.wait_ge(_sx, w),
                                                    v.tensor_copy(sbt[:, :], sxt[:, :])))
                p.emit()
                for kc in range(8):
                    p.add("tensor", lambda t, kc=kc: t.transpose(
                        psT[:, :], sbt[:, kc * 128:(kc + 1) * 128], sid[:, :]).then_inc(_smm, 1))
                    s_mm.bump(1)
                    p.add("vector", lambda v, tl=tl, kc=kc, w=s_mm.v: (
                        v.wait_ge(_smm, w),
                        v.tensor_copy(xspan[:, kc, 1 + tl * 128:1 + (tl + 1) * 128], psT[:, :])))
                    p.emit()

        def save_tail():
            p.add("gpsimd", lambda g: g.tensor_copy(xspan[:, :, 0:1], xspan[:, :, 512:513]))
            p.emit()

        def shs(kc, lo, hi):
            if kc < 4:
                return xspan[:, kc, 1 + lo:1 + hi]
            return xspan[:, kc, lo:hi]

        p.add("gpsimd", lambda g: g.dma_start(
            out=wbig[:, :, 0:1024],
            in_=wqkv[:, 2048:3072].rearrange("(k p) j -> p k j", p=128)).then_inc(_sg, 16))
        s_g.bump(16)
        vw = s_g.v
        p.emit()
        for sp in range(NSP):
            build_span(sp)
            for tl in range(4):
                for cg in range(2):
                    pst = (psA, psB)[cg]

                    def mmv(t, tl=tl, cg=cg, pst=pst, vw=vw):
                        t.wait_ge(_sg, vw)
                        for kc in range(8):
                            mmi = t.matmul(pst[:, :], shs(kc, tl * 128, (tl + 1) * 128),
                                     wbig[:, kc, cg * 512:(cg + 1) * 512],
                                     start=(kc == 0), stop=(kc == 7))
                        mmi.then_inc(_smm, 1)
                    p.add("tensor", mmv)
                    s_mm.bump(1)
                    p.add("vector", lambda v, cg=cg, pst=pst, w=s_mm.v: (
                        v.wait_ge(_smm, w),
                        v.tensor_copy(sbt[:, cg * 512:(cg + 1) * 512], pst[:, :])))
                p.emit()
                p.add("sync", lambda s, sp=sp, tl=tl, w=s_o.v + 16: (
                    s.dma_start(out=v_hbm[(sp * 4 + tl) * 128:(sp * 4 + tl + 1) * 128, :],
                                in_=sbt[:, :]).then_inc(_so, 16),
                    s.wait_ge(_so, w)))
                s_o.bump(16)
                p.emit()
            save_tail()

        p.add("gpsimd", lambda g: g.dma_start(
            out=wbig[:, :, :],
            in_=wqkv[:, 0:2048].rearrange("(k p) j -> p k j", p=128)).then_inc(_sg, 16))
        s_g.bump(16)
        qkw = s_g.v
        p.add("gpsimd", lambda g: g.memset(xspan[:, :, 0:1], 0.0))
        p.emit()
        for sp in range(NSP):
            build_span(sp)
            for jc in range(16):
                isq = jc < 8
                if isq and sp == 0:
                    continue
                pst = (psA, psB)[jc % 2]

                def mmqk(t, sp=sp, jc=jc, pst=pst, qkw=qkw):
                    t.wait_ge(_sg, qkw)
                    for kc in range(8):
                        mmi = t.matmul(pst[:, :], wbig[:, kc, (jc % 8) * 128 + (0 if jc < 8 else 1024):
                                                 (jc % 8) * 128 + 128 + (0 if jc < 8 else 1024)],
                                 shs(kc, 0, 512), start=(kc == 0), stop=(kc == 7))
                    mmi.then_inc(_smm, 1)
                p.add("tensor", mmqk)
                s_mm.bump(1)

                def evqk(v, sp=sp, jc=jc, pst=pst, w=s_mm.v):
                    v.wait_ge(_smm, w)
                    if jc < 8:
                        v.tensor_copy(sqf[:, jc, (sp - 1) * 512:sp * 512], pst[:, :])
                    else:
                        v.tensor_copy(skf[:, jc - 8, sp * 512:(sp + 1) * 512], pst[:, :])
                p.add("vector", evqk)
                if jc % 2 == 1:
                    p.emit()
            p.emit()
            save_tail()

        for which, dst, scol, t1d, t2d, lo, ntg in (
                (0, sqf, sqsc, tqc, tqs, H0, 4), (1, skf, sksc, tkc, tks, 0, 5)):
            for tg in range(ntg):
                for kc in range(8):
                    p.add("scalar", lambda s, dst=dst, kc=kc, tg=tg: s.activation(
                        srb[:, :], dst[:, kc, tg * 512:(tg + 1) * 512], AF.Square,
                        bias=0.0, scale=1.0))
                    p.emit()
                    p.add("tensor", lambda t, kc=kc: t.matmul(
                        psN[0:16, :], sbind[:, kc, :], srb[:, :],
                        start=(kc == 0), stop=(kc == 7)).then_inc(_smm, 1))
                    s_mm.bump(1)
                    p.emit()
                p.add("vector", lambda v, tg=tg, w=s_mm.v: (
                    v.wait_ge(_smm, w),
                    v.tensor_copy(sqn[:, tg * 512:(tg + 1) * 512], psN[0:16, :])))
                p.emit()
            p.add("scalar", lambda s, ntg=ntg: s.activation(
                sqn[:, 0:ntg * 512], sqn[:, 0:ntg * 512], AF.Sqrt, bias=0.0, scale=1.0))
            p.emit()
            p.add("vector", lambda v, ntg=ntg: (
                v.tensor_scalar_max(sqn[:, 0:ntg * 512], sqn[:, 0:ntg * 512], 1e-12),
                v.reciprocal(sqn[:, 0:ntg * 512], sqn[:, 0:ntg * 512]),
                v.tensor_copy(sqnb[:, 0:ntg * 512], sqn[:, 0:ntg * 512])))
            p.emit()
            for tg in range(ntg):
                p.add("gpsimd", lambda g, t1d=t1d, t2d=t2d, lo=lo, tg=tg: (
                    g.dma_start(out=ttb1[:, :],
                                in_=t1d[:, lo + tg * 512:lo + (tg + 1) * 512]).then_inc(_sg, 16),
                    g.dma_start(out=ttb2[:, :],
                                in_=t2d[:, lo + tg * 512:lo + (tg + 1) * 512]).then_inc(_sg, 16)))
                s_g.bump(32)
                tw = s_g.v
                for kc in range(8):
                    def mmb(t, kc=kc, tg=tg):
                        t.matmul(psW[:, :], sbind2[:, kc, :], sqnb[:, tg * 512:(tg + 1) * 512],
                                 start=True, stop=True).then_inc(_smm, 1)
                    p.add("tensor", mmb)
                    s_mm.bump(1)
                    p.add("vector", lambda v, dst=dst, scol=scol, kc=kc, tg=tg, w=s_mm.v: (
                        v.wait_ge(_smm, w),
                        v.tensor_tensor(dst[:, kc, tg * 512:(tg + 1) * 512],
                                        dst[:, kc, tg * 512:(tg + 1) * 512], psW[:, :], ALU.mult),
                        v.tensor_scalar(dst[:, kc, tg * 512:(tg + 1) * 512],
                                        dst[:, kc, tg * 512:(tg + 1) * 512],
                                        scol[:, kc:kc + 1], None, ALU.mult)))
                    p.emit()

                    def mmr(t, dst=dst, kc=kc, tg=tg):
                        t.matmul(psW[:, :], sprm[:, :], dst[:, kc, tg * 512:(tg + 1) * 512],
                                 start=True, stop=True).then_inc(_smm, 1)
                    p.add("tensor", mmr)
                    s_mm.bump(1)
                    p.add("vector", lambda v, dst=dst, kc=kc, tg=tg, tw=tw, w=s_mm.v: (
                        v.wait_ge(_smm, w), v.wait_ge(_sg, tw),
                        v.tensor_tensor(srb[:, :], psW[:, :], ttb2[:, :], ALU.mult),
                        v.tensor_tensor(dst[:, kc, tg * 512:(tg + 1) * 512],
                                        dst[:, kc, tg * 512:(tg + 1) * 512],
                                        ttb1[:, :], ALU.mult),
                        v.tensor_add(dst[:, kc, tg * 512:(tg + 1) * 512],
                                     dst[:, kc, tg * 512:(tg + 1) * 512], srb[:, :])))
                    p.emit()

        for w in range(8):
            msk = smk0 if w == 0 else smkN
            mskT = smt0 if w == 0 else smtb
            klo = 256 * w + 256
            p.add("sync", lambda s, klo=klo: s.dma_start(
                out=svtw[:, :, :],
                in_=v_hbm[klo:klo + 512, :].rearrange("(kt p) h -> p kt h", p=128))
                .then_inc(_sx, 16))
            s_x.bump(16)
            vww = s_x.v
            p.emit()
            for h in range(16):
                kc = h // 2
                ro = 64 * (h % 2)

                def mms1(t, h=h, kc=kc, ro=ro, klo=klo, w=w):
                    for qt in range(2):
                        mmi = t.matmul(psS[:, qt * 512:(qt + 1) * 512],
                                 sqf[ro:ro + 64, kc, w * 256 + qt * 128:w * 256 + (qt + 1) * 128],
                                 skf[ro:ro + 64, kc, klo:klo + 512],
                                 start=True, stop=True)
                    mmi.then_inc(_smm, 1)
                p.add("tensor", mms1)
                s_mm.bump(1)

                def ev1(v, msk=msk, w2=s_mm.v):
                    v.wait_ge(_smm, w2)
                    v.tensor_add(ssc[:, :], psS[:, :], msk[:, :])
                    v.reduce_max(nm[:, 0:1], ssc[:, 0:512], axis=AX.X, negate=True)
                    v.reduce_max(nm[:, 1:2], ssc[:, 512:1024], axis=AX.X, negate=True)
                p.add("vector", ev1)
                p.emit()

                def ex1(s, h=h):
                    s.activation(se1[:, 0:512], ssc[:, 0:512], AF.Exp, bias=nm[:, 0:1],
                                 scale=1.0, accum_out=ssum[:, 0:1])
                    s.activation(se1[:, 512:1024], ssc[:, 512:1024], AF.Exp, bias=nm[:, 1:2],
                                 scale=1.0, accum_out=ssum[:, 1:2])
                p.add("scalar", ex1)
                p.emit()
                p.add("scalar", lambda s: s.activation(ssum[:, :], ssum[:, :], AF.Ln,
                                                       bias=0.0, scale=1.0))
                p.emit()
                p.add("gpsimd", lambda g, h=h: g.tensor_sub(stw[:, 2 * h:2 * h + 2],
                                                             nm[:, :], ssum[:, :]))
                p.emit()

            p.add("tensor", lambda t: t.transpose(psT2[0:32, 0:128], stw[:, :],
                                                  sidf[:, :]).then_inc(_smm, 1))
            s_mm.bump(1)
            p.add("vector", lambda v, w2=s_mm.v: (
                v.wait_ge(_smm, w2), v.tensor_copy(strb[:, :], psT2[0:32, 0:128])))
            p.emit()

            for h in range(16):
                kc = h // 2
                ro = 64 * (h % 2)

                def mms2(t, h=h, kc=kc, ro=ro, klo=klo, w=w):
                    for ktc in range(4):
                        for qt in range(2):
                            sl2 = slice(ktc * 256 + qt * 128, ktc * 256 + (qt + 1) * 128)
                            t.matmul(psS[:, sl2],
                                     skf[ro:ro + 64, kc, klo + ktc * 128:klo + (ktc + 1) * 128],
                                     sqf[ro:ro + 64, kc,
                                         w * 256 + qt * 128:w * 256 + (qt + 1) * 128],
                                     start=True, stop=False)
                            mmi = t.matmul(psS[:, sl2], serw[:, 2 * h + qt, :], strb[:, :],
                                     start=False, stop=True)
                    mmi.then_inc(_smm, 1)
                p.add("tensor", mms2)
                s_mm.bump(1)
                p.add("scalar", lambda s, w2=s_mm.v: (
                    s.wait_ge(_smm, w2),
                    s.activation(seT[:, :], psS[:, :], AF.Exp, bias=0.0, scale=1.0)))
                p.emit()
                p.add("gpsimd", lambda g, mskT=mskT: g.tensor_mul(seT[:, :], seT[:, :], mskT[:, :]))
                p.emit()

                def mmav(t, h=h):
                    for ktc in range(4):
                        mmi = t.matmul(psY[:, :], svtw[:, ktc, 64 * h:64 * h + 64],
                                 seT[:, ktc * 256:(ktc + 1) * 256],
                                 start=(ktc == 0), stop=(ktc == 3))
                    mmi.then_inc(_smm, 1)
                p.add("tensor", mmav)
                s_mm.bump(1)
                p.add("vector", lambda v, h=h, w2=s_mm.v: (
                    v.wait_ge(_smm, w2),
                    v.tensor_copy(yfw[64 * (h % 2):64 * (h % 2) + 64, h // 2, :], psY[:, :])))
                p.emit()

            p.add("sync", lambda s, w=w, w2=s_o.v + 16: (
                s.dma_start(out=y_hbm.rearrange("p (k t) -> p k t", k=8)[:, :, w * 256:(w + 1) * 256],
                            in_=yfw[:, :, :]).then_inc(_so, 16),
                s.wait_ge(_so, w2)))
            s_o.bump(16)
            p.emit()

        p.add("gpsimd", lambda g: g.dma_start(
            out=wbig[:, :, 0:1024],
            in_=wo.rearrange("(k p) c -> p k c", p=128)).then_inc(_sg, 16))
        s_g.bump(16)
        wow = s_g.v
        p.emit()
        for tt in range(16):
            p.add("sync", lambda s, tt=tt: (
                s.dma_start(out=sxt[:, :], in_=xh[H0 + tt * 128:H0 + (tt + 1) * 128, :])
                .then_inc(_sx, 16),
                s.dma_start(out=ybuf[:, :, :],
                            in_=y_hbm.rearrange("p (k t) -> p k t", k=8)[:, :, tt * 128:(tt + 1) * 128])
                .then_inc(_sx, 16)))
            s_x.bump(32)
            xw = s_x.v
            for cg in range(2):
                pst = (psA, psB)[cg]

                def mmo(t, tt=tt, cg=cg, pst=pst, xw=xw, wow=wow):
                    t.wait_ge(_sx, xw)
                    t.wait_ge(_sg, wow)
                    for kc in range(8):
                        mmi = t.matmul(pst[:, :], ybuf[:, kc, :],
                                 wbig[:, kc, cg * 512:(cg + 1) * 512],
                                 start=(kc == 0), stop=(kc == 7))
                    mmi.then_inc(_smm, 1)
                p.add("tensor", mmo)
                s_mm.bump(1)
                p.add("vector", lambda v, cg=cg, pst=pst, w2=s_mm.v, xw=xw: (
                    v.wait_ge(_smm, w2), v.wait_ge(_sx, xw),
                    v.tensor_add(souo[:, cg * 512:(cg + 1) * 512], pst[:, :],
                                 sxt[:, cg * 512:(cg + 1) * 512])))
            p.emit()
            p.add("sync", lambda s, tt=tt, w2=s_o.v + 16: (
                s.dma_start(out=o_x[tt * 128:(tt + 1) * 128, :], in_=souo[:, :]).then_inc(_so, 16),
                s.wait_ge(_so, w2)))
            s_o.bump(16)
            p.emit()
    return nc


def inputs_l1f_attn(ins, core, xcur, layer):
    start = TLOC * core
    xh = np.zeros((EXT, C), np.float32)
    lo = start - H0
    src_lo = max(0, lo)
    xh[src_lo - lo:] = xcur[src_lo:start + TLOC]
    DH = 64
    half = np.arange(0, DH, 2, dtype=np.float64)
    inv_freq = 1.0 / 10000.0 ** (half / DH)
    tpos = np.arange(lo, start + TLOC, dtype=np.float64)
    freqs = tpos[:, None] * inv_freq[None, :]
    base_scale = (half + 0.4 * DH) / (1.4 * DH)
    power = (tpos - T // 2) / 128.0
    rsc = base_scale[None, :] ** power[:, None]
    cosf = np.cos(freqs); sinf = np.sin(freqs)
    def tab(m):
        m64 = np.concatenate([m, m], axis=1)
        return np.ascontiguousarray(np.tile(m64.T, (2, 1)))
    tqc_ = tab(cosf * rsc); tqs_ = tab(sinf * rsc)
    tkc_ = tab(cosf / rsc); tks_ = tab(sinf / rsc)
    mskA = np.zeros((128, 1024), np.float32)
    msk0 = np.zeros((128, 1024), np.float32)
    mskTb = np.zeros((128, 1024), np.float32)
    mskT0 = np.zeros((128, 1024), np.float32)
    for qt in range(2):
        for pp in range(128):
            i = qt * 128 + pp
            for kk in range(512):
                valid = (kk >= i) and (kk <= i + 256)
                first_invalid = kk < 256
                mskA[pp, qt * 512 + kk] = 0.0 if valid else -1e30
                msk0[pp, qt * 512 + kk] = (0.0 if (valid and not (core == 0 and first_invalid))
                                           else -1e30)
    for ktc in range(4):
        for pp in range(128):
            kk = ktc * 128 + pp
            for qq in range(256):
                valid = (kk >= qq) and (kk <= qq + 256)
                first_invalid = kk < 256
                col = ktc * 256 + qq
                mskTb[pp, col] = 1.0 if valid else 0.0
                mskT0[pp, col] = 1.0 if (valid and not (core == 0 and first_invalid)) else 0.0
    bind = np.zeros((128, 8, 16), np.float32)
    bind2 = np.zeros((16, 8, 128), np.float32)
    for kc in range(8):
        for pp in range(128):
            h = 2 * kc + pp // 64
            bind[pp, kc, h] = 1.0
            bind2[h, kc, pp] = 1.0
    erows = np.zeros((32, 32, 128), np.float32)
    for r in range(32):
        erows[r, r, :] = 1.0
    prm2 = np.zeros((128, 128), np.float32)
    for b in range(2):
        for i in range(32):
            prm2[b * 64 + 32 + i, b * 64 + i] = -1.0
            prm2[b * 64 + i, b * 64 + 32 + i] = 1.0
    return {
        "xh": f32c(xh),
        "wqkv": f32c(ins["L1_ffn_Wqkv"][layer]),
        "wo": f32c(ins["L1_ffn_Wo"][layer]),
        "qsc": f32c(np.tile(ins["L1_ffn_q_scale"][layer], 2)[:, None].repeat(8, 1)),
        "ksc": f32c(np.tile(ins["L1_ffn_k_scale"][layer], 2)[:, None].repeat(8, 1)),
        "tqc": f32c(tqc_), "tqs": f32c(tqs_), "tkc": f32c(tkc_), "tks": f32c(tks_),
        "mk0": msk0, "mkN": mskA, "mtb": mskTb, "mt0": mskT0,
        "bind": f32c(bind.reshape(128, 128)),
        "bind2": f32c(bind2.reshape(16, 8 * 128)),
        "erows": f32c(erows.reshape(32, 32 * 128)),
        "prm2": prm2,
        "ident": f32c(np.eye(128)),
    }



EXT2 = 2176

def build_l1f_ffn(final=False, fu=None):
    nc = fu["nc"] if fu is not None else bass.Bass(
        "TRN2", target_bir_lowering=False, detect_race_conditions=False)
    pfx = fu["pfx"] if fu is not None else ""
    xh = dram_in(nc, fu, "xh", [EXT2, C], F32)
    w1a = dram_in(nc, fu, "w1a", [C, 2730], F32)
    w1g = dram_in(nc, fu, "w1g", [C, 2730], F32)
    w2 = dram_in(nc, fu, "w2", [2816, C], F32)
    b1a = dram_in(nc, fu, "b1a", [128, 22], F32)
    b1g = dram_in(nc, fu, "b1g", [128, 22], F32)
    b2r = dram_in(nc, fu, "b2r", [1, C], F32)
    lng = dram_in(nc, fu, "lng", [128, 8], F32)
    lnb = dram_in(nc, fu, "lnb", [128, 8], F32)
    ident = dram_in(nc, fu, "ident", [128, 128], F32)
    if final:
        s2r = dram_in(nc, fu, "s2r", [TLOC, C], F32)
    o_x = dram_out(nc, fu, "o_x", [TLOC, C], F32)

    with ExitStack() as _st:
        E = _st.enter_context
        xfm = E(nc.sbuf_tensor(pfx + "xfm", [128, 8, EXT2], BF16))
        prodf = E(nc.sbuf_tensor(pfx + "prodf", [128, 22, TLOC], BF16))
        sw2 = E(nc.sbuf_tensor(pfx + "sw2", [128, 22, C], BF16))
        swa = E(nc.sbuf_tensor(pfx + "swa", [128, 8, 128], BF16))
        swb = E(nc.sbuf_tensor(pfx + "swb", [128, 8, 128], BF16))
        sid = E(nc.sbuf_tensor(pfx + "sid", [128, 128], BF16))
        sb1a = E(nc.sbuf_tensor(pfx + "sb1a", [128, 22], F32))
        sb1g = E(nc.sbuf_tensor(pfx + "sb1g", [128, 22], F32))
        sb2 = E(nc.sbuf_tensor(pfx + "sb2", [1, C], BF16))
        slng = E(nc.sbuf_tensor(pfx + "slng", [128, 8], F32))
        slnb = E(nc.sbuf_tensor(pfx + "slnb", [128, 8], F32))
        srow = E(nc.sbuf_tensor(pfx + "srow", [1, TLOC], F32))
        srow2 = E(nc.sbuf_tensor(pfx + "srow2", [1, TLOC], F32))
        srow3 = E(nc.sbuf_tensor(pfx + "srow3", [1, TLOC // 2], F32))
        sxt = E(nc.sbuf_tensor(pfx + "sxt", [128, C], F32))
        sbt = E(nc.sbuf_tensor(pfx + "sbt", [128, 128], BF16))
        souo = E(nc.sbuf_tensor(pfx + "souo", [128, C], F32))
        sst2 = E(nc.sbuf_tensor(pfx + "sst2", [128, C], F32))
        mcol = E(nc.sbuf_tensor(pfx + "mcol", [128, 1], F32))
        vcol = E(nc.sbuf_tensor(pfx + "vcol", [128, 1], F32))
        sgel = E(nc.sbuf_tensor(pfx + "sgel", [128, 512], BF16))
        ssq = sgel
        onesc = E(nc.sbuf_tensor(pfx + "onesc", [128, 1], BF16))
        onesr = E(nc.sbuf_tensor(pfx + "onesr", [1, 128], BF16))
        onesrf = E(nc.sbuf_tensor(pfx + "onesrf", [1, 128], F32))
        psA = E(nc.psum_tensor(pfx + "psA", [128, 512], F32))
        psB = E(nc.psum_tensor(pfx + "psB", [128, 512], F32))
        psR = E(nc.psum_tensor(pfx + "psR", [1, 512], F32))
        psR2 = E(nc.psum_tensor(pfx + "psR2", [1, 512], F32))
        psP1 = E(nc.psum_tensor(pfx + "psP1", [128, 512], F32))
        psP2 = E(nc.psum_tensor(pfx + "psP2", [128, 512], F32))
        psT = E(nc.psum_tensor(pfx + "psT", [128, 128], BF16))
        p, (_sx, _sg, _smm, _sev, _so), (s_x, s_g, s_mm, s_ev, s_o) = fu_start(nc, fu, E)

        def loads(g):
            g.dma_start(out=sw2[:, :, :],
                        in_=w2.rearrange("(k p) c -> p k c", p=128)).then_inc(_sg, 16)
            g.dma_start(out=sid[:, :], in_=ident).then_inc(_sg, 16)
            g.dma_start(out=sb2[:, :], in_=b2r).then_inc(_sg, 16)
        p.add("gpsimd", loads)
        p.add("sync", lambda s: (
            s.dma_start(out=sb1a[:, :], in_=b1a).then_inc(_sx, 16),
            s.dma_start(out=sb1g[:, :], in_=b1g).then_inc(_sx, 16),
            s.dma_start(out=slng[:, :], in_=lng).then_inc(_sx, 16),
            s.dma_start(out=slnb[:, :], in_=lnb).then_inc(_sx, 16)))
        s_g.bump(48); s_x.bump(64)
        p.add("vector", lambda v, w=s_x.v, wg=s_g.v: (
            v.wait_ge(_sx, w), v.wait_ge(_sg, wg),
            v.memset(onesr[:, :], 1.0), v.memset(onesrf[:, :], 1.0)))
        p.add("gpsimd", lambda g: g.memset(onesc[:, :], 1.0))
        p.emit()

        for tt in range(EXT2 // 128):
            p.add("sync", lambda s, tt=tt: s.dma_start(
                out=sxt[:, :], in_=xh[tt * 128:(tt + 1) * 128, :]).then_inc(_sx, 16))
            s_x.bump(16)
            p.add("vector", lambda v, w=s_x.v: v.wait_ge(_sx, w))
            p.emit()
            for kc in range(8):
                p.add("vector", lambda v, kc=kc: v.tensor_copy(
                    sbt[:, :], sxt[:, kc * 128:(kc + 1) * 128]))
                p.emit()
                p.add("tensor", lambda t, kc=kc: t.transpose(
                    psT[:, :], sbt[:, :], sid[:, :]).then_inc(_smm, 1))
                s_mm.bump(1)
                p.add("vector", lambda v, tt=tt, kc=kc, w=s_mm.v: (
                    v.wait_ge(_smm, w),
                    v.tensor_copy(xfm[:, kc, tt * 128:(tt + 1) * 128], psT[:, :])))
                p.emit()

        def shsl(kc, lo, hi):
            base = 128
            if kc < 4:
                return xfm[:, kc, base + lo:base + hi]
            return xfm[:, kc, base + lo - 1:base + hi - 1]

        for tg in range(4):
            def mmst(t, tg=tg):
                for kc in range(8):
                    mmi = t.matmul(psR[:, :], onesc[:, :], shsl(kc, tg * 512, (tg + 1) * 512),
                             start=(kc == 0), stop=(kc == 7))
                mmi.then_inc(_smm, 1)
            p.add("tensor", mmst)
            s_mm.bump(1)
            p.add("vector", lambda v, tg=tg, w=s_mm.v: (
                v.wait_ge(_smm, w),
                v.tensor_copy(srow[0:1, tg * 512:(tg + 1) * 512], psR[:, :])))
            p.emit()
            for kc in range(8):
                p.add("scalar", lambda s, kc=kc, tg=tg: s.activation(
                    ssq[:, :], shsl(kc, tg * 512, (tg + 1) * 512), AF.Square,
                    bias=0.0, scale=1.0))
                p.emit()
                p.add("tensor", lambda t, kc=kc, tg=tg: t.matmul(
                    psR2[:, :], onesc[:, :], ssq[:, :],
                    start=(kc == 0), stop=(kc == 7)).then_inc(_smm, 1))
                s_mm.bump(1)
                p.emit()
            p.add("vector", lambda v, tg=tg, w=s_mm.v: (
                v.wait_ge(_smm, w),
                v.tensor_copy(srow2[0:1, tg * 512:(tg + 1) * 512], psR2[:, :])))
            p.emit()

        def rowops(v):
            v.tensor_scalar(srow[:, :], srow[:, :], 1.0 / C, None, ALU.mult)
            v.tensor_scalar(srow2[:, :], srow2[:, :], 1.0 / C, None, ALU.mult)
            for hh in range(2):
                sl = slice(hh * (TLOC // 2), (hh + 1) * (TLOC // 2))
                v.tensor_tensor(srow3[:, :], srow[:, sl], srow[:, sl], ALU.mult)
                v.tensor_sub(srow2[:, sl], srow2[:, sl], srow3[:, :])
            v.tensor_scalar(srow2[:, :], srow2[:, :], 1.0, 1e-5, ALU.mult, ALU.add)
        p.add("vector", rowops)
        p.emit()
        p.add("scalar", lambda s: s.activation(srow2[:, :], srow2[:, :], AF.Sqrt,
                                               bias=0.0, scale=1.0))
        p.emit()
        p.add("vector", lambda v: (
            v.reciprocal(srow2[:, :], srow2[:, :]),
            v.tensor_mul(srow[:, :], srow[:, :], srow2[:, :]),
            v.tensor_scalar(srow[:, :], srow[:, :], -1.0, None, ALU.mult)))
        p.emit()

        for tg in (3, 2, 1, 0):
            def mmbc(t, tg=tg):
                t.matmul(psP1[:, :], onesrf[0:1, :], srow2[0:1, tg * 512:(tg + 1) * 512],
                         start=True, stop=True)
                t.matmul(psP2[:, :], onesrf[0:1, :], srow[0:1, tg * 512:(tg + 1) * 512],
                         start=True, stop=True).then_inc(_smm, 1)
            p.add("tensor", mmbc)
            s_mm.bump(1)

            def apl(v, tg=tg, w=s_mm.v):
                v.wait_ge(_smm, w)
                for kc in range(8):
                    v.tensor_tensor(sst2[:, 0:512], shsl(kc, tg * 512, (tg + 1) * 512),
                                    psP1[:, :], ALU.mult)
                    v.tensor_tensor(sst2[:, 0:512], sst2[:, 0:512], psP2[:, :], ALU.add)
                    v.tensor_scalar(xfm[:, kc, 128 + tg * 512:128 + (tg + 1) * 512],
                                    sst2[:, 0:512], slng[:, kc:kc + 1], slnb[:, kc:kc + 1],
                                    ALU.mult, ALU.add)
            p.add("vector", apl)
            p.emit()

        wps = ((swa, swb), (swa, swb))
        for jc in range(22):
            wa, wg_ = wps[jc % 2]
            jw = 128 if jc < 21 else 42
            p.add("gpsimd", lambda g, jc=jc, wa=wa, wg_=wg_, jw=jw: (
                g.dma_start(out=wa[:, :, 0:jw],
                            in_=w1a[:, jc * 128:jc * 128 + jw].rearrange("(k p) j -> p k j", p=128))
                .then_inc(_sg, 16),
                g.dma_start(out=wg_[:, :, 0:jw],
                            in_=w1g[:, jc * 128:jc * 128 + jw].rearrange("(k p) j -> p k j", p=128))
                .then_inc(_sg, 16)))
            s_g.bump(32)
            gw = s_g.v
            for tg in range(4):
                def mmh(t, jc=jc, tg=tg, wa=wa, wg_=wg_, jw=jw, gw=gw):
                    t.wait_ge(_sg, gw)
                    for kc in range(8):
                        t.matmul(psA[0:jw, :], wa[:, kc, 0:jw],
                                 xfm[:, kc, 128 + tg * 512:128 + (tg + 1) * 512],
                                 start=(kc == 0), stop=(kc == 7))
                    for kc in range(8):
                        mmi = t.matmul(psB[0:jw, :], wg_[:, kc, 0:jw],
                                 xfm[:, kc, 128 + tg * 512:128 + (tg + 1) * 512],
                                 start=(kc == 0), stop=(kc == 7))
                    mmi.then_inc(_smm, 1)
                p.add("tensor", mmh)
                s_mm.bump(1)
                p.add("scalar", lambda s, jc=jc, jw=jw, w=s_mm.v: (
                    s.wait_ge(_smm, w),
                    s.activation(sgel[0:jw, :], psB[0:jw, :], AF.Gelu,
                                 bias=sb1g[0:jw, jc:jc + 1], scale=1.0)))
                p.emit()
                p.add("vector", lambda v, jc=jc, tg=tg, jw=jw: (
                    v.scalar_tensor_tensor(prodf[0:jw, jc, tg * 512:(tg + 1) * 512],
                                           psA[0:jw, :], sb1a[0:jw, jc:jc + 1],
                                           sgel[0:jw, :], ALU.add, ALU.mult)))
                p.emit()

        for tt in range(16):
            p.add("sync", lambda s, tt=tt: s.dma_start(
                out=sxt[:, :], in_=xh[128 + tt * 128:128 + (tt + 1) * 128, :]).then_inc(_sx, 16))
            s_x.bump(16)
            xw = s_x.v
            for cg in range(2):
                pst = (psA, psB)[cg]

                def mmo(t, tt=tt, cg=cg, pst=pst):
                    for kc in range(22):
                        jw = 128 if kc < 21 else 42
                        t.matmul(pst[:, :], prodf[0:jw, kc, tt * 128:(tt + 1) * 128],
                                 sw2[0:jw, kc, cg * 512:(cg + 1) * 512],
                                 start=(kc == 0), stop=False)
                    t.matmul(pst[:, :], onesr[0:1, 0:128],
                             sb2[0:1, cg * 512:(cg + 1) * 512],
                             start=False, stop=True).then_inc(_smm, 1)
                p.add("tensor", mmo)
                s_mm.bump(1)
                p.add("vector", lambda v, cg=cg, pst=pst, w=s_mm.v, xw=xw: (
                    v.wait_ge(_smm, w), v.wait_ge(_sx, xw),
                    v.tensor_add(souo[:, cg * 512:(cg + 1) * 512], pst[:, :],
                                 sxt[:, cg * 512:(cg + 1) * 512])))
            p.emit()
            if final:
                p.add("sync", lambda s, tt=tt: s.dma_start(
                    out=sxt[:, :], in_=s2r[tt * 128:(tt + 1) * 128, :]).then_inc(_sx, 16))
                s_x.bump(16)
                p.add("vector", lambda v, w=s_x.v: (
                    v.wait_ge(_sx, w),
                    v.tensor_add(souo[:, :], souo[:, :], sxt[:, :])))
                p.emit()
                emit_ln_tm(p, nc, souo, sst2, mcol, vcol)
            p.add("sync", lambda s, tt=tt, w=s_o.v + 16: (
                s.dma_start(out=o_x[tt * 128:(tt + 1) * 128, :], in_=souo[:, :]).then_inc(_so, 16),
                s.wait_ge(_so, w)))
            s_o.bump(16)
            p.emit()
    return nc


def inputs_l1f_ffn(ins, core, xcur, layer, s2=None):
    start = TLOC * core
    xh = np.zeros((EXT2, C), np.float32)
    lo = start - 128
    src_lo = max(0, lo)
    xh[src_lo - lo:] = xcur[src_lo:start + TLOC]
    W1 = ins["L1_ffn_W1"][layer]
    b1 = ins["L1_ffn_b1"][layer]
    DIN = 2730

    def pack22(vec):
        out = np.zeros((128, 22), np.float32)
        for jc in range(22):
            jw = 128 if jc < 21 else 42
            out[0:jw, jc] = vec[jc * 128:jc * 128 + jw]
        return out
    d = {
        "xh": f32c(xh),
        "w1a": f32c(W1[:, :DIN]),
        "w1g": f32c(W1[:, DIN:]),
        "w2": f32c(np.concatenate([ins["L1_ffn_W2"][layer],
                                   np.zeros((2816 - 2730, C), np.float32)], axis=0)),
        "b1a": pack22(b1[:DIN]),
        "b1g": pack22(b1[DIN:]),
        "b2r": f32c(ins["L1_ffn_b2"][layer][None, :]),
        "lng": f32c(ins["L1_ffn_ln_g"][layer].reshape(8, 128).T),
        "lnb": f32c(ins["L1_ffn_ln_b"][layer].reshape(8, 128).T),
        "ident": f32c(np.eye(128)),
    }
    if s2 is not None:
        d["s2r"] = f32c(s2[start:start + TLOC])
    return d




def build_fused():
    nc = bass.Bass("TRN2", target_bir_lowering=False, detect_race_conditions=False)
    ALLC = [list(range(NCORE))]

    def D(nm, sh, dt):
        return nc.dram_tensor(nm, sh, dt).ap()

    t_u = D("t_u", [DSL, T], BF16)
    t_z = D("t_z", [DSL, T], BF16)
    t_xdblp = D("t_xdblp", [97, T], F32)
    t_xdag = D("t_xdag", [NCORE, 97, T], F32)
    t_ycat = D("t_ycat", [512, T], BF16)
    t_ycin8 = D("t_ycin8", [NCORE, 512, TLOC], BF16)
    t_ycown = D("t_ycown", [4096, TLOC], BF16)
    t_svec = D("t_svec", [1, T], F32)
    t_svr = D("t_svr", [1, TLOC], F32)
    t_s0 = D("t_s0", [TLOC, C], F32)
    t_qf = D("t_qf", [128, TLOC], BF16)
    t_kf = D("t_kf", [128, TLOC], BF16)
    t_kfag = D("t_kfag", [NCORE, 128, TLOC], BF16)
    t_vt = D("t_vt", [TLOC, 128], BF16)
    t_vtag = D("t_vtag", [T, 128], BF16)
    t_w = D("t_w", [TLOC, 16], F32)
    t_qn = D("t_qn", [16, 128], F32)
    t_km = D("t_km", [1, 1], F32)
    t_kmag = D("t_kmag", [8, 1], F32)
    t_s1 = D("t_s1", [TLOC, C], F32)
    t_s1ft = D("t_s1ft", [C, TLOC], BF16)
    t_sfstg = D("t_sfstg", [NCORE, 128, TLOC], BF16)
    t_yn = D("t_yn", [T, 128], BF16)
    t_ynstage = D("t_ynstage", [NCORE, TLOC, 128], BF16)
    t_ynown = D("t_ynown", [TLOC, C], BF16)
    t_s2 = D("t_s2", [TLOC, C], F32)
    t_dx = [D(f"t_dx{k}", [TLOC, C], F32) for k in range(11)]
    t_rsin = D("t_rsin", [4096, C], BF16)
    t_hxout = D("t_hxout", [512, C], BF16)
    t_rsinf = D("t_rsinf", [1024, C], BF16)
    t_hxoutf = D("t_hxoutf", [128, C], BF16)
    t_xha = D("t_xha", [EXT, C], F32)
    t_xhf = D("t_xhf", [EXT2, C], F32)

    g_dofs = nc.dram_tensor("g_dofs", [1, 4], I32, kind="ExternalInput").ap()
    g_m8 = nc.dram_tensor("g_m8", [128, 8], F32, kind="ExternalInput").ap()
    g_idf = nc.dram_tensor("g_idf", [128, 128], F32, kind="ExternalInput").ap()
    o_x = nc.dram_tensor("o_x", [TLOC, C], F32, kind="ExternalOutput").ap()

    with ExitStack() as _st:
        E = _st.enter_context
        hs = tuple(E(nc.semaphore(f"fs{i}")) for i in range(5))
        _sx, _sg, _smm, _sev, _so = hs
        ctrs = tuple(Sem(h) for h in hs)
        s_x, s_g, s_mm, s_ev, s_o = ctrs
        p = Plan(nc)
        sofs = E(nc.sbuf_tensor("g_sofs", [1, 4], I32))
        sm8 = E(nc.sbuf_tensor("g_sm8", [128, 8], F32))

        p.add("sync", lambda s: (
            s.dma_start(out=sofs[:, :], in_=g_dofs).then_inc(_sx, 16),
            s.dma_start(out=sm8[:, :], in_=g_m8).then_inc(_sx, 16)))
        s_x.bump(32)
        p.add("gpsimd", lambda g, w=s_x.v: g.wait_ge(_sx, w))
        p.emit()
        nc.all_engine_barrier(sem_only=True)

        def FU(px, ext):
            return {"nc": nc, "pfx": px, "p": p, "sems": hs, "ctrs": ctrs, "ext": ext}

        def bar():
            p.emit()
            nc.all_engine_barrier(sem_only=True)

        def ag(src_ap, dst_ap):
            s_ev.bump(1)

            def f(g, w=s_ev.v):
                g.collective_compute("AllGather", ALU.bypass, replica_groups=ALLC,
                                     ins=[src_ap.opt()], outs=[dst_ap.opt()]).then_inc(_sev, 1)
                g.wait_ge(_sev, w)
            p.add("gpsimd", f)
            bar()

        _gctr = [0]

        def a2a(src_ap, dst_ap):
            s_ev.bump(1)

            def f(g, w=s_ev.v):
                with allow_ici_all_to_all():
                    g.collective_compute("AllToAll", ALU.bypass, replica_groups=ALLC,
                                         ins=[src_ap.opt()], outs=[dst_ap.opt()]).then_inc(_sev, 1)
                g.wait_ge(_sev, w)
            p.add("gpsimd", f)
            bar()

        def dyn_glue(fn, ndma, oidx):
            s_g.bump(16 * ndma)
            _gctr[0] += 1

            def f(g, w=s_g.v, nm=f"gdr{_gctr[0]}"):
                with g.register(nm) as r0:
                    g.reg_load(r0, sofs[0:1, oidx:oidx + 1])
                    fn(g, g.snap(r0))
                g.wait_ge(_sg, w)
            p.add("gpsimd", f)
            bar()

        build_l0a_pre(fu=FU("a1", {"o_u": t_u, "o_z": t_z, "o_xdblp": t_xdblp}))
        bar()
        ag(t_xdblp, t_xdag)
        build_l0a_scan(fu=FU("a3", {"xdblp": t_xdag, "u_in": t_u, "z_in": t_z,
                                    "o_ycat": t_ycat, "o_svec": t_svec}))
        bar()
        s_g.bump(16 * NCORE)
        p.add("gpsimd", lambda g, w=s_g.v: (
            [g.dma_start(out=t_ycin8[j],
                         in_=t_ycat[:, j * TLOC:(j + 1) * TLOC]).then_inc(_sg, 16)
             for j in range(NCORE)],
            g.wait_ge(_sg, w)))
        bar()
        a2a(t_ycin8, t_ycown)

        def g2b(g, tok):
            g.dma_start(out=t_svr, in_=t_svec[0:1, bass.ds(tok, TLOC)]).then_inc(_sg, 16)
        dyn_glue(g2b, 1, 0)
        build_l0a_out(fu=FU("a4", {"ycat": t_ycown, "svr": t_svr, "o_s": t_s0}))
        bar()

        build_l0f_qkv(fu=FU("b1", {"ss": t_s0, "o_qf": t_qf, "o_kf": t_kf, "o_vt": t_vt,
                                   "o_w": t_w, "o_qn": t_qn, "o_km": t_km}))
        bar()
        ag(t_kf, t_kfag)
        ag(t_vt, t_vtag)
        ag(t_km, t_kmag)
        build_l0f_attn(fu=FU("b2", {
            "kf": t_kfag.rearrange("c p t -> p c t"),
            "vt": t_vtag, "qf": t_qf,
            "qn": t_qn.rearrange("(o a) b -> o (a b)", o=1),
            "kms": t_kmag, "wrt": t_w, "ss": t_s0, "o_s": t_s1}))
        bar()

        with ExitStack() as _gs:
            gE = _gs.enter_context
            stile = gE(nc.sbuf_tensor("g_trt", [128, C], F32))
            sid = gE(nc.sbuf_tensor("g_trid", [128, 128], F32))
            sout = gE(nc.sbuf_tensor("g_tro", [128, 8, 128], BF16))
            pstr = gE(nc.psum_tensor("g_trp", [128, 128], F32))
            s_x.bump(16)
            p.add("sync", lambda s: s.dma_start(out=sid[:, :], in_=g_idf).then_inc(_sx, 16))
            p.emit()
            for tt in range(16):
                s_x.bump(16)
                xw = s_x.v
                p.add("sync", lambda s, tt=tt, w=s_o.v: (
                    s.wait_ge(_so, w),
                    s.dma_start(out=stile[:, :],
                                in_=t_s1[tt * 128:(tt + 1) * 128, :]).then_inc(_sx, 16)))
                for f in range(8):
                    s_mm.bump(1)

                    def mmt(t, f=f, xw=xw, ew=s_ev.v):
                        t.wait_ge(_sx, xw)
                        t.wait_ge(_sev, ew)
                        t.transpose(pstr[:, :], stile[:, f * 128:(f + 1) * 128],
                                    sid[:, :]).then_inc(_smm, 1)
                    p.add("tensor", mmt)
                    s_ev.bump(1)
                    p.add("vector", lambda v, f=f, w=s_mm.v: (
                        v.wait_ge(_smm, w),
                        v.tensor_copy(sout[:, f, :], pstr[:, :]).then_inc(_sev, 1)))
                    p.emit()
                s_o.bump(16)
                p.add("sync", lambda s, tt=tt, w=s_ev.v, wo=s_o.v: (
                    s.wait_ge(_sev, w),
                    s.dma_start(
                        out=t_s1ft.rearrange("(f p) t -> p f t", p=128)[:, :, tt * 128:(tt + 1) * 128],
                        in_=sout[:, :, :]).then_inc(_so, 16),
                    s.wait_ge(_so, wo)))
                p.emit()
        bar()
        a2a(t_s1ft, t_sfstg)
        build_l1a_ret(fu=FU("c1", {"sfm": t_sfstg.rearrange("r p t -> p r t"), "o_yn": t_yn}))
        bar()
        a2a(t_yn, t_ynstage)
        s_g.bump(16 * NCORE)
        p.add("gpsimd", lambda g, w=s_g.v: (
            [g.dma_start(out=t_ynown[:, 128 * h:128 * (h + 1)],
                         in_=t_ynstage[h]).then_inc(_sg, 16) for h in range(NCORE)],
            g.wait_ge(_sg, w)))
        bar()
        build_l1a_out(fu=FU("c2", {"sfT": t_s1ft, "ss": t_s1, "yn": t_ynown, "o_s": t_s2}))
        bar()

        def halo_glue(xp, sub, gi):
            HR = 4 if sub == 0 else 1
            rsin = t_rsin if sub == 0 else t_rsinf
            rsout = t_hxout if sub == 0 else t_hxoutf
            with ExitStack() as _hs:
                hE = _hs.enter_context
                tb = hE(nc.sbuf_tensor(f"g_tb{gi}", [128, HR * C], F32))
                tb8 = hE(nc.sbuf_tensor(f"g_tc{gi}", [128, NCORE, HR * C], BF16))
                s_x.bump(16)
                p.add("sync", lambda s: s.dma_start(
                    out=tb[:, :],
                    in_=xp[TLOC - 128 * HR:TLOC, :].rearrange("(k p) c -> p k c", p=128))
                    .then_inc(_sx, 16))
                p.emit()
                s_ev.bump(NCORE)
                p.add("vector", lambda v, w=s_x.v: (
                    v.wait_ge(_sx, w),
                    [v.tensor_tensor(tb8[:, k, :], tb[:, :],
                                     sm8[:, k:k + 1].to_broadcast([128, HR * C]),
                                     ALU.mult).then_inc(_sev, 1) for k in range(NCORE)]))
                s_o.bump(16 * NCORE)
                p.add("sync", lambda s, w=s_ev.v, wo=s_o.v: (
                    s.wait_ge(_sev, w),
                    [s.dma_start(
                        out=rsin[128 * HR * k:128 * HR * (k + 1), :]
                        .rearrange("(j p) c -> p j c", p=128),
                        in_=tb8[:, k, :]).then_inc(_so, 16) for k in range(NCORE)],
                    s.wait_ge(_so, wo)))
                p.emit()
                bar()
            s_ev.bump(1)

            def frs(g, w=s_ev.v):
                g.collective_compute("ReduceScatter", ALU.add, replica_groups=ALLC,
                                     ins=[rsin.opt()], outs=[rsout.opt()]).then_inc(_sev, 1)
                g.wait_ge(_sev, w)
            p.add("gpsimd", frs)
            bar()
            HAL, xh = (H0, t_xha) if sub == 0 else (128, t_xhf)
            s_g.bump(32)

            def fst(g, w=s_g.v, HAL=HAL, xh=xh, rsout=rsout):
                g.dma_start(out=xh[0:HAL, :], in_=rsout).then_inc(_sg, 16)
                g.dma_start(out=xh[HAL:HAL + TLOC, :], in_=xp).then_inc(_sg, 16)
                g.wait_ge(_sg, w)
            p.add("gpsimd", fst)
            bar()

        xprev = t_s2
        for L in range(6):
            for sub in range(2):
                k = 2 * L + sub
                last = (k == 11)
                halo_glue(xprev, sub, k)
                xout = o_x if last else t_dx[k]
                if sub == 0:
                    build_l1f_attn(fu=FU(f"da{L}", {"xh": t_xha, "o_x": xout}))
                else:
                    ext = {"xh": t_xhf, "o_x": xout}
                    if last:
                        ext["s2r"] = t_s2
                    build_l1f_ffn(final=last, fu=FU(f"df{L}", ext))
                bar()
                xprev = xout
    return nc


def inputs_fused(ins, core):
    m = {}

    def put(px, d, drop=()):
        for kk, vv in d.items():
            if kk not in drop:
                m[px + kk] = vv
    zf = np.zeros((NCORE, 97, T), np.float32)
    zu = np.zeros((2048, T), BFNP)
    zyc = np.zeros((4096, T), BFNP)
    zsv = np.zeros((1, T), np.float32)
    zs = np.zeros((T, C), np.float32)
    zb1 = [{"o_qf": np.zeros((128, TLOC), BFNP), "o_qn": np.zeros((16, 128), np.float32),
            "o_km": np.zeros((1, 1), np.float32), "o_w": np.zeros((TLOC, 16), np.float32)}
           for _ in range(NCORE)]
    put("a1", inputs_l0a_pre(ins, core))
    put("a3", inputs_l0a_scan(ins, core, zf, zu, zu), drop=("xdblp", "u_in", "z_in"))
    d4 = inputs_l0a_out(ins, core, zyc, zsv)
    perm = np.concatenate([np.concatenate([r * 256 + np.arange(256),
                                           2048 + r * 256 + np.arange(256)])
                           for r in range(NCORE)])
    d4["w1T"] = np.ascontiguousarray(d4["w1T"][perm])
    put("a4", d4, drop=("ycat", "svr"))
    put("b1", inputs_l0f_qkv(ins, core, zs), drop=("ss",))
    put("b2", inputs_l0f_attn(ins, core, np.zeros((128, T), BFNP),
                              np.zeros((T, 128), BFNP), zb1, zs),
        drop=("kf", "vt", "qf", "qn", "kms", "wrt", "ss"))
    put("c1", inputs_l1a_ret(ins, core, zs), drop=("sfm",))
    put("c2", inputs_l1a_out(ins, core, zs, np.zeros((T, C), BFNP)),
        drop=("sfT", "ss", "yn"))
    for L in range(6):
        put(f"da{L}", inputs_l1f_attn(ins, core, zs, L), drop=("xh",))
        put(f"df{L}", inputs_l1f_ffn(ins, core, zs, L, s2=zs if L == 5 else None),
            drop=("xh", "s2r"))
    prev = (core - 1) % NCORE
    m["g_dofs"] = np.array([[core * TLOC, core * 4096 * TLOC,
                             core * NCORE * 128 * TLOC, core * TLOC * 128]], np.int32)
    m8 = np.zeros((128, 8), np.float32)
    if core != NCORE - 1:
        m8[:, (core + 1) % NCORE] = 1.0
    m["g_m8"] = m8
    m["g_idf"] = np.eye(128, dtype=np.float32)
    return m


def kernel(**inputs):
    ins = inputs
    nc = prog("fused", build_fused)
    maps = [inputs_fused(ins, c) for c in range(NCORE)]
    res = run_bass_kernel_spmd(nc, maps, core_ids=list(range(NCORE))).results
    xcur = np.concatenate([np.asarray(r["o_x"]) for r in res], axis=0)
    return np.ascontiguousarray(xcur, dtype=np.float32).reshape(1, T, C)


# version 12 (final)
# speedup vs baseline: 1.0402x; 1.0402x over previous
import numpy as np
import ml_dtypes
from contextlib import ExitStack
import concourse.bass as bass
import concourse.mybir as mybir
from concourse.bass_utils import run_bass_kernel_spmd
from concourse.aot_env import allow_ici_all_to_all

BFNP = ml_dtypes.bfloat16
F32 = mybir.dt.float32
I32 = mybir.dt.int32
BF16 = mybir.dt.bfloat16
AF = mybir.ActivationFunctionType
ALU = mybir.AluOpType
AX = mybir.AxisListType

T, C, NCORE, TLOC = 16384, 1024, 8, 2048
DI, DS, DTR = 2048, 16, 64
DSL = DI // NCORE
DNL = DSL * DS

ENGS = ("sync", "tensor", "vector", "scalar", "gpsimd")


class Plan:

    def __init__(self, nc):
        self.nc = nc
        self.q = {e: [] for e in ENGS}

    def add(self, eng, fn):
        self.q[eng].append(fn)

    def emit(self):
        if not any(self.q.values()):
            return
        with self.nc.Block() as blk:
            for eng in ENGS:
                fns = self.q[eng]
                if fns:
                    getattr(blk, eng)(lambda e, fns=fns: [f(e) for f in fns])
        self.q = {e: [] for e in ENGS}


class Sem:
    def __init__(self, handle):
        self.h = handle
        self.v = 0

    def bump(self, n):
        self.v += n
        return self.v


def f32c(a):
    return np.ascontiguousarray(np.asarray(a, dtype=np.float32))


def dram_in(nc, fu, name, shape, dt):
    if fu is not None and name in fu["ext"]:
        return fu["ext"][name]
    pfx = fu["pfx"] if fu is not None else ""
    return nc.dram_tensor(pfx + name, shape, dt, kind="ExternalInput").ap()


def dram_out(nc, fu, name, shape, dt):
    if fu is not None and name in fu["ext"]:
        return fu["ext"][name]
    pfx = fu["pfx"] if fu is not None else ""
    return nc.dram_tensor(pfx + name, shape, dt, kind="ExternalOutput").ap()


def fu_start(nc, fu, E):
    if fu is not None:
        return fu["p"], fu["sems"], fu["ctrs"]
    p = Plan(nc)
    hs = [E(nc.semaphore(f"s{i}")) for i in range(5)]
    cs = [Sem(h) for h in hs]
    return p, tuple(hs), tuple(cs)


def two(dram_ap, g=2, p=128):
    return dram_ap.rearrange("(g p) t -> p g t", p=p)



def build_l0a_pre(fu=None):
    nc = fu["nc"] if fu is not None else bass.Bass(
        "TRN2", target_bir_lowering=False, detect_race_conditions=False)
    pfx = fu["pfx"] if fu is not None else ""
    xT = dram_in(nc, fu, "xT", [C, T], F32)
    wT = dram_in(nc, fu, "wT", [C, 512], F32)
    cwp = dram_in(nc, fu, "cwp", [128, 8], F32)
    cbp = dram_in(nc, fu, "cbp", [128, 2], F32)
    wxT = dram_in(nc, fu, "wxT", [DSL, 97], F32)

    o_u = dram_out(nc, fu, "o_u", [DSL, T], BF16)
    o_z = dram_out(nc, fu, "o_z", [DSL, T], BF16)
    o_xdblp = dram_out(nc, fu, "o_xdblp", [97, T], F32)

    TCH = 2048
    NCH = T // TCH

    with ExitStack() as _st:
        E = _st.enter_context
        swT = E(nc.sbuf_tensor(pfx + "swT", [128, 8, 512], BF16))
        scw = E(nc.sbuf_tensor(pfx + "scw", [128, 8], F32))
        scb = E(nc.sbuf_tensor(pfx + "scb", [128, 2], F32))
        swxT = E(nc.sbuf_tensor(pfx + "swxT", [128, 2, 97], BF16))
        xb0 = E(nc.sbuf_tensor(pfx + "xb0", [128, 8, TCH], BF16))
        xb1 = E(nc.sbuf_tensor(pfx + "xb1", [128, 8, TCH], BF16))
        xib = E(nc.sbuf_tensor(pfx + "xib", [128, 2, TCH + 3], BF16))
        zb = E(nc.sbuf_tensor(pfx + "zb", [128, 2, TCH], BF16))
        upre = E(nc.sbuf_tensor(pfx + "upre", [128, 2, TCH], F32))
        ub = E(nc.sbuf_tensor(pfx + "ub", [128, 2, TCH], BF16))
        xdv = E(nc.sbuf_tensor(pfx + "xdv", [97, TCH], F32))
        psA = E(nc.psum_tensor(pfx + "psA", [128, 512], F32))
        psB = E(nc.psum_tensor(pfx + "psB", [128, 512], F32))
        psX = E(nc.psum_tensor(pfx + "psX", [97, 512], F32))
        p, (_sx, _sg, _smm, _sev, _so), (s_x, s_g, s_mm, s_ev, s_o) = fu_start(nc, fu, E)
        ps = (psA, psB)

        p.add("gpsimd", lambda g: (
            g.dma_start(out=swT[:, :, :], in_=wT.rearrange("(k p) j -> p k j", p=128))
            .then_inc(_sg, 16),
            g.dma_start(out=swxT[:, :, :], in_=two(wxT)).then_inc(_sg, 16)))
        p.add("sync", lambda s: (
            s.dma_start(out=scw[:, :], in_=cwp).then_inc(_sx, 16),
            s.dma_start(out=scb[:, :], in_=cbp).then_inc(_sx, 16)))
        s_x.bump(32)
        s_g.bump(32)
        p.add("gpsimd", lambda g, w=s_x.v, wg=s_g.v: (
            g.wait_ge(_sx, w), g.wait_ge(_sg, wg),
            g.memset(xib[:, :, 0:3], 0.0)))
        p.emit()

        gctr = [0]
        for ch in range(NCH):
            xb = (xb0, xb1)[ch % 2]
            p.add("gpsimd", lambda gp, ch=ch, xb=xb: (
                gp.dma_start(out=xb[:, :, :],
                             in_=xT[:, ch * TCH:(ch + 1) * TCH]
                             .rearrange("(k p) t -> p k t", p=128)).then_inc(_sg, 16)))
            s_g.bump(16)
            xgw = s_g.v
            p.emit()

            for oc in range(4):
                for tg in range(4):
                    gi = gctr[0]
                    pst = ps[gi % 2]

                    def mm(t, oc=oc, tg=tg, gi=gi, pst=pst, xb=xb, xgw=xgw):
                        t.wait_ge(_sg, xgw)
                        if gi >= 2:
                            t.wait_ge(_sev, gi - 1)
                        for k in range(7):
                            t.matmul(pst[:, :], swT[:, k, oc * 128:(oc + 1) * 128],
                                     xb[:, k, tg * 512:(tg + 1) * 512],
                                     start=(k == 0), stop=False)
                        t.matmul(pst[:, :], swT[:, 7, oc * 128:(oc + 1) * 128],
                                 xb[:, 7, tg * 512:(tg + 1) * 512],
                                 start=False, stop=True).then_inc(_smm, 1)
                    p.add("tensor", mm)
                    s_mm.bump(1)
                    mmv = s_mm.v

                    def ev(v, oc=oc, tg=tg, pst=pst, mmv=mmv):
                        v.wait_ge(_smm, mmv)
                        dst = (xib[:, oc, 3 + tg * 512:3 + (tg + 1) * 512] if oc < 2
                               else zb[:, oc - 2, tg * 512:(tg + 1) * 512])
                        v.tensor_copy(dst, pst[:, :]).then_inc(_sev, 1)
                    p.add("vector", ev)
                    s_ev.bump(1)
                    gctr[0] += 1
            p.emit()

            def conv(v):
                for gr in range(2):
                    v.tensor_scalar(upre[:, gr, :], xib[:, gr, 3:3 + TCH],
                                    scw[:, gr * 4 + 3:gr * 4 + 4], None, ALU.mult)
                    for j in range(3):
                        v.scalar_tensor_tensor(upre[:, gr, :], xib[:, gr, j:j + TCH],
                                               scw[:, gr * 4 + j:gr * 4 + j + 1],
                                               upre[:, gr, :], ALU.mult, ALU.add)
            p.add("vector", conv)
            p.emit()
            p.add("gpsimd", lambda g: g.tensor_copy(xib[:, :, 0:3], xib[:, :, TCH:TCH + 3]))
            p.emit()

            def act(s):
                for gr in range(2):
                    s.activation(ub[:, gr, :], upre[:, gr, :], AF.Silu,
                                 bias=scb[:, gr:gr + 1], scale=1.0)
            p.add("scalar", act)
            p.emit()

            for tg in range(4):
                def mmx(t, tg=tg):
                    t.matmul(psX[:, :], swxT[:, 0, :], ub[:, 0, tg * 512:(tg + 1) * 512],
                             start=True, stop=False)
                    t.matmul(psX[:, :], swxT[:, 1, :], ub[:, 1, tg * 512:(tg + 1) * 512],
                             start=False, stop=True).then_inc(_smm, 1)
                p.add("tensor", mmx)
                s_mm.bump(1)
                mmv = s_mm.v

                def evx(v, tg=tg, mmv=mmv):
                    v.wait_ge(_smm, mmv)
                    v.tensor_copy(xdv[:, tg * 512:(tg + 1) * 512], psX[:, :])
                p.add("vector", evx)
                p.emit()

            def outs(s, ch=ch, w=s_o.v + 48):
                s.dma_start(out=o_u[:, ch * TCH:(ch + 1) * TCH].rearrange("(g p) t -> p g t", p=128),
                            in_=ub[:, :, :]).then_inc(_so, 16)
                s.dma_start(out=o_z[:, ch * TCH:(ch + 1) * TCH].rearrange("(g p) t -> p g t", p=128),
                            in_=zb[:, :, :]).then_inc(_so, 16)
                s.dma_start(out=o_xdblp[:, ch * TCH:(ch + 1) * TCH], in_=xdv[:, :]).then_inc(_so, 16)
                s.wait_ge(_so, w)
            p.add("sync", outs)
            s_o.bump(48)
            p.emit()
    return nc


def inputs_l0a_pre(ins, core):
    W_in = ins["L0_attn_W_in"]
    sd = slice(DSL * core, DSL * (core + 1))
    wT = np.concatenate([W_in[sd].T, W_in[DI + DSL * core:DI + DSL * (core + 1)].T], axis=1)
    cw = ins["L0_attn_conv_w"][sd]
    cwp = np.concatenate([cw[:128], cw[128:]], axis=1)
    cb = ins["L0_attn_conv_b"][sd]
    cbp = np.stack([cb[:128], cb[128:]], axis=1)
    return {
        "xT": f32c(ins["L0_attn_x"][0].T),
        "wT": f32c(wT),
        "cwp": f32c(cwp),
        "cbp": f32c(cbp),
        "wxT": f32c(np.concatenate([ins["L0_attn_W_x"][:, sd].T,
                                    ins["L0_attn_Dp"][sd][:, None]], axis=1)),
    }



def build_l0a_scan(stage=3, parts=15, fu=None):
    nc = fu["nc"] if fu is not None else bass.Bass(
        "TRN2", target_bir_lowering=False, detect_race_conditions=False)
    pfx = fu["pfx"] if fu is not None else ""
    xdblp = dram_in(nc, fu, "xdblp", [NCORE, 97, T], F32)
    u_in = dram_in(nc, fu, "u_in", [DSL, T], BF16)
    z_in = dram_in(nc, fu, "z_in", [DSL, T], BF16)
    wdtT = dram_in(nc, fu, "wdtT", [DTR, DSL], F32)
    bdtp = dram_in(nc, fu, "bdtp", [128, 2], F32)
    alogc = dram_in(nc, fu, "alogc", [128, 32], F32)
    alogs = dram_in(nc, fu, "alogs", [DS, DSL], F32)
    dpp = dram_in(nc, fu, "dpp", [128, 2], F32)
    selS = dram_in(nc, fu, "selS", [128, 16 * 128], F32)
    selR = dram_in(nc, fu, "selR", [128, 16 * 128], F32)

    o_ycat = dram_out(nc, fu, "o_ycat", [512, T], BF16)
    o_svec = dram_out(nc, fu, "o_svec", [1, T], F32)

    d_fm = nc.dram_tensor(pfx + "d_fm", [DSL, T], BF16).ap()
    du_fm = nc.dram_tensor(pfx + "du_fm", [DSL, T], BF16).ap()
    y_fm = nc.dram_tensor(pfx + "y_fm", [DSL, T], BF16).ap()

    PW = 1024
    NP = T // PW
    SP = 2048
    NS = T // SP

    with ExitStack() as _st:
        E = _st.enter_context
        sxdb = E(nc.sbuf_tensor(pfx + "sxdb", [97, T], BF16))
        brep = E(nc.sbuf_tensor(pfx + "brep", [128, T], BF16))
        crep = E(nc.sbuf_tensor(pfx + "crep", [128, T], BF16))
        swdtT = E(nc.sbuf_tensor(pfx + "swdtT", [DTR, DSL], BF16))
        sbdt = E(nc.sbuf_tensor(pfx + "sbdt", [128, 2], F32))
        sdp = E(nc.sbuf_tensor(pfx + "sdp", [128, 2], F32))
        sA = E(nc.sbuf_tensor(pfx + "sA", [128, 32], F32))
        sAs = E(nc.sbuf_tensor(pfx + "sAs", [DS, DSL], BF16))
        sselS = E(nc.sbuf_tensor(pfx + "sselS", [128, 16, 128], BF16))
        sselR = E(nc.sbuf_tensor(pfx + "sselR", [128, 16, 128], BF16))
        dtmp = E(nc.sbuf_tensor(pfx + "dtmp", [128, 2, SP], F32))
        dtmp2 = E(nc.sbuf_tensor(pfx + "dtmp2", [128, 2, SP], F32))
        dtb = E(nc.sbuf_tensor(pfx + "dtb", [128, 2, SP], BF16))
        ubch = E(nc.sbuf_tensor(pfx + "ubch", [128, 2, SP], BF16))
        ybch = E(nc.sbuf_tensor(pfx + "ybch", [128, 2, SP], BF16))
        dsl = E(nc.sbuf_tensor(pfx + "dsl", [128, 2, PW], BF16))
        dusl = E(nc.sbuf_tensor(pfx + "dusl", [128, 2, PW], BF16))
        dab = E(nc.sbuf_tensor(pfx + "dab", [128, PW + 1], F32))
        d1b = E(nc.sbuf_tensor(pfx + "d1b", [128, PW + 1], F32))
        sst = E(nc.sbuf_tensor(pfx + "sst", [128, PW + 1], F32))
        scb2 = E(nc.sbuf_tensor(pfx + "scb2", [128, PW], BF16))
        stcol = E(nc.sbuf_tensor(pfx + "stcol", [128, 32], F32))
        yev = E(nc.sbuf_tensor(pfx + "yev", [128, 2, PW], BF16))
        yacc = E(nc.sbuf_tensor(pfx + "yacc", [128, 2, PW], F32))
        onesr = E(nc.sbuf_tensor(pfx + "onesr", [1, 128], BF16))
        psD = E(nc.psum_tensor(pfx + "psD", [128, PW], F32))
        psU = E(nc.psum_tensor(pfx + "psU", [128, PW], F32))
        psY0 = E(nc.psum_tensor(pfx + "psY0", [128, PW], F32))
        psY1 = E(nc.psum_tensor(pfx + "psY1", [128, PW], F32))
        p, (_sx, _sg, _smm, _sev, _so), (s_x, s_g, s_mm, s_ev, s_o) = fu_start(nc, fu, E)

        with (nc.sbuf_tensor(pfx + "sxd", [97, T // 16], F32) as sxd,
              nc.sbuf_tensor(pfx + "xdst", [97, T // 16], F32) as xdst):
            def loads(s):
                s.dma_start(out=sbdt[:, :], in_=bdtp).then_inc(_sx, 16)
                s.dma_start(out=sdp[:, :], in_=dpp).then_inc(_sx, 16)
                s.dma_start(out=sA[:, :], in_=alogc).then_inc(_sx, 16)
            p.add("sync", loads)

            def loads2(g):
                g.dma_start(out=swdtT[:, :], in_=wdtT).then_inc(_sg, 16)
                g.dma_start(out=sAs[:, :], in_=alogs).then_inc(_sg, 16)
                g.dma_start(out=sselS[:, :, :],
                            in_=selS.rearrange("p (s j) -> p s j", s=16)).then_inc(_sg, 16)
                g.dma_start(out=sselR[:, :, :],
                            in_=selR.rearrange("p (s j) -> p s j", s=16)).then_inc(_sg, 16)
            p.add("gpsimd", loads2)
            s_x.bump(16 * 3)
            s_g.bump(16 * 4)
            p.emit()

            for hh in range(16):
                tsp = slice(hh * (T // 16), (hh + 1) * (T // 16))
                p.add("sync", lambda s, tsp=tsp: s.dma_start(
                    out=sxd[:, :], in_=xdblp[0][:, tsp]).then_inc(_sx, 16))
                s_x.bump(16)
                p.add("vector", lambda v, w=s_x.v: v.wait_ge(_sx, w))
                p.emit()
                if fu is not None:
                    p.add("vector", lambda v, tsp=tsp: v.tensor_copy(sxdb[:, tsp], sxd[:, :]))
                    p.emit()
                    continue
                for r in range(1, NCORE):
                    p.add("sync", lambda s, r=r, tsp=tsp: s.dma_start(
                        out=xdst[:, :], in_=xdblp[r][:, tsp]).then_inc(_sx, 16))
                    s_x.bump(16)
                    p.add("vector", lambda v, w=s_x.v: (
                        v.wait_ge(_sx, w),
                        v.tensor_add(sxd[:, :], sxd[:, :], xdst[:, :])))
                    p.emit()
                p.add("vector", lambda v, tsp=tsp: v.tensor_copy(sxdb[:, tsp], sxd[:, :]))
                p.emit()

        def prep_s(s, w=s_x.v, wg=s_g.v):
            s.wait_ge(_sx, w)
            s.wait_ge(_sg, wg)
            s.activation(sA[:, :], sA[:, :], AF.Exp, bias=0.0, scale=1.0)
            s.activation(sAs[:, :], sAs[:, :], AF.Exp, bias=0.0, scale=1.0)
        p.add("scalar", prep_s)
        p.emit()

        def prep_v(v):
            v.tensor_scalar(sAs[:, :], sAs[:, :], -1.0, None, ALU.mult)
            v.memset(onesr[:, :], 1.0)
            v.memset(dab[:, :], 1.0)
        p.add("vector", prep_v)
        p.add("gpsimd", lambda g: (
            g.tensor_scalar(sA[:, :], sA[:, :], -1.0, None, ALU.mult),
            g.memset(stcol[:, :], 0.0)))
        p.emit()

        def repl_bc(s, w=s_x.v + 16 * 16):
            for j in range(8):
                s.dma_start(out=brep[j * 16:(j + 1) * 16, :], in_=sxdb[64:80, :]).then_inc(_sx, 16)
                s.dma_start(out=crep[j * 16:(j + 1) * 16, :], in_=sxdb[80:96, :]).then_inc(_sx, 16)
            s.wait_ge(_sx, w)
        p.add("sync", repl_bc)
        s_x.bump(16 * 16)
        p.emit()

        for sp in range(NS):
            tsp = slice(sp * SP, (sp + 1) * SP)
            p.add("sync", lambda s, tsp=tsp: s.dma_start(
                out=ubch[:, :, :], in_=two(u_in)[:, :, tsp]).then_inc(_sx, 16))
            s_x.bump(16)
            uw = s_x.v
            p.emit()

            for gr in range(2):
                for tg in range(SP // 512):
                    def mmd(t, gr=gr, tg=tg, sp=sp):
                        t.matmul(psD[:, 0:512], swdtT[:, gr * 128:(gr + 1) * 128],
                                 sxdb[0:64, sp * SP + tg * 512:sp * SP + (tg + 1) * 512],
                                 start=True, stop=True).then_inc(_smm, 1)
                    p.add("tensor", mmd)
                    s_mm.bump(1)
                    mmv = s_mm.v

                    def evd(s, gr=gr, tg=tg, mmv=mmv):
                        s.wait_ge(_smm, mmv)
                        s.activation(dtmp[:, gr, tg * 512:(tg + 1) * 512], psD[:, 0:512],
                                     AF.Abs, bias=sbdt[:, gr:gr + 1], scale=1.0)
                        s.activation(dtmp2[:, gr, tg * 512:(tg + 1) * 512], psD[:, 0:512],
                                     AF.Relu, bias=sbdt[:, gr:gr + 1], scale=1.0)
                    p.add("scalar", evd)
                    p.emit()

            def sp1(s):
                for gr in range(2):
                    s.activation(dtmp[:, gr, :], dtmp[:, gr, :], AF.Exp, bias=0.0, scale=-1.0)
                    s.activation(dtmp[:, gr, :], dtmp[:, gr, :], AF.Ln, bias=1.0, scale=1.0)
            p.add("scalar", sp1)
            p.emit()

            def sp2(v, uw=uw):
                v.wait_ge(_sx, uw)
                for gr in range(2):
                    v.tensor_add(dtmp[:, gr, :], dtmp[:, gr, :], dtmp2[:, gr, :])
                    v.tensor_copy(dtb[:, gr, :], dtmp[:, gr, :])
                    v.tensor_mul(ubch[:, gr, :], dtb[:, gr, :], ubch[:, gr, :])
            p.add("vector", sp2)
            p.emit()

            def dout(s, tsp=tsp, w=s_o.v + 32):
                s.dma_start(out=two(d_fm)[:, :, tsp], in_=dtb[:, :, :]).then_inc(_so, 16)
                s.dma_start(out=two(du_fm)[:, :, tsp], in_=ubch[:, :, :]).then_inc(_so, 16)
                s.wait_ge(_so, w)
            p.add("sync", dout)
            s_o.bump(32)
            p.emit()

        import os as _os
        NPeff = int(_os.environ.get("A3NP", str(NP)))
        for pc in range(NPeff if stage >= 2 else 0):
            tsp = slice(pc * PW, (pc + 1) * PW)
            p.add("sync", lambda s, tsp=tsp: (
                s.dma_start(out=dsl[:, :, :], in_=two(d_fm)[:, :, tsp]).then_inc(_sx, 16),
                s.dma_start(out=dusl[:, :, :], in_=two(du_fm)[:, :, tsp]).then_inc(_sx, 16)))
            s_x.bump(32)
            pw_w = s_x.v
            p.add("vector", lambda v: v.memset(yacc[:, :, :], 0.0))
            p.emit()

            for cc in range(32):
                half = cc // 16
                sel = cc % 16

                def mms(t, half=half, sel=sel, pw_w=pw_w):
                    t.wait_ge(_sx, pw_w)
                    for tg in range(PW // 512):
                        t.matmul(psD[:, tg * 512:(tg + 1) * 512], sselS[:, sel, :],
                                 dsl[:, half, tg * 512:(tg + 1) * 512], start=True, stop=True)
                        mmi = t.matmul(psU[:, tg * 512:(tg + 1) * 512], sselS[:, sel, :],
                                 dusl[:, half, tg * 512:(tg + 1) * 512], start=True, stop=True)
                    mmi.then_inc(_smm, 1)
                p.add("tensor", mms)
                s_mm.bump(1)
                mmv = s_mm.v

                def mkda(s, cc=cc, mmv=mmv):
                    s.wait_ge(_smm, mmv)
                    s.activation(dab[:, 1:PW + 1], psD[:, :], AF.Exp, bias=0.0,
                                 scale=sA[:, cc:cc + 1])
                    s.activation(d1b[:, 0:1], stcol[:, cc:cc + 1], AF.Copy,
                                 bias=0.0, scale=1.0).then_inc(_sev, 1)
                p.add("scalar", mkda)
                s_ev.bump(1)
                dav = s_ev.v

                def mkd1(v, tsp=tsp, mmv=mmv, cc=cc, dav=dav):
                    v.wait_ge(_smm, mmv)
                    v.tensor_tensor(d1b[:, 1:PW + 1], psU[:, :], brep[:, tsp], ALU.mult)
                    v.wait_ge(_sev, dav)
                    v.tensor_tensor_scan(sst[:, :], dab[:, :], d1b[:, :],
                                         0.0, ALU.mult, ALU.add).then_inc(_sev, 1)
                p.add("vector", mkd1)
                s_ev.bump(1)
                scanv = s_ev.v
                p.add("scalar", lambda s, cc=cc, scanv=scanv: (
                    s.wait_ge(_sev, scanv),
                    s.activation(stcol[:, cc:cc + 1], sst[:, PW:PW + 1], AF.Copy,
                                 bias=0.0, scale=1.0)))

                def mksc(g, tsp=tsp, scanv=scanv):
                    g.wait_ge(_sev, scanv)
                    g.tensor_tensor(scb2[:, :], sst[:, 1:PW + 1], crep[:, tsp], ALU.mult).then_inc(_sev, 1)
                p.add("gpsimd", mksc)
                s_ev.bump(1)
                scv = s_ev.v

                psY = (psY0, psY1)[half]

                def mmy(t, sel=sel, psY=psY, scv=scv):
                    t.wait_ge(_sev, scv)
                    for tg in range(PW // 512):
                        mmi = t.matmul(psY[:, tg * 512:(tg + 1) * 512], sselR[:, sel, :],
                                 scb2[:, tg * 512:(tg + 1) * 512],
                                 start=True, stop=True)
                    mmi.then_inc(_smm, 1)
                p.add("tensor", mmy)
                s_mm.bump(1)
                p.add("vector", lambda v, half=half, psY=psY, w=s_mm.v: (
                    v.wait_ge(_smm, w),
                    v.tensor_add(yacc[:, half, :], yacc[:, half, :], psY[:, :])))
                p.emit()

            def evy(s):
                s.activation(yev[:, 0, :], yacc[:, 0, :], AF.Identity, bias=0.0, scale=1.0)
                s.activation(yev[:, 1, :], yacc[:, 1, :], AF.Identity, bias=0.0, scale=1.0)
            p.add("scalar", evy)
            p.emit()
            p.add("sync", lambda s, tsp=tsp, w=s_o.v + 16: (
                s.dma_start(out=two(y_fm)[:, :, tsp], in_=yev[:, :, :]).then_inc(_so, 16),
                s.wait_ge(_so, w)))
            s_o.bump(16)
            p.emit()

        for sp in range(NS if stage >= 3 else 0):
            tsp = slice(sp * SP, (sp + 1) * SP)
            p.add("sync", lambda s, tsp=tsp: (
                s.dma_start(out=ubch[:, :, :], in_=two(u_in)[:, :, tsp]).then_inc(_sx, 16),
                s.dma_start(out=dtb[:, :, :], in_=two(z_in)[:, :, tsp]).then_inc(_sx, 16),
                s.dma_start(out=ybch[:, :, :], in_=two(y_fm)[:, :, tsp]).then_inc(_sx, 16)))
            s_x.bump(48)
            fw = s_x.v
            p.emit()

            def fin1(s, fw=fw):
                s.wait_ge(_sx, fw)
                for gr in range(2):
                    s.activation(dtmp2[:, gr, :], dtb[:, gr, :], AF.Silu, bias=0.0, scale=1.0)
            p.add("scalar", fin1)

            def fin2(v, fw=fw):
                v.wait_ge(_sx, fw)
                for gr in range(2):
                    v.scalar_tensor_tensor(dtmp[:, gr, :], ubch[:, gr, :], sdp[:, gr:gr + 1],
                                           ybch[:, gr, :], ALU.mult, ALU.add)
            p.add("vector", fin2)
            p.emit()

            def fin3(v):
                for gr in range(2):
                    v.tensor_mul(dtmp[:, gr, :], dtmp[:, gr, :], dtmp2[:, gr, :])
                    v.tensor_copy(ubch[:, gr, :], dtmp[:, gr, :])
            p.add("vector", fin3)
            p.emit()

            for gr in range(2):
                for tg in range(SP // 512):
                    def mmyt(t, gr=gr, tg=tg, sp=sp):
                        tt = slice(sp * SP + tg * 512, sp * SP + (tg + 1) * 512)
                        t.matmul(psY0[:, 0:512], sAs[:, gr * 128:(gr + 1) * 128],
                                 crep[0:16, tt], start=True, stop=True).then_inc(_smm, 1)
                    p.add("tensor", mmyt)
                    s_mm.bump(1)
                    mmv = s_mm.v

                    def evyt(s, gr=gr, tg=tg, mmv=mmv):
                        s.wait_ge(_smm, mmv)
                        s.activation(ybch[:, gr, tg * 512:(tg + 1) * 512], psY0[:, 0:512],
                                     AF.Identity, bias=0.0, scale=1.0)
                    p.add("scalar", evyt)
                    p.emit()

            def fout(s, tsp=tsp, w=s_o.v + 32):
                s.dma_start(out=two(o_ycat[0:256, :])[:, :, tsp], in_=ubch[:, :, :]).then_inc(_so, 16)
                s.dma_start(out=two(o_ycat[256:512, :])[:, :, tsp], in_=ybch[:, :, :]).then_inc(_so, 16)
                s.wait_ge(_so, w)
            p.add("sync", fout)
            s_o.bump(32)
            p.emit()

        p.add("gpsimd", lambda g, w=s_g.v + 16: (
            g.dma_start(out=o_svec, in_=sxdb[96:97, :]).then_inc(_sg, 16),
            g.wait_ge(_sg, w)))
        s_g.bump(16)
        p.emit()
    return nc

def inputs_l0a_scan(ins, core, xdblp_all, u_all, z_all):
    sd = slice(DSL * core, DSL * (core + 1))
    A_log = ins["L0_attn_A_log"][sd]
    alogc = np.ascontiguousarray(A_log.reshape(-1).reshape(32, 128).T)
    selS = np.zeros((128, 16, 128), np.float32)
    selR = np.zeros((128, 16, 128), np.float32)
    for s in range(16):
        for r in range(128):
            selS[8 * s + r // 16, s, r] = 1.0
        for pp in range(128):
            selR[pp, s, 8 * s + pp // 16] = 1.0
    bdt = ins["L0_attn_b_dt"][sd]
    dp = ins["L0_attn_Dp"][sd]
    return {
        "xdblp": f32c(xdblp_all),
        "u_in": np.ascontiguousarray(u_all[sd]).astype(BFNP, copy=False)
        if u_all.dtype != BFNP else np.ascontiguousarray(u_all[sd]),
        "z_in": np.ascontiguousarray(z_all[sd]).astype(BFNP, copy=False)
        if z_all.dtype != BFNP else np.ascontiguousarray(z_all[sd]),
        "wdtT": f32c(ins["L0_attn_W_dt"][sd].T),
        "bdtp": f32c(np.stack([bdt[:128], bdt[128:]], axis=1)),
        "alogc": f32c(alogc),
        "alogs": f32c(A_log.T),
        "dpp": f32c(np.stack([dp[:128], dp[128:]], axis=1)),
        "selS": f32c(selS.reshape(128, 16 * 128)),
        "selR": f32c(selR.reshape(128, 16 * 128)),
    }



def build_l0a_out(fu=None):
    nc = fu["nc"] if fu is not None else bass.Bass(
        "TRN2", target_bir_lowering=False, detect_race_conditions=False)
    pfx = fu["pfx"] if fu is not None else ""
    ycat = dram_in(nc, fu, "ycat", [4096, TLOC], BF16)
    svr = dram_in(nc, fu, "svr", [1, TLOC], F32)
    w1T = dram_in(nc, fu, "w1T", [4096, DI], F32)
    w2T = dram_in(nc, fu, "w2T", [DI, C], F32)
    xs = dram_in(nc, fu, "xs", [TLOC, C], F32)
    o_s = dram_out(nc, fu, "o_s", [TLOC, C], F32)

    TH = 1024

    with ExitStack() as _st:
        E = _st.enter_context
        syc = E(nc.sbuf_tensor(pfx + "syc", [128, 32, TH], BF16))
        sw1a = E(nc.sbuf_tensor(pfx + "sw1a", [128, 32, 128], BF16))
        sw1b = E(nc.sbuf_tensor(pfx + "sw1b", [128, 32, 128], BF16))
        shf = E(nc.sbuf_tensor(pfx + "shf", [128, 16, TH], BF16))
        ssv = E(nc.sbuf_tensor(pfx + "ssv", [1, TLOC], F32))
        ssvb = E(nc.sbuf_tensor(pfx + "ssvb", [1, TLOC], BF16))
        sob = E(nc.sbuf_tensor(pfx + "sob", [1, 128], BF16))
        sw2 = E(nc.sbuf_tensor(pfx + "sw2", [128, 16, C], BF16))
        sxt = E(nc.sbuf_tensor(pfx + "sxt", [128, C], F32))
        sot = E(nc.sbuf_tensor(pfx + "sot", [128, C], F32))
        stt2 = E(nc.sbuf_tensor(pfx + "stt", [128, C], F32))
        mcol = E(nc.sbuf_tensor(pfx + "mcol", [128, 1], F32))
        vcol = E(nc.sbuf_tensor(pfx + "vcol", [128, 1], F32))
        psA = E(nc.psum_tensor(pfx + "psA", [128, 512], F32))
        psB = E(nc.psum_tensor(pfx + "psB", [128, 512], F32))
        p, (_sx, _sg, _smm, _sev, _so), (s_x, s_g, s_mm, s_ev, s_o) = fu_start(nc, fu, E)
        ps = (psA, psB)
        p.add("gpsimd", lambda g: g.dma_start(
            out=sw2[:, :, :], in_=w2T.rearrange("(k p) c -> p k c", p=128)).then_inc(_sg, 16))
        s_g.bump(16)
        w2w = s_g.v
        p.add("sync", lambda s: s.dma_start(out=ssv[:, :], in_=svr).then_inc(_sx, 16))
        s_x.bump(16)
        p.add("vector", lambda v, w=s_x.v: (v.wait_ge(_sx, w),
                                            v.tensor_copy(ssvb[:, :], ssv[:, :]),
                                            v.memset(sob[:, :], 1.0)))
        p.emit()

        for th in range(2):
            tsp = slice(th * TH, (th + 1) * TH)
            p.add("sync", lambda s, tsp=tsp: s.dma_start(
                out=syc[:, :, :], in_=ycat.rearrange("(k p) t -> p k t", p=128)[:, :, tsp])
                .then_inc(_sx, 16))
            s_x.bump(16)
            ycw = s_x.v
            p.emit()

            for tg2 in range(TH // 512):
                def mmsv(t, th=th, tg2=tg2, ycw=ycw):
                    t.wait_ge(_sx, ycw)
                    t.matmul(psA[:, :], sob[0:1, :],
                             ssvb[0:1, th * TH + tg2 * 512:th * TH + (tg2 + 1) * 512],
                             start=True, stop=True).then_inc(_smm, 1)
                p.add("tensor", mmsv)
                s_mm.bump(1)
                mmv = s_mm.v

                ytks = ([4 * rr + 2 for rr in range(8)] + [4 * rr + 3 for rr in range(8)]
                        if fu is not None else list(range(16, 32)))

                def addsv(v, tg2=tg2, mmv=mmv, ytks=ytks):
                    v.wait_ge(_smm, mmv)
                    for k in ytks:
                        v.tensor_add(syc[:, k, tg2 * 512:(tg2 + 1) * 512],
                                     syc[:, k, tg2 * 512:(tg2 + 1) * 512], psA[:, :])
                p.add("vector", addsv)
                p.emit()

            for jc in range(16):
                swl = (sw1a, sw1b)[jc % 2]
                p.add("gpsimd", lambda g, jc=jc, swl=swl: g.dma_start(
                    out=swl[:, :, :],
                    in_=w1T[:, jc * 128:(jc + 1) * 128].rearrange("(k p) j -> p k j", p=128))
                    .then_inc(_sg, 16))
                s_g.bump(16)
                xw = s_x.v
                gw = s_g.v

                for tg in range(2):
                    gi = jc * 2 + tg
                    pst = ps[gi % 2]

                    def mmh(t, jc=jc, tg=tg, gi=gi, pst=pst, swl=swl, xw=xw, gw=gw):
                        t.wait_ge(_sx, xw)
                        t.wait_ge(_sg, gw)
                        for k in range(32):
                            mmi = t.matmul(pst[:, :], swl[:, k, :],
                                     syc[:, k, tg * 512:(tg + 1) * 512],
                                     start=(k == 0), stop=(k == 31))
                        mmi.then_inc(_smm, 1)
                    p.add("tensor", mmh)
                    s_mm.bump(1)
                    mmv = s_mm.v

                    def evh(s, jc=jc, tg=tg, pst=pst, mmv=mmv):
                        s.wait_ge(_smm, mmv)
                        s.activation(shf[:, jc, tg * 512:(tg + 1) * 512], pst[:, :],
                                     AF.Relu, bias=0.0, scale=1.0).then_inc(_sev, 1)
                    p.add("scalar", evh)
                    s_ev.bump(1)
                p.emit()

            for tt in range(TH // 128):
                p.add("sync", lambda s, th=th, tt=tt: s.dma_start(
                    out=sxt[:, :], in_=xs[th * TH + tt * 128:th * TH + (tt + 1) * 128, :])
                    .then_inc(_sx, 16))
                s_x.bump(16)
                xw = s_x.v
                for cg in range(2):
                    pst = ps[cg % 2]

                    def mmo(t, tt=tt, cg=cg, pst=pst, w2w=w2w):
                        t.wait_ge(_sg, w2w)
                        for k in range(16):
                            mmi = t.matmul(pst[:, :], shf[:, k, tt * 128:(tt + 1) * 128],
                                     sw2[:, k, cg * 512:(cg + 1) * 512],
                                     start=(k == 0), stop=(k == 15))
                        mmi.then_inc(_smm, 1)
                    p.add("tensor", mmo)
                    s_mm.bump(1)
                    mmv = s_mm.v

                    def evo(v, cg=cg, pst=pst, mmv=mmv, xw=xw):
                        v.wait_ge(_smm, mmv)
                        v.wait_ge(_sx, xw)
                        v.tensor_add(sot[:, cg * 512:(cg + 1) * 512], pst[:, :],
                                     sxt[:, cg * 512:(cg + 1) * 512])
                    p.add("vector", evo)
                p.emit()

                emit_ln_tm(p, nc, sot, stt2, mcol, vcol)
                p.add("sync", lambda s, th=th, tt=tt, w=s_o.v + 16: (
                    s.dma_start(out=o_s[th * TH + tt * 128:th * TH + (tt + 1) * 128, :],
                                in_=sot[:, :]).then_inc(_so, 16),
                    s.wait_ge(_so, w)))
                s_o.bump(16)
                p.emit()
    return nc


def inputs_l0a_out(ins, core, ycat_full, svec_row):
    ts = slice(TLOC * core, TLOC * (core + 1))
    return {
        "ycat": np.ascontiguousarray(ycat_full[:, ts]),
        "svr": f32c(svec_row[:, ts]),
        "w1T": f32c(ins["L0_attn_W_out1"].T),
        "w2T": f32c(ins["L0_attn_W_out2"].T),
        "xs": f32c(ins["L0_attn_x"][0][ts]),
    }



_PROG_CACHE = {}


def prog(name, builder, *args):
    key = (name,) + args
    if key not in _PROG_CACHE:
        _PROG_CACHE[key] = builder(*args)
    return _PROG_CACHE[key]


def runp(name, builder, maps, bargs=()):
    nc = prog(name, builder, *bargs)
    return run_bass_kernel_spmd(nc, maps, core_ids=list(range(NCORE))).results


def kernel_unfused(**inputs):
    ins = inputs
    r = runp("a1", build_l0a_pre, [inputs_l0a_pre(ins, c) for c in range(NCORE)])
    u_all = np.concatenate([np.asarray(x["o_u"]) for x in r], axis=0)
    z_all = np.concatenate([np.asarray(x["o_z"]) for x in r], axis=0)
    xdblp_all = np.stack([np.asarray(x["o_xdblp"]) for x in r], axis=0)

    r = runp("a3", build_l0a_scan,
             [inputs_l0a_scan(ins, c, xdblp_all, u_all, z_all) for c in range(NCORE)])
    ycat_full = np.concatenate(
        [np.concatenate([np.asarray(x["o_ycat"][0:256]) for x in r], axis=0),
         np.concatenate([np.asarray(x["o_ycat"][256:512]) for x in r], axis=0)], axis=0)
    svec_row = np.asarray(r[0]["o_svec"], np.float32)

    r = runp("a4", build_l0a_out,
             [inputs_l0a_out(ins, c, ycat_full, svec_row) for c in range(NCORE)])
    stream0 = np.concatenate([np.asarray(x["o_s"]) for x in r], axis=0)

    b1res = runp("b1", build_l0f_qkv, [inputs_l0f_qkv(ins, c, stream0) for c in range(NCORE)])
    kf_all = np.concatenate([np.asarray(x["o_kf"]) for x in b1res], axis=1)
    vt_all = np.concatenate([np.asarray(x["o_vt"]) for x in b1res], axis=0)
    r = runp("b2", build_l0f_attn,
             [inputs_l0f_attn(ins, c, kf_all, vt_all, b1res, stream0) for c in range(NCORE)])
    stream1 = np.concatenate([np.asarray(x["o_s"]) for x in r], axis=0)

    r = runp("c1", build_l1a_ret, [inputs_l1a_ret(ins, h, stream1) for h in range(NCORE)])
    yn_full = np.concatenate([np.asarray(x["o_yn"]) for x in r], axis=1)
    r = runp("c2", build_l1a_out,
             [inputs_l1a_out(ins, c, stream1, yn_full) for c in range(NCORE)])
    stream2 = np.concatenate([np.asarray(x["o_s"]) for x in r], axis=0)

    xcur = stream2
    for layer in range(6):
        r = runp("da", build_l1f_attn,
                 [inputs_l1f_attn(ins, c, xcur, layer) for c in range(NCORE)])
        xcur = np.concatenate([np.asarray(x["o_x"]) for x in r], axis=0)
        last = layer == 5
        r = runp("df", build_l1f_ffn,
                 [inputs_l1f_ffn(ins, c, xcur, layer, s2=stream2 if last else None)
                  for c in range(NCORE)], bargs=(last,))
        xcur = np.concatenate([np.asarray(x["o_x"]) for x in r], axis=0)

    return np.ascontiguousarray(xcur, dtype=np.float32).reshape(1, T, C)



def emit_ln_tm(p, nc, sot, stt2, mcol, vcol, width=C):
    p.add("vector", lambda v: v.reduce_sum(mcol[:, :], sot[:, :], axis=AX.X))
    p.emit()

    def s1(v):
        v.tensor_scalar(stt2[:, :], mcol[:, 0:1].broadcast_to([128, width]),
                        1.0 / width, None, ALU.mult)
        v.tensor_sub(stt2[:, :], sot[:, :], stt2[:, :])
    p.add("vector", s1)
    p.emit()
    p.add("scalar", lambda s: s.activation(sot[:, :], stt2[:, :], AF.Square,
                                           bias=0.0, scale=1.0))
    p.emit()
    p.add("vector", lambda v: v.reduce_sum(vcol[:, :], sot[:, :], axis=AX.X))
    p.emit()
    p.add("vector", lambda v: v.tensor_scalar(
        sot[:, :], vcol[:, 0:1].broadcast_to([128, width]), 1.0 / width, 1e-5,
        ALU.mult, ALU.add))
    p.emit()

    def s2(s):
        s.activation(sot[:, :], sot[:, :], AF.Ln, bias=0.0, scale=1.0)
        s.activation(sot[:, :], sot[:, :], AF.Exp, bias=0.0, scale=-0.5)
    p.add("scalar", s2)
    p.emit()
    p.add("vector", lambda v: v.tensor_mul(sot[:, :], stt2[:, :], sot[:, :]))
    p.emit()



def build_l0f_qkv(fu=None):
    nc = fu["nc"] if fu is not None else bass.Bass(
        "TRN2", target_bir_lowering=False, detect_race_conditions=False)
    pfx = fu["pfx"] if fu is not None else ""
    ss = dram_in(nc, fu, "ss", [TLOC, C], F32)
    simT = dram_in(nc, fu, "simT", [16, C], F32)
    gatesr = dram_in(nc, fu, "gatesr", [1, 16], F32)
    qkvp = dram_in(nc, fu, "qkvp", [C, 6144], F32)
    ident = dram_in(nc, fu, "ident", [128, 128], F32)

    o_qf = dram_out(nc, fu, "o_qf", [128, TLOC], BF16)
    o_kf = dram_out(nc, fu, "o_kf", [128, TLOC], BF16)
    o_vt = dram_out(nc, fu, "o_vt", [TLOC, 128], BF16)
    o_w = dram_out(nc, fu, "o_w", [TLOC, 16], F32)
    o_qn = dram_out(nc, fu, "o_qn", [16, 128], F32)
    o_km = dram_out(nc, fu, "o_km", [1, 1], F32)

    with ExitStack() as _st:
        E = _st.enter_context
        stm = E(nc.sbuf_tensor(pfx + "stm", [128, 16, C], F32))
        xfm = E(nc.sbuf_tensor(pfx + "xfm", [128, 8, TLOC], BF16))
        sbt = E(nc.sbuf_tensor(pfx + "sbt", [128, C], BF16))
        sid = E(nc.sbuf_tensor(pfx + "sid", [128, 128], BF16))
        ssim = E(nc.sbuf_tensor(pfx + "ssim", [16, C], F32))
        ssimb = E(nc.sbuf_tensor(pfx + "ssimb", [16, C], BF16))
        snf = E(nc.sbuf_tensor(pfx + "snf", [128, 8, 16], BF16))
        sgat = E(nc.sbuf_tensor(pfx + "sgat", [1, 16], F32))
        sgatb = E(nc.sbuf_tensor(pfx + "sgatb", [1, 16], BF16))
        sgrep = E(nc.sbuf_tensor(pfx + "sgrep", [128, 16], F32))
        rnx = E(nc.sbuf_tensor(pfx + "rnx", [128, 16], F32))
        wql = E(nc.sbuf_tensor(pfx + "wql", [128, 8, 512], BF16))
        wq2 = E(nc.sbuf_tensor(pfx + "wq2", [128, 8, 512], BF16))
        qtm = E(nc.sbuf_tensor(pfx + "qtm", [128, 16, 128], F32))
        ktm = E(nc.sbuf_tensor(pfx + "ktm", [128, 16, 128], F32))
        vtm = E(nc.sbuf_tensor(pfx + "vtm", [128, 16, 128], F32))
        swt = E(nc.sbuf_tensor(pfx + "swt", [128, 16, 16], F32))
        tA = E(nc.sbuf_tensor(pfx + "tA", [128, 64], F32))
        tB = E(nc.sbuf_tensor(pfx + "tB", [128, 64], F32))
        tC = E(nc.sbuf_tensor(pfx + "tC", [128, 64], F32))
        tD = E(nc.sbuf_tensor(pfx + "tD", [128, 64], F32))
        tE = E(nc.sbuf_tensor(pfx + "tE", [128, 64], F32))
        sgrep4 = E(nc.sbuf_tensor(pfx + "sgrep4", [128, 64], F32))
        c1 = E(nc.sbuf_tensor(pfx + "c1", [128, 1], F32))
        c2 = E(nc.sbuf_tensor(pfx + "c2", [128, 1], F32))
        qnc = E(nc.sbuf_tensor(pfx + "qnc", [128, 16], F32))
        knc = E(nc.sbuf_tensor(pfx + "knc", [128, 16], F32))
        ones1 = E(nc.sbuf_tensor(pfx + "ones1", [1, 128], BF16))
        bfq = E(nc.sbuf_tensor(pfx + "bfq", [128, 128], BF16))
        psA = E(nc.psum_tensor(pfx + "psA", [128, 512], F32))
        psB = E(nc.psum_tensor(pfx + "psB", [128, 512], F32))
        psT = E(nc.psum_tensor(pfx + "psT", [128, 128], BF16))
        psS = E(nc.psum_tensor(pfx + "psS", [128, 16], F32))
        psR = E(nc.psum_tensor(pfx + "psR", [16, 128], F32))
        p, (_sx, _sg, _smm, _sev, _so), (s_x, s_g, s_mm, s_ev, s_o) = fu_start(nc, fu, E)

        def loads(s):
            s.dma_start(out=stm[:, :, :], in_=ss.rearrange("(tt p) c -> p tt c", p=128)).then_inc(_sx, 16)
            s.dma_start(out=ssim[:, :], in_=simT).then_inc(_sx, 16)
            s.dma_start(out=sgat[:, :], in_=gatesr).then_inc(_sx, 16)
        p.add("sync", loads)
        p.add("gpsimd", lambda g: g.dma_start(out=sid[:, :], in_=ident).then_inc(_sg, 16))
        s_x.bump(48); s_g.bump(16)
        p.add("vector", lambda v, w=s_x.v, wg=s_g.v: (
            v.wait_ge(_sx, w), v.wait_ge(_sg, wg), v.memset(ones1[:, :], 1.0),
            v.memset(qtm[:, :, :], 0.0), v.memset(ktm[:, :, :], 0.0),
            v.memset(vtm[:, :, :], 0.0)))
        p.add("gpsimd", lambda g: g.memset(c2[:, :], 0.0))
        p.emit()

        def sn1(s):
            s.activation(ssimb[:, :], ssim[:, :], AF.Square, bias=0.0, scale=1.0)
        p.add("scalar", sn1)
        p.emit()

        def sn2(v):
            v.reduce_sum(c1[0:16, 0:1], ssimb[:, :], axis=AX.X)
        p.add("vector", sn2)
        p.emit()
        p.add("scalar", lambda s: s.activation(c1[0:16, 0:1], c1[0:16, 0:1], AF.Sqrt,
                                               bias=0.0, scale=1.0))
        p.emit()

        p.add("gpsimd", lambda g: g.tensor_scalar_max(c1[0:16, 0:1], c1[0:16, 0:1], 1e-12))
        p.emit()
        p.add("scalar", lambda s: (
            s.activation(c1[0:16, 0:1], c1[0:16, 0:1], AF.Ln, bias=0.0, scale=1.0),
            s.activation(c1[0:16, 0:1], c1[0:16, 0:1], AF.Exp, bias=0.0, scale=-1.0)))
        p.emit()
        p.add("vector", lambda v: v.tensor_tensor(
            ssimb[:, :], ssim[:, :], c1[0:16, 0:1].broadcast_to([16, C]), ALU.mult))
        p.emit()

        for kc in range(8):
            p.add("tensor", lambda t, kc=kc: (
                t.transpose(psT[:, 0:16], ssimb[0:16, kc * 128:(kc + 1) * 128],
                            sid[0:16, 0:16]).then_inc(_smm, 1)))
            s_mm.bump(1)
            p.add("vector", lambda v, kc=kc, w=s_mm.v: (
                v.wait_ge(_smm, w),
                v.tensor_copy(snf[:, kc, :], psT[:, 0:16])))
            p.emit()

        p.add("scalar", lambda s: s.activation(sgat[:, :], sgat[:, :], AF.Sigmoid,
                                               bias=0.0, scale=1.0))
        p.emit()
        p.add("gpsimd", lambda g: g.tensor_copy(sgatb[:, :], sgat[:, :]))
        p.emit()
        p.add("tensor", lambda t: t.matmul(psS[:, :], ones1[0:1, :], sgatb[:, :],
                                           start=True, stop=True).then_inc(_smm, 1))
        s_mm.bump(1)
        p.add("scalar", lambda s, w=s_mm.v: (
            s.wait_ge(_smm, w),
            s.activation(sgrep[:, :], psS[:, :], AF.Copy, bias=0.0, scale=1.0),
            s.activation(sgrep4[:, :].rearrange("p (r j) -> p r j", r=4),
                         psS[:, :].unsqueeze(1).broadcast_to([128, 4, 16]),
                         AF.Copy, bias=0.0, scale=1.0)))
        p.emit()

        sqs = E(nc.sbuf_tensor(pfx + "sqs", [128, C], F32))
        for tt in range(16):
            p.add("scalar", lambda s, tt=tt: s.activation(
                sqs[:, :], stm[:, tt, :], AF.Square, bias=0.0, scale=1.0))
            p.emit()
            p.add("vector", lambda v, tt=tt: v.reduce_sum(c1[:, :], sqs[:, :], axis=AX.X))
            p.emit()
            p.add("scalar", lambda s, tt=tt: s.activation(c1[:, :], c1[:, :], AF.Sqrt,
                                                          bias=0.0, scale=1.0))
            p.emit()
            p.add("gpsimd", lambda g: g.tensor_scalar_max(c1[:, :], c1[:, :], 1e-12))
            p.add("vector", lambda v, tt=tt: v.tensor_copy(sbt[:, :], stm[:, tt, :]))
            p.emit()
            p.add("scalar", lambda s, tt=tt: (
                s.activation(c1[:, :], c1[:, :], AF.Ln, bias=0.0, scale=1.0),
                s.activation(rnx[:, tt:tt + 1], c1[:, :], AF.Exp, bias=0.0, scale=-1.0)))
            p.emit()
            for kc in range(8):
                p.add("tensor", lambda t, tt=tt, kc=kc: (
                    t.transpose(psT[:, :], sbt[:, kc * 128:(kc + 1) * 128], sid[:, :])
                    .then_inc(_smm, 1)))
                s_mm.bump(1)
                p.add("vector", lambda v, tt=tt, kc=kc, w=s_mm.v: (
                    v.wait_ge(_smm, w),
                    v.tensor_copy(xfm[:, kc, tt * 128:(tt + 1) * 128], psT[:, :])))
                p.emit()

            def lgm(t, tt=tt):
                for kc in range(8):
                    mmi = t.matmul(psS[:, :], xfm[:, kc, tt * 128:(tt + 1) * 128],
                             snf[:, kc, :], start=(kc == 0), stop=(kc == 7))
                mmi.then_inc(_smm, 1)
            p.add("tensor", lgm)
            s_mm.bump(1)

            def rt0(s, tt=tt, w=s_mm.v):
                s.wait_ge(_smm, w)
                s.activation(tA[:, :].rearrange("p (r j) -> p r j", r=4),
                             psS[:, :].unsqueeze(1).broadcast_to([128, 4, 16]),
                             AF.Copy, bias=0.0, scale=1.0)
            p.add("scalar", rt0)
            p.emit()

            def rt1(v, tt=tt):
                v.tensor_tensor(tA[:, :], tA[:, :], rnx[:, tt:tt + 1].broadcast_to([128, 64]),
                                ALU.mult)
                v.tensor_sub(tA[:, :], tA[:, :], sgrep4[:, :])
                v.tensor_relu(tB[:, :], tA[:, :])
                v.tensor_single_scalar(tC[:, :], tA[:, :], 0.0, ALU.is_gt)
                v.reduce_sum(c1[:, :], tC[:, 0:16], axis=AX.X)
                v.reduce_max(c2[:, :], tA[:, 0:16], axis=AX.X)
            p.add("vector", rt1)
            p.emit()

            def rt2(v):
                v.tensor_scalar(tE[:, :], c1[:, 0:1].broadcast_to([128, 64]),
                                -1.0, 1.0, ALU.mult, ALU.add)
                v.tensor_relu(tE[:, :], tE[:, :])
                v.tensor_tensor(tD[:, :], tA[:, :], c2[:, 0:1].broadcast_to([128, 64]),
                                ALU.is_ge)
                v.tensor_scalar(tD[:, :], tD[:, :], -1e30, None, ALU.mult)
                v.tensor_add(tD[:, :], tD[:, :], tA[:, :])
                v.reduce_max(c2[:, :], tD[:, 0:16], axis=AX.X)
            p.add("vector", rt2)
            p.emit()

            def rt3(v):
                v.tensor_tensor(tD[:, :], tA[:, :], c2[:, 0:1].broadcast_to([128, 64]),
                                ALU.is_ge)
                v.tensor_mul(tD[:, :], tD[:, :], tE[:, :])
                v.tensor_mul(tE[:, :], tC[:, :], tD[:, :])
                v.tensor_sub(tC[:, :], tC[:, :], tE[:, :])
                v.tensor_add(tC[:, :], tC[:, :], tD[:, :])
                v.tensor_mul(tA[:, :], tB[:, :], tC[:, :])
                v.tensor_scalar(tD[:, :], tC[:, :], 1e30, -1e30, ALU.mult, ALU.add)
                v.tensor_add(tA[:, :], tA[:, :], tD[:, :])
                v.reduce_max(c2[:, :], tA[:, 0:16], axis=AX.X, negate=True)
            p.add("vector", rt3)
            p.emit()
            p.add("scalar", lambda s, tt=tt: s.activation(
                tA[:, :], tA[:, :], AF.Exp, bias=c2[:, :], scale=1.0))
            p.emit()
            p.add("vector", lambda v: v.reduce_sum(c2[:, :], tA[:, 0:16], axis=AX.X))
            p.emit()
            p.add("scalar", lambda s: (
                s.activation(c2[:, :], c2[:, :], AF.Ln, bias=0.0, scale=1.0),
                s.activation(c2[:, :], c2[:, :], AF.Exp, bias=0.0, scale=-1.0)))
            p.emit()
            p.add("vector", lambda v: v.tensor_tensor(
                tB[:, :], tA[:, :], c2[:, 0:1].broadcast_to([128, 64]), ALU.mult))
            p.emit()
            p.add("gpsimd", lambda g, tt=tt: g.tensor_copy(swt[:, tt, :], tB[:, 0:16]))
            p.emit()

        wbufs = (wql, wq2)
        nw = 0
        for proj in range(3):
            dst = (qtm, ktm, vtm)[proj]
            for eg in range(4):
                wb = wbufs[nw % 2]
                p.add("gpsimd", lambda g, proj=proj, eg=eg, wb=wb: g.dma_start(
                    out=wb[:, :, :],
                    in_=qkvp[:, proj * 2048 + eg * 512:proj * 2048 + (eg + 1) * 512]
                    .rearrange("(k p) j -> p k j", p=128)).then_inc(_sg, 16))
                s_g.bump(16)
                gw = s_g.v
                nw += 1
                p.emit()

                for tt in range(16):
                    pst = (psA, psB)[tt % 2]

                    def mmq(t, tt=tt, wb=wb, pst=pst, gw=gw):
                        t.wait_ge(_sg, gw)
                        for kc in range(8):
                            mmi = t.matmul(pst[:, :], xfm[:, kc, tt * 128:(tt + 1) * 128],
                                     wb[:, kc, :], start=(kc == 0), stop=(kc == 7))
                        mmi.then_inc(_smm, 1)
                    p.add("tensor", mmq)
                    s_mm.bump(1)

                    def accq(v, tt=tt, eg=eg, dst=dst, pst=pst, w=s_mm.v):
                        v.wait_ge(_smm, w)
                        for e in range(4):
                            v.scalar_tensor_tensor(
                                dst[:, tt, :], pst[:, e * 128:(e + 1) * 128],
                                swt[:, tt, eg * 4 + e:eg * 4 + e + 1],
                                dst[:, tt, :], ALU.mult, ALU.add)
                    p.add("vector", accq)
                    if tt % 2 == 1:
                        p.emit()
                p.emit()

        for tt in range(16):
            p.add("scalar", lambda s, tt=tt: s.activation(
                sqs[:, 0:128], qtm[:, tt, :], AF.Square, bias=0.0, scale=1.0))
            p.add("scalar", lambda s, tt=tt: s.activation(
                sqs[:, 128:256], ktm[:, tt, :], AF.Square, bias=0.0, scale=1.0))
            p.emit()

            def nq(v, tt=tt):
                v.reduce_sum(qnc[:, tt:tt + 1], sqs[:, 0:128], axis=AX.X)
                v.reduce_sum(knc[:, tt:tt + 1], sqs[:, 128:256], axis=AX.X)
            p.add("vector", nq)
            p.emit()
        p.add("scalar", lambda s: (
            s.activation(qnc[:, :], qnc[:, :], AF.Sqrt, bias=0.0, scale=1.0),
            s.activation(knc[:, :], knc[:, :], AF.Sqrt, bias=0.0, scale=1.0)))
        p.emit()

        def km1(v):
            v.reduce_max(c1[:, :], knc[:, :], axis=AX.X)
        p.add("vector", km1)
        p.emit()
        p.add("gpsimd", lambda g: g.tensor_copy(bfq[:, 0:1], c1[:, :]))
        p.emit()
        p.add("tensor", lambda t: t.transpose(psT[:, :], bfq[:, :], sid[:, :])
              .then_inc(_smm, 1))
        s_mm.bump(1)
        p.add("vector", lambda v, w=s_mm.v: (
            v.wait_ge(_smm, w),
            v.tensor_copy(sqs[0:1, 0:128], psT[0:1, 0:128]),
            v.reduce_max(c2[0:1, 0:1], sqs[0:1, 0:128], axis=AX.X)))
        p.emit()

        for tt in range(16):
            for which, src, dstf in ((0, qtm, o_qf), (1, ktm, o_kf)):
                p.add("vector", lambda v, tt=tt, src=src: v.tensor_copy(bfq[:, :], src[:, tt, :]))
                p.emit()
                p.add("tensor", lambda t, tt=tt: t.transpose(psT[:, :], bfq[:, :], sid[:, :])
                      .then_inc(_smm, 1))
                s_mm.bump(1)
                p.add("vector", lambda v, w=s_mm.v: (
                    v.wait_ge(_smm, w), v.tensor_copy(bfq[:, :], psT[:, :])))
                p.emit()
                p.add("sync", lambda s, tt=tt, dstf=dstf, w=s_o.v + 16: (
                    s.dma_start(out=dstf[:, tt * 128:(tt + 1) * 128], in_=bfq[:, :])
                    .then_inc(_so, 16), s.wait_ge(_so, w)))
                s_o.bump(16)
                p.emit()

        vtb = E(nc.sbuf_tensor(pfx + "vtb", [128, 16, 128], BF16))
        p.add("vector", lambda v: [v.tensor_copy(vtb[:, tt, :], vtm[:, tt, :])
                                   for tt in range(16)])
        p.emit()

        def outs(s, w=s_o.v + 64):
            s.dma_start(out=o_vt.rearrange("(tt p) h -> p tt h", p=128), in_=vtb[:, :, :]).then_inc(_so, 16)
            s.dma_start(out=o_w.rearrange("(tt p) e -> p tt e", p=128), in_=swt[:, :, :]).then_inc(_so, 16)
            s.dma_start(out=o_qn.rearrange("a b -> b a"), in_=qnc[:, :],
                        allow_slow_non_contiguous=True).then_inc(_so, 16)
            s.dma_start(out=o_km, in_=c2[0:1, 0:1]).then_inc(_so, 16)
            s.wait_ge(_so, w)
        p.add("sync", outs)
        s_o.bump(64)
        p.emit()
    return nc


def inputs_l0f_qkv(ins, core, stream0):
    ts = slice(TLOC * core, TLOC * (core + 1))
    qkvp = np.concatenate([ins["L0_ffn_q_proj"].transpose(1, 0, 2).reshape(C, 2048),
                           ins["L0_ffn_k_proj"].transpose(1, 0, 2).reshape(C, 2048),
                           ins["L0_ffn_v_proj"].transpose(1, 0, 2).reshape(C, 2048)], axis=1)
    return {
        "ss": f32c(stream0[ts]),
        "simT": f32c(ins["L0_ffn_sim_matrix"].T),
        "gatesr": f32c(ins["L0_ffn_gates"][None, :]),
        "qkvp": f32c(qkvp),
        "ident": f32c(np.eye(128)),
    }



def build_l0f_attn(fu=None):
    nc = fu["nc"] if fu is not None else bass.Bass(
        "TRN2", target_bir_lowering=False, detect_race_conditions=False)
    pfx = fu["pfx"] if fu is not None else ""
    kf = dram_in(nc, fu, "kf", [128, T], BF16)
    vt = dram_in(nc, fu, "vt", [T, 128], BF16)
    qf = dram_in(nc, fu, "qf", [128, TLOC], BF16)
    qn = dram_in(nc, fu, "qn", [1, TLOC], F32)
    kms = dram_in(nc, fu, "kms", [8, 1], F32)
    wrt = dram_in(nc, fu, "wrt", [TLOC, 16], F32)
    oproj = dram_in(nc, fu, "oproj", [2048, C], F32)
    ss = dram_in(nc, fu, "ss", [TLOC, C], F32)
    ident = dram_in(nc, fu, "ident", [128, 128], F32)
    eindi = dram_in(nc, fu, "eindi", [16, 2048], F32)
    o_s = dram_out(nc, fu, "o_s", [TLOC, C], F32)

    with ExitStack() as _st:
        E = _st.enter_context
        skf = E(nc.sbuf_tensor(pfx + "skf", [128, T], BF16))
        svt = E(nc.sbuf_tensor(pfx + "svt", [128, 128, 128], BF16))
        sqf = E(nc.sbuf_tensor(pfx + "sqf", [128, TLOC], BF16))
        smr = E(nc.sbuf_tensor(pfx + "smr", [1, TLOC], BF16))
        skm = E(nc.sbuf_tensor(pfx + "skm", [8, 1], F32))
        skm2 = E(nc.sbuf_tensor(pfx + "skm2", [1, 8], F32))
        snegs = E(nc.sbuf_tensor(pfx + "snegs", [1, 128], BF16))
        sonec = E(nc.sbuf_tensor(pfx + "sonec", [128, 1], BF16))
        sw = E(nc.sbuf_tensor(pfx + "sw", [128, 16, 16], F32))
        sop = E(nc.sbuf_tensor(pfx + "sop", [128, 16, C], BF16))
        sid = E(nc.sbuf_tensor(pfx + "sid", [128, 128], BF16))
        satt8 = E(nc.sbuf_tensor(pfx + "satt8", [128, 4, 1024], BF16))
        ofacc = E(nc.sbuf_tensor(pfx + "ofacc", [128, 2, 1024], F32))
        rsr = E(nc.sbuf_tensor(pfx + "rsr", [1, 2048], F32))
        rsc = E(nc.sbuf_tensor(pfx + "rsc", [128, 16], F32))
        swT = E(nc.sbuf_tensor(pfx + "swT", [16, 2048], BF16))
        bfw = E(nc.sbuf_tensor(pfx + "bfw", [128, 16], BF16))
        sidf = E(nc.sbuf_tensor(pfx + "sidf", [1, 1], F32))
        eind = E(nc.sbuf_tensor(pfx + "eind", [16, 2048], BF16))
        ohf = E(nc.sbuf_tensor(pfx + "ohf", [128, 16, 1024], BF16))
        sxt = E(nc.sbuf_tensor(pfx + "sxt", [128, C], F32))
        souo = E(nc.sbuf_tensor(pfx + "souo", [128, C], F32))
        sst2 = E(nc.sbuf_tensor(pfx + "sst2", [128, C], F32))
        mcol = E(nc.sbuf_tensor(pfx + "mcol", [128, 1], F32))
        vcol = E(nc.sbuf_tensor(pfx + "vcol", [128, 1], F32))
        p, (_sx, _sg, _smm, _sev, _so), (s_x, s_g, s_mm, s_ev, s_o) = fu_start(nc, fu, E)

        def loads(s):
            s.dma_start(out=skf[:, :], in_=kf).then_inc(_sx, 16)
            s.dma_start(out=svt[:, :, :], in_=vt.rearrange("(kt p) h -> p kt h", p=128)).then_inc(_sx, 16)
            s.dma_start(out=sqf[:, :], in_=qf).then_inc(_sx, 16)
            s.dma_start(out=rsr[:, :], in_=qn).then_inc(_sx, 16)
            s.dma_start(out=skm[:, :], in_=kms).then_inc(_sx, 16)
            s.dma_start(out=skm2[:, :], in_=kms.rearrange("a b -> b a"),
                        allow_slow_non_contiguous=True).then_inc(_sx, 16)
            s.dma_start(out=sw[:, :, :], in_=wrt.rearrange("(tt p) e -> p tt e", p=128)).then_inc(_sx, 16)
        p.add("sync", loads)
        p.add("gpsimd", lambda g: (
            g.dma_start(out=sop[:, :, :], in_=oproj.rearrange("(k p) c -> p k c", p=128)).then_inc(_sg, 16),
            g.dma_start(out=sid[:, :], in_=ident).then_inc(_sg, 16),
            g.dma_start(out=eind[:, :], in_=eindi).then_inc(_sg, 16)))
        s_x.bump(16 * 7); s_g.bump(48)
        p.add("vector", lambda v, w=s_x.v, wg=s_g.v: (
            v.wait_ge(_sx, w), v.wait_ge(_sg, wg),
            v.memset(snegs[:, :], -1.0)))
        p.add("gpsimd", lambda g: (g.memset(sonec[:, :], 1.0),
                                   g.memset(sidf[:, :], 1.0)))
        p.emit()

        p.add("vector", lambda v: v.reduce_max(mcol[0:1, 0:1], skm2[:, :], axis=AX.X))
        p.emit()

        def mrow(v):
            v.tensor_tensor(rsr[:, :], rsr[:, :],
                            mcol[0:1, 0:1].broadcast_to([1, 2048]), ALU.mult)
            v.tensor_copy(smr[:, :], rsr[:, :])
        p.add("vector", mrow)
        p.emit()

        with ExitStack() as _ps:
            psS0 = _ps.enter_context(nc.psum_tensor(pfx + "psS0", [128, 1024], F32))
            psS1 = _ps.enter_context(nc.psum_tensor(pfx + "psS1", [128, 1024], F32))
            psOf = _ps.enter_context(nc.psum_tensor(pfx + "psOf", [128, 1024], F32))
            psRr = _ps.enter_context(nc.psum_tensor(pfx + "psRr", [1, 1024], F32))
            p.add("vector", lambda v: (v.memset(ofacc[:, :, :], 0.0),
                                       v.memset(rsr[:, :], 0.0)))
            p.emit()
            for qh in range(2):
                pss = (psS0, psS1)
                for kb in range(32):
                    for ki in range(4):
                        kt = kb * 4 + ki
                        ps = pss[ki % 2]
                        evw = max(0, s_ev.v - 1)

                        def mmsc2(t, kt=kt, ki=ki, ps=ps, qh=qh, evw=evw):
                            if ki >= 2:
                                t.wait_ge(_sev, evw)
                            for g2 in range(2):
                                t.matmul(ps[:, g2 * 512:(g2 + 1) * 512],
                                         skf[:, kt * 128:(kt + 1) * 128],
                                         sqf[:, qh * 1024 + g2 * 512:qh * 1024 + (g2 + 1) * 512],
                                         start=True, stop=False)
                                mmi = t.matmul(ps[:, g2 * 512:(g2 + 1) * 512], snegs[0:1, :],
                                         smr[0:1, qh * 1024 + g2 * 512:qh * 1024 + (g2 + 1) * 512],
                                         start=False, stop=True)
                            mmi.then_inc(_smm, 1)
                        p.add("tensor", mmsc2)
                        s_mm.bump(1)
                        mmv = s_mm.v

                        def expev(s, ki=ki, ps=ps, mmv=mmv):
                            s.wait_ge(_smm, mmv)
                            s.activation(satt8[:, ki, :], ps[:, :], AF.Exp, bias=0.0,
                                         scale=0.08838834764831845).then_inc(_sev, 1)
                        p.add("scalar", expev)
                        s_ev.bump(1)
                    attv = s_ev.v

                    def mmav(t, kb=kb, attv=attv):
                        t.wait_ge(_sev, attv)
                        for g2 in range(2):
                            for ki in range(4):
                                t.matmul(psOf[:, g2 * 512:(g2 + 1) * 512],
                                         svt[:, kb * 4 + ki, :],
                                         satt8[:, ki, g2 * 512:(g2 + 1) * 512],
                                         start=(ki == 0), stop=(ki == 3))
                        for g2 in range(2):
                            for ki in range(4):
                                mmi = t.matmul(psRr[:, g2 * 512:(g2 + 1) * 512],
                                         sonec[:, :],
                                         satt8[:, ki, g2 * 512:(g2 + 1) * 512],
                                         start=(ki == 0), stop=(ki == 3))
                        mmi.then_inc(_smm, 1)
                    p.add("tensor", mmav)
                    s_mm.bump(1)

                    def accof(v, qh=qh, w=s_mm.v):
                        v.wait_ge(_smm, w)
                        v.tensor_add(ofacc[:, qh, :], ofacc[:, qh, :], psOf[:, :])
                        v.tensor_add(rsr[:, qh * 1024:(qh + 1) * 1024],
                                     rsr[:, qh * 1024:(qh + 1) * 1024], psRr[:, :]).then_inc(_sev, 1)
                    p.add("vector", accof)
                    s_ev.bump(1)
                p.emit()

        def rcp(v):
            v.reciprocal(rsr[:, :], rsr[:, :])
        p.add("vector", rcp)
        p.emit()

        with ExitStack() as _ps:
            psW = _ps.enter_context(nc.psum_tensor(pfx + "psW", [128, 512], F32))
            psC = _ps.enter_context(nc.psum_tensor(pfx + "psC", [128, 16], F32))
            psCb = _ps.enter_context(nc.psum_tensor(pfx + "psCb", [16, 128], BF16))
            psO2 = _ps.enter_context(nc.psum_tensor(pfx + "psO2", [128, 1024], F32))

            for tt in range(16):
                p.add("tensor", lambda t, tt=tt: t.transpose(
                    psC[:, tt:tt + 1], rsr[0:1, tt * 128:(tt + 1) * 128],
                    sidf[:, :]).then_inc(_smm, 1))
                s_mm.bump(1)
            p.add("scalar", lambda s, w=s_mm.v: (
                s.wait_ge(_smm, w),
                s.activation(rsc[:, :], psC[:, :], AF.Copy, bias=0.0, scale=1.0)))
            p.emit()

            for tt in range(16):
                p.add("gpsimd", lambda g, tt=tt: g.tensor_copy(bfw[:, :], sw[:, tt, :]))
                p.emit()
                p.add("tensor", lambda t, tt=tt: t.transpose(
                    psCb[:, :], bfw[:, :], sid[:, :]).then_inc(_smm, 1))
                s_mm.bump(1)
                p.add("vector", lambda v, tt=tt, w=s_mm.v: (
                    v.wait_ge(_smm, w),
                    v.tensor_copy(swT[0:16, tt * 128:(tt + 1) * 128], psCb[0:16, :])))
                p.emit()

            for th2 in range(2):
              for e in range(16):
                for qg in range(2):
                    def mmwb(t, th2=th2, e=e, qg=qg):
                        t.matmul(psW[:, :], eind[0:16, e * 128:(e + 1) * 128],
                                 swT[0:16, th2 * 1024 + qg * 512:th2 * 1024 + (qg + 1) * 512],
                                 start=True, stop=True).then_inc(_smm, 1)
                    p.add("tensor", mmwb)
                    s_mm.bump(1)

                    def mkoh(v, th2=th2, e=e, qg=qg, w=s_mm.v):
                        v.wait_ge(_smm, w)
                        v.tensor_tensor(ohf[:, e, qg * 512:(qg + 1) * 512],
                                        ofacc.rearrange("p h q -> p (h q)")
                                        [:, th2 * 1024 + qg * 512:th2 * 1024 + (qg + 1) * 512],
                                        psW[:, :], ALU.mult)
                    p.add("vector", mkoh)
                    p.emit()

              for tt in range(th2 * 8, th2 * 8 + 8):
                p.add("sync", lambda s, tt=tt: s.dma_start(
                    out=sxt[:, :], in_=ss[tt * 128:(tt + 1) * 128, :]).then_inc(_sx, 16))
                s_x.bump(16)
                xw = s_x.v

                def mmo(t, tt=tt, th2=th2):
                    for cg in range(2):
                        for kc in range(16):
                            mmi = t.matmul(psO2[:, cg * 512:(cg + 1) * 512],
                                     ohf[:, kc, (tt - th2 * 8) * 128:(tt - th2 * 8 + 1) * 128],
                                     sop[:, kc, cg * 512:(cg + 1) * 512],
                                     start=(kc == 0), stop=(kc == 15))
                    mmi.then_inc(_smm, 1)
                p.add("tensor", mmo)
                s_mm.bump(1)

                def evf(v, tt=tt, w=s_mm.v, xw=xw):
                    v.wait_ge(_smm, w)
                    v.wait_ge(_sx, xw)
                    v.tensor_tensor(souo[:, :], psO2[:, 0:1024],
                                    rsc[:, tt:tt + 1].broadcast_to([128, 1024]), ALU.mult)
                    v.tensor_add(souo[:, :], souo[:, :], sxt[:, :])
                p.add("vector", evf)
                p.emit()
                emit_ln_tm(p, nc, souo, sst2, mcol, vcol)
                p.add("sync", lambda s, tt=tt, w=s_o.v + 16: (
                    s.dma_start(out=o_s[tt * 128:(tt + 1) * 128, :], in_=souo[:, :]).then_inc(_so, 16),
                    s.wait_ge(_so, w)))
                s_o.bump(16)
                p.emit()
    return nc


def inputs_l0f_attn(ins, core, kf_all, vt_all, b1res, stream0):
    ts = slice(TLOC * core, TLOC * (core + 1))
    return {
        "kf": np.ascontiguousarray(kf_all),
        "vt": np.ascontiguousarray(vt_all),
        "qf": np.ascontiguousarray(b1res[core]["o_qf"]),
        "qn": f32c(np.asarray(b1res[core]["o_qn"], np.float32).reshape(1, TLOC)),
        "kms": f32c(np.concatenate([np.asarray(r["o_km"], np.float32) for r in b1res], axis=0)),
        "wrt": f32c(np.asarray(b1res[core]["o_w"], np.float32)),
        "oproj": f32c(ins["L0_ffn_o_proj"].reshape(2048, C)),
        "ss": f32c(stream0[ts]),
        "ident": f32c(np.eye(128)),
        "eindi": f32c(np.repeat(np.eye(16, dtype=np.float32), 128, axis=1)),
    }



def build_l1a_ret(fu=None):
    nc = fu["nc"] if fu is not None else bass.Bass(
        "TRN2", target_bir_lowering=False, detect_race_conditions=False)
    pfx = fu["pfx"] if fu is not None else ""
    sfm = dram_in(nc, fu, "sfm", [128, T], F32)
    wq = dram_in(nc, fu, "wq", [128, 128], F32)
    wk = dram_in(nc, fu, "wk", [128, 128], F32)
    wv = dram_in(nc, fu, "wv", [128, 128], F32)
    tq1 = dram_in(nc, fu, "tq1", [128, T], F32)
    tq2 = dram_in(nc, fu, "tq2", [128, T], F32)
    tk1 = dram_in(nc, fu, "tk1", [128, T], F32)
    tk2 = dram_in(nc, fu, "tk2", [128, T], F32)
    prm = dram_in(nc, fu, "prm", [128, 128], F32)
    dtc = dram_in(nc, fu, "dtc", [128, 128], F32)
    gcols = dram_in(nc, fu, "gcols", [128, 4], F32)
    gnp = dram_in(nc, fu, "gnp", [1, 256], F32)
    ident = dram_in(nc, fu, "ident", [128, 128], F32)
    o_yn = dram_out(nc, fu, "o_yn", [T, 128], BF16)

    SPAN = 4096
    with ExitStack() as _st:
        E = _st.enter_context
        sx = E(nc.sbuf_tensor(pfx + "sx", [128, T], BF16))
        sqQ = E(nc.sbuf_tensor(pfx + "sqQ", [128, T], BF16))
        sqK = E(nc.sbuf_tensor(pfx + "sqK", [128, T], BF16))
        sqV = E(nc.sbuf_tensor(pfx + "sqV", [128, 128, 128], BF16))
        stb = E(nc.sbuf_tensor(pfx + "stb", [128, SPAN], BF16))
        stb2 = E(nc.sbuf_tensor(pfx + "stb2", [128, SPAN], BF16))
        srot = E(nc.sbuf_tensor(pfx + "srot", [128, SPAN], BF16))
        swq = E(nc.sbuf_tensor(pfx + "swq", [128, 128], BF16))
        swk = E(nc.sbuf_tensor(pfx + "swk", [128, 128], BF16))
        swv = E(nc.sbuf_tensor(pfx + "swv", [128, 128], BF16))
        sprm = E(nc.sbuf_tensor(pfx + "sprm", [128, 128], BF16))
        sdt = E(nc.sbuf_tensor(pfx + "sdt", [128, 128], BF16))
        sid = E(nc.sbuf_tensor(pfx + "sid", [128, 128], BF16))
        sgc = E(nc.sbuf_tensor(pfx + "sgc", [128, 4], F32))
        sgnp = E(nc.sbuf_tensor(pfx + "sgnp", [1, 256], F32))
        sgw = E(nc.sbuf_tensor(pfx + "sgw", [128, 128], F32))
        sgb = E(nc.sbuf_tensor(pfx + "sgb", [128, 128], F32))
        sS = E(nc.sbuf_tensor(pfx + "sS", [128, 128], F32))
        sSb = E(nc.sbuf_tensor(pfx + "sSb", [128, 128], BF16))
        attb = E(nc.sbuf_tensor(pfx + "attb", [128, 128], BF16))
        y2b = E(nc.sbuf_tensor(pfx + "y2b", [128, 128], F32))
        ytl = E(nc.sbuf_tensor(pfx + "ytl", [128, 128], F32))
        ktb = E(nc.sbuf_tensor(pfx + "ktb", [128, 128], BF16))
        vsb = E(nc.sbuf_tensor(pfx + "vsb", [128, 128], BF16))
        ynb = E(nc.sbuf_tensor(pfx + "ynb", [128, 128], BF16))
        mc = E(nc.sbuf_tensor(pfx + "mc", [128, 1], F32))
        vc = E(nc.sbuf_tensor(pfx + "vc", [128, 1], F32))
        sqs = E(nc.sbuf_tensor(pfx + "sqs", [128, 128], F32))
        onesf = E(nc.sbuf_tensor(pfx + "onesf", [1, 128], F32))
        psA = E(nc.psum_tensor(pfx + "psA", [128, 512], F32))
        psB = E(nc.psum_tensor(pfx + "psB", [128, 512], F32))
        psT = E(nc.psum_tensor(pfx + "psT", [128, 128], BF16))
        p, (_sx, _sg, _smm, _sev, _so), (s_x, s_g, s_mm, s_ev, s_o) = fu_start(nc, fu, E)

        def loads(g):
            g.dma_start(out=sx[:, :], in_=sfm).then_inc(_sg, 16)
            g.dma_start(out=swq[:, :], in_=wq).then_inc(_sg, 16)
            g.dma_start(out=swk[:, :], in_=wk).then_inc(_sg, 16)
            g.dma_start(out=swv[:, :], in_=wv).then_inc(_sg, 16)
            g.dma_start(out=sprm[:, :], in_=prm).then_inc(_sg, 16)
            g.dma_start(out=sdt[:, :], in_=dtc).then_inc(_sg, 16)
            g.dma_start(out=sid[:, :], in_=ident).then_inc(_sg, 16)
        p.add("gpsimd", loads)
        p.add("sync", lambda s: (
            s.dma_start(out=sgc[:, :], in_=gcols).then_inc(_sx, 16),
            s.dma_start(out=sgnp[:, :], in_=gnp).then_inc(_sx, 16)))
        s_g.bump(16 * 7); s_x.bump(32)
        p.add("vector", lambda v, w=s_x.v, wg=s_g.v: (
            v.wait_ge(_sx, w), v.wait_ge(_sg, wg),
            v.memset(sS[:, :], 0.0), v.memset(sSb[:, :], 0.0),
            v.memset(onesf[:, :], 1.0)))
        p.emit()

        for which, dst in ((0, sgw), (1, sgb)):
            p.add("tensor", lambda t, which=which: t.matmul(
                psA[:, 0:128], onesf[:, :], sgnp[0:1, which * 128:(which + 1) * 128],
                start=True, stop=True).then_inc(_smm, 1))
            s_mm.bump(1)
            p.add("vector", lambda v, dst=dst, w=s_mm.v: (
                v.wait_ge(_smm, w), v.tensor_copy(dst[:, :], psA[:, 0:128])))
            p.emit()

        for tg in range(T // 512):
            def mmqk(t, tg=tg):
                t.matmul(psA[:, :], swq[:, :], sx[:, tg * 512:(tg + 1) * 512],
                         start=True, stop=True)
                t.matmul(psB[:, :], swk[:, :], sx[:, tg * 512:(tg + 1) * 512],
                         start=True, stop=True).then_inc(_smm, 1)
            p.add("tensor", mmqk)
            s_mm.bump(1)
            p.add("vector", lambda v, tg=tg, w=s_mm.v: (
                v.wait_ge(_smm, w),
                v.tensor_copy(sqQ[:, tg * 512:(tg + 1) * 512], psA[:, :]),
                v.tensor_copy(sqK[:, tg * 512:(tg + 1) * 512], psB[:, :])))
            p.emit()
        for tt in range(128):
            pst = (psA, psB)[tt % 2]
            p.add("tensor", lambda t, tt=tt, pst=pst: t.matmul(
                pst[:, 0:128], sx[:, tt * 128:(tt + 1) * 128], swv[:, :],
                start=True, stop=True).then_inc(_smm, 1))
            s_mm.bump(1)
            p.add("vector", lambda v, tt=tt, pst=pst, w=s_mm.v: (
                v.wait_ge(_smm, w),
                v.tensor_copy(sqV[:, tt, :], pst[:, 0:128])))
            if tt % 2 == 1:
                p.emit()
        p.emit()

        for which, qk, t1d, t2d in ((0, sqQ, tq1, tq2), (1, sqK, tk1, tk2)):
            for sp in range(T // SPAN):
                tsp = slice(sp * SPAN, (sp + 1) * SPAN)
                p.add("gpsimd", lambda g, t1d=t1d, t2d=t2d, tsp=tsp: (
                    g.dma_start(out=stb[:, :], in_=t1d[:, tsp]).then_inc(_sg, 16),
                    g.dma_start(out=stb2[:, :], in_=t2d[:, tsp]).then_inc(_sg, 16)))
                s_g.bump(32)
                tw = s_g.v
                for g2 in range(SPAN // 512):
                    pst = (psA, psB)[g2 % 2]
                    p.add("tensor", lambda t, qk=qk, sp=sp, g2=g2, pst=pst: t.matmul(
                        pst[:, :], sprm[:, :], qk[:, sp * SPAN + g2 * 512:sp * SPAN + (g2 + 1) * 512],
                        start=True, stop=True).then_inc(_smm, 1))
                    s_mm.bump(1)
                    p.add("vector", lambda v, g2=g2, pst=pst, w=s_mm.v: (
                        v.wait_ge(_smm, w),
                        v.tensor_copy(srot[:, g2 * 512:(g2 + 1) * 512], pst[:, :])))
                    if g2 % 2 == 1:
                        p.emit()
                p.emit()

                def xp(v, qk=qk, tsp=tsp, tw=tw):
                    v.wait_ge(_sg, tw)
                    v.tensor_mul(srot[:, :], srot[:, :], stb2[:, :])
                    v.tensor_mul(qk[:, tsp], qk[:, tsp], stb[:, :])
                    v.tensor_add(qk[:, tsp], qk[:, tsp], srot[:, :])
                p.add("vector", xp)
                p.emit()

        for tt in range(128):
            def mmr(t, tt=tt):
                sl = slice(tt * 128, (tt + 1) * 128)
                t.matmul(psA[:, 0:128], sqK[:, sl], sqQ[:, sl],
                         start=True, stop=True).then_inc(_smm, 1)
                t.matmul(psB[:, 0:128], sqQ[:, sl], sSb[:, :],
                         start=True, stop=True).then_inc(_smm, 1)
                t.transpose(psT[:, :], sqK[:, sl], sid[:, :]).then_inc(_smm, 1)
            p.add("tensor", mmr)
            s_mm.bump(3)
            w0 = s_mm.v

            def ev1(v, tt=tt, w0=w0):
                v.wait_ge(_smm, w0)
                v.tensor_tensor(attb[:, :], psA[:, 0:128], sdt[:, :], ALU.mult)
                v.tensor_copy(y2b[:, :], psB[:, 0:128])
                v.tensor_copy(ktb[:, :], psT[:, :])
                v.tensor_scalar(vsb[:, :], sqV[:, tt, :], sgc[:, 1:2], None, ALU.mult)
                v.then_inc_dummy if False else None
            p.add("vector", ev1)
            p.emit()

            def mmr2(t, tt=tt):
                t.matmul(psA[:, 0:128], attb[:, :], sqV[:, tt, :],
                         start=True, stop=True).then_inc(_smm, 1)
                t.matmul(psB[:, 0:128], ktb[:, :], vsb[:, :],
                         start=True, stop=True).then_inc(_smm, 1)
            p.add("tensor", mmr2)
            s_mm.bump(2)
            w1 = s_mm.v

            def ev2(v, tt=tt, w1=w1):
                v.wait_ge(_smm, w1)
                v.scalar_tensor_tensor(ytl[:, :], y2b[:, :], sgc[:, 0:1], psA[:, 0:128],
                                       ALU.mult, ALU.add)
                v.scalar_tensor_tensor(sS[:, :], sS[:, :], sgc[:, 2:3], psB[:, 0:128],
                                       ALU.mult, ALU.add)
                v.tensor_copy(sSb[:, :], sS[:, :])
            p.add("vector", ev2)
            p.emit()

            p.add("vector", lambda v: v.reduce_sum(mc[:, :], ytl[:, :], axis=AX.X))
            p.emit()

            def gn1(v):
                v.tensor_scalar(sqs[:, :], mc[:, 0:1].broadcast_to([128, 128]),
                                1.0 / 128, None, ALU.mult)
                v.tensor_sub(ytl[:, :], ytl[:, :], sqs[:, :])
            p.add("vector", gn1)
            p.emit()
            p.add("scalar", lambda s: s.activation(sqs[:, :], ytl[:, :], AF.Square,
                                                   bias=0.0, scale=1.0))
            p.emit()
            p.add("vector", lambda v: v.reduce_sum(vc[:, :], sqs[:, :], axis=AX.X))
            p.emit()
            p.add("vector", lambda v: v.tensor_scalar(
                sqs[:, :], vc[:, 0:1].broadcast_to([128, 128]), 1.0 / 128, 1e-5,
                ALU.mult, ALU.add))
            p.emit()
            p.add("scalar", lambda s: (
                s.activation(sqs[:, :], sqs[:, :], AF.Ln, bias=0.0, scale=1.0),
                s.activation(sqs[:, :], sqs[:, :], AF.Exp, bias=0.0, scale=-0.5)))
            p.emit()

            def gn3(v, tt=tt):
                v.tensor_mul(ytl[:, :], ytl[:, :], sqs[:, :])
                v.tensor_mul(ytl[:, :], ytl[:, :], sgw[:, :])
                v.tensor_add(ytl[:, :], ytl[:, :], sgb[:, :])
                v.tensor_copy(ynb[:, :], ytl[:, :])
            p.add("vector", gn3)
            p.emit()
            p.add("sync", lambda s, tt=tt, w=s_o.v + 16: (
                s.dma_start(out=o_yn[tt * 128:(tt + 1) * 128, :], in_=ynb[:, :]).then_inc(_so, 16),
                s.wait_ge(_so, w)))
            s_o.bump(16)
            p.emit()
    return nc


def inputs_l1a_ret(ins, core, stream1):
    import math
    h = core
    gam = 1.0 - np.exp(np.linspace(math.log(1 / 32), math.log(1 / 512), 8))
    g = np.float64(gam[h])
    d = 128
    half = d // 2
    scale = (np.arange(0, d, 2, dtype=np.float64) + 0.4 * d) / (1.4 * d)
    pos = np.arange(T, dtype=np.float64)
    sc = scale[None, :] ** (pos / 512.0)[:, None]
    inv_freq = 1.0 / 10000.0 ** (np.arange(half, dtype=np.float64) / half)
    sinus = pos[:, None] * inv_freq[None, :]
    sin, cos = np.sin(sinus), np.cos(sinus)
    cs = np.repeat(cos * sc, 2, axis=1)
    ss_ = np.repeat(sin * sc, 2, axis=1)
    ci = np.repeat(cos / sc, 2, axis=1)
    si = np.repeat(sin / sc, 2, axis=1)
    prm = np.zeros((128, 128), np.float32)
    for i in range(64):
        prm[2 * i + 1, 2 * i] = -1.0
        prm[2 * i, 2 * i + 1] = 1.0
    idx = np.arange(128)
    dec = g ** (idx[None, :] - idx[:, None])
    dtc = np.where(idx[None, :] >= idx[:, None], dec, 0.0)
    gcols = np.stack([g ** idx, g ** (128 - idx),
                      np.full(128, g ** 128), np.zeros(128)], axis=1)
    sl = slice(128 * h, 128 * (h + 1))
    return {
        "sfm": f32c(stream1.T[sl]),
        "wq": f32c(ins["L1_attn_W_Q"][h]),
        "wk": f32c(ins["L1_attn_W_K"][h]),
        "wv": f32c(ins["L1_attn_W_V"][h]),
        "tq1": f32c(cs.T), "tq2": f32c(ss_.T), "tk1": f32c(ci.T), "tk2": f32c(si.T),
        "prm": prm,
        "dtc": f32c(dtc),
        "gcols": f32c(gcols),
        "gnp": f32c(np.concatenate([ins["L1_attn_gn_weight"][sl],
                                    ins["L1_attn_gn_bias"][sl]])[None, :]),
        "ident": f32c(np.eye(128)),
    }



def build_l1a_out(fu=None):
    nc = fu["nc"] if fu is not None else bass.Bass(
        "TRN2", target_bir_lowering=False, detect_race_conditions=False)
    pfx = fu["pfx"] if fu is not None else ""
    sfT = dram_in(nc, fu, "sfT", [C, TLOC], F32)
    ss = dram_in(nc, fu, "ss", [TLOC, C], F32)
    yn = dram_in(nc, fu, "yn", [TLOC, C], BF16)
    wg = dram_in(nc, fu, "wg", [C, C], F32)
    wo = dram_in(nc, fu, "wo", [C, C], F32)
    ident = dram_in(nc, fu, "ident", [128, 128], F32)
    o_s = dram_out(nc, fu, "o_s", [TLOC, C], F32)

    with ExitStack() as _st:
        E = _st.enter_context
        sxf = E(nc.sbuf_tensor(pfx + "sxf", [128, 8, TLOC], BF16))
        swg = E(nc.sbuf_tensor(pfx + "swg", [128, 8, C], BF16))
        swo = E(nc.sbuf_tensor(pfx + "swo", [128, 8, C], BF16))
        syn = E(nc.sbuf_tensor(pfx + "syn", [128, 16, C], BF16))
        sid = E(nc.sbuf_tensor(pfx + "sid", [128, 128], BF16))
        sgt = E(nc.sbuf_tensor(pfx + "sgt", [128, C], BF16))
        spr = E(nc.sbuf_tensor(pfx + "spr", [128, C], BF16))
        sprf = E(nc.sbuf_tensor(pfx + "sprf", [128, 8, TLOC], BF16))
        sxt = E(nc.sbuf_tensor(pfx + "sxt", [128, C], F32))
        souo = E(nc.sbuf_tensor(pfx + "souo", [128, C], F32))
        sst2 = E(nc.sbuf_tensor(pfx + "sst2", [128, C], F32))
        mcol = E(nc.sbuf_tensor(pfx + "mcol", [128, 1], F32))
        vcol = E(nc.sbuf_tensor(pfx + "vcol", [128, 1], F32))
        psA = E(nc.psum_tensor(pfx + "psA", [128, 512], F32))
        psB = E(nc.psum_tensor(pfx + "psB", [128, 512], F32))
        psT = E(nc.psum_tensor(pfx + "psT", [128, 128], BF16))
        p, (_sx, _sg, _smm, _sev, _so), (s_x, s_g, s_mm, s_ev, s_o) = fu_start(nc, fu, E)

        def loads(g):
            g.dma_start(out=sxf[:, :, :], in_=sfT.rearrange("(k p) t -> p k t", p=128)).then_inc(_sg, 16)
            g.dma_start(out=swg[:, :, :], in_=wg.rearrange("(k p) c -> p k c", p=128)).then_inc(_sg, 16)
            g.dma_start(out=swo[:, :, :], in_=wo.rearrange("(k p) c -> p k c", p=128)).then_inc(_sg, 16)
            g.dma_start(out=syn[:, :, :], in_=yn.rearrange("(tt p) c -> p tt c", p=128)).then_inc(_sg, 16)
            g.dma_start(out=sid[:, :], in_=ident).then_inc(_sg, 16)
        p.add("gpsimd", loads)
        s_g.bump(16 * 5)
        p.add("vector", lambda v, wg_=s_g.v: v.wait_ge(_sg, wg_))
        p.emit()

        for tt in range(16):
            for cg in range(2):
                pst = (psA, psB)[cg]

                def mmg(t, tt=tt, cg=cg, pst=pst):
                    for k in range(8):
                        mmi = t.matmul(pst[:, :], sxf[:, k, tt * 128:(tt + 1) * 128],
                                 swg[:, k, cg * 512:(cg + 1) * 512],
                                 start=(k == 0), stop=(k == 7))
                    mmi.then_inc(_smm, 1)
                p.add("tensor", mmg)
                s_mm.bump(1)
                p.add("scalar", lambda s, tt=tt, cg=cg, pst=pst, w=s_mm.v: (
                    s.wait_ge(_smm, w),
                    s.activation(sgt[:, cg * 512:(cg + 1) * 512], pst[:, :],
                                 AF.Silu, bias=0.0, scale=1.0)))
            p.emit()

            def prod(v, tt=tt):
                v.tensor_mul(spr[:, :], sgt[:, :], syn[:, tt, :])
            p.add("vector", prod)
            p.emit()
            for kc in range(8):
                p.add("tensor", lambda t, tt=tt, kc=kc: t.transpose(
                    psT[:, :], spr[:, kc * 128:(kc + 1) * 128], sid[:, :]).then_inc(_smm, 1))
                s_mm.bump(1)
                p.add("vector", lambda v, tt=tt, kc=kc, w=s_mm.v: (
                    v.wait_ge(_smm, w),
                    v.tensor_copy(sprf[:, kc, tt * 128:(tt + 1) * 128], psT[:, :])))
                p.emit()

        for tt in range(16):
            p.add("sync", lambda s, tt=tt: s.dma_start(
                out=sxt[:, :], in_=ss[tt * 128:(tt + 1) * 128, :]).then_inc(_sx, 16))
            s_x.bump(16)
            xw = s_x.v
            for cg in range(2):
                pst = (psA, psB)[cg]

                def mmo(t, tt=tt, cg=cg, pst=pst):
                    for k in range(8):
                        mmi = t.matmul(pst[:, :], sprf[:, k, tt * 128:(tt + 1) * 128],
                                 swo[:, k, cg * 512:(cg + 1) * 512],
                                 start=(k == 0), stop=(k == 7))
                    mmi.then_inc(_smm, 1)
                p.add("tensor", mmo)
                s_mm.bump(1)
                p.add("vector", lambda v, tt=tt, cg=cg, pst=pst, w=s_mm.v, xw=xw: (
                    v.wait_ge(_smm, w), v.wait_ge(_sx, xw),
                    v.tensor_add(souo[:, cg * 512:(cg + 1) * 512], pst[:, :],
                                 sxt[:, cg * 512:(cg + 1) * 512])))
            p.emit()
            emit_ln_tm(p, nc, souo, sst2, mcol, vcol)
            p.add("sync", lambda s, tt=tt, w=s_o.v + 16: (
                s.dma_start(out=o_s[tt * 128:(tt + 1) * 128, :], in_=souo[:, :]).then_inc(_so, 16),
                s.wait_ge(_so, w)))
            s_o.bump(16)
            p.emit()
    return nc


def inputs_l1a_out(ins, core, stream1, yn_full):
    ts = slice(TLOC * core, TLOC * (core + 1))
    return {
        "sfT": f32c(stream1.T[:, ts]),
        "ss": f32c(stream1[ts]),
        "yn": np.ascontiguousarray(yn_full[ts]),
        "wg": f32c(ins["L1_attn_W_G"]),
        "wo": f32c(ins["L1_attn_W_O"]),
        "ident": f32c(np.eye(128)),
    }



EXT = 2560
H0 = 512

def build_l1f_attn(fu=None):
    nc = fu["nc"] if fu is not None else bass.Bass(
        "TRN2", target_bir_lowering=False, detect_race_conditions=False)
    pfx = fu["pfx"] if fu is not None else ""
    xh = dram_in(nc, fu, "xh", [EXT, C], F32)
    wqkv = dram_in(nc, fu, "wqkv", [C, 3072], F32)
    wo = dram_in(nc, fu, "wo", [C, C], F32)
    qsc = dram_in(nc, fu, "qsc", [128, 8], F32)
    ksc = dram_in(nc, fu, "ksc", [128, 8], F32)
    tqc = dram_in(nc, fu, "tqc", [128, EXT], F32)
    tqs = dram_in(nc, fu, "tqs", [128, EXT], F32)
    tkc = dram_in(nc, fu, "tkc", [128, EXT], F32)
    tks = dram_in(nc, fu, "tks", [128, EXT], F32)
    mk0 = dram_in(nc, fu, "mk0", [128, 1024], F32)
    mkN = dram_in(nc, fu, "mkN", [128, 1024], F32)
    mtb = dram_in(nc, fu, "mtb", [128, 1024], F32)
    mt0 = dram_in(nc, fu, "mt0", [128, 1024], F32)
    bind = dram_in(nc, fu, "bind", [128, 8 * 16], F32)
    bind2 = dram_in(nc, fu, "bind2", [16, 8 * 128], F32)
    erows = dram_in(nc, fu, "erows", [32, 32 * 128], F32)
    prm2 = dram_in(nc, fu, "prm2", [128, 128], F32)
    ident = dram_in(nc, fu, "ident", [128, 128], F32)
    o_x = dram_out(nc, fu, "o_x", [TLOC, C], F32)

    v_hbm = nc.dram_tensor(pfx + "v_hbm", [EXT, C], BF16).ap()
    y_hbm = nc.dram_tensor(pfx + "y_hbm", [128, 8 * TLOC], BF16).ap()

    NSP = EXT // 512
    with ExitStack() as _st:
        E = _st.enter_context
        sqf = E(nc.sbuf_tensor(pfx + "sqf", [128, 8, TLOC], BF16))
        skf = E(nc.sbuf_tensor(pfx + "skf", [128, 8, EXT], BF16))
        wbig = E(nc.sbuf_tensor(pfx + "wbig", [128, 8, 2048], BF16))
        xspan = E(nc.sbuf_tensor(pfx + "xspan", [128, 8, 513], BF16))
        svtw = E(nc.sbuf_tensor(pfx + "svtw", [128, 4, 1024], BF16))
        yfw = E(nc.sbuf_tensor(pfx + "yfw", [128, 8, 256], BF16))
        ybuf = E(nc.sbuf_tensor(pfx + "ybuf", [128, 8, 128], BF16))
        ttb1 = E(nc.sbuf_tensor(pfx + "ttb1", [128, 512], BF16))
        ttb2 = E(nc.sbuf_tensor(pfx + "ttb2", [128, 512], BF16))
        smk0 = E(nc.sbuf_tensor(pfx + "smk0", [128, 1024], BF16))
        smkN = E(nc.sbuf_tensor(pfx + "smkN", [128, 1024], BF16))
        smtb = E(nc.sbuf_tensor(pfx + "smtb", [128, 1024], BF16))
        smt0 = E(nc.sbuf_tensor(pfx + "smt0", [128, 1024], BF16))
        sbind = E(nc.sbuf_tensor(pfx + "sbind", [128, 8, 16], BF16))
        sbind2 = E(nc.sbuf_tensor(pfx + "sbind2", [16, 8, 128], BF16))
        serw = E(nc.sbuf_tensor(pfx + "serw", [32, 32, 128], BF16))
        sprm = E(nc.sbuf_tensor(pfx + "sprm", [128, 128], BF16))
        sid = E(nc.sbuf_tensor(pfx + "sid", [128, 128], BF16))
        sidf = E(nc.sbuf_tensor(pfx + "sidf", [128, 128], F32))
        sqsc = E(nc.sbuf_tensor(pfx + "sqsc", [128, 8], F32))
        sksc = E(nc.sbuf_tensor(pfx + "sksc", [128, 8], F32))
        sbt = E(nc.sbuf_tensor(pfx + "sbt", [128, C], BF16))
        sqn = E(nc.sbuf_tensor(pfx + "sqn", [16, EXT], F32))
        sqnb = E(nc.sbuf_tensor(pfx + "sqnb", [16, EXT], BF16))
        srb = E(nc.sbuf_tensor(pfx + "srb", [128, 512], BF16))
        ssc = E(nc.sbuf_tensor(pfx + "ssc", [128, 1024], F32))
        se1 = E(nc.sbuf_tensor(pfx + "se1", [128, 1024], F32))
        seT = E(nc.sbuf_tensor(pfx + "seT", [128, 1024], BF16))
        stw = E(nc.sbuf_tensor(pfx + "stw", [128, 32], F32))
        strb = E(nc.sbuf_tensor(pfx + "strb", [32, 128], BF16))
        nm = E(nc.sbuf_tensor(pfx + "nm", [128, 2], F32))
        ssum = E(nc.sbuf_tensor(pfx + "ssum", [128, 2], F32))
        sxt = E(nc.sbuf_tensor(pfx + "sxt", [128, C], F32))
        souo = E(nc.sbuf_tensor(pfx + "souo", [128, C], F32))
        psA = E(nc.psum_tensor(pfx + "psA", [128, 512], F32))
        psB = E(nc.psum_tensor(pfx + "psB", [128, 512], F32))
        psW = E(nc.psum_tensor(pfx + "psW", [128, 512], F32))
        psS = E(nc.psum_tensor(pfx + "psS", [128, 1024], F32))
        psY = E(nc.psum_tensor(pfx + "psY", [64, 256], F32))
        psT = E(nc.psum_tensor(pfx + "psT", [128, 128], BF16))
        psN = psW
        psT2 = psS
        p, (_sx, _sg, _smm, _sev, _so), (s_x, s_g, s_mm, s_ev, s_o) = fu_start(nc, fu, E)

        def loads(g):
            g.dma_start(out=smk0[:, :], in_=mk0).then_inc(_sg, 16)
            g.dma_start(out=smkN[:, :], in_=mkN).then_inc(_sg, 16)
            g.dma_start(out=smtb[:, :], in_=mtb).then_inc(_sg, 16)
            g.dma_start(out=smt0[:, :], in_=mt0).then_inc(_sg, 16)
            g.dma_start(out=sbind[:, :, :], in_=bind.rearrange("p (k j) -> p k j", k=8)).then_inc(_sg, 16)
            g.dma_start(out=sbind2[:, :, :], in_=bind2.rearrange("p (k j) -> p k j", k=8)).then_inc(_sg, 16)
            g.dma_start(out=serw[:, :, :], in_=erows.rearrange("p (k j) -> p k j", k=32)).then_inc(_sg, 16)
            g.dma_start(out=sprm[:, :], in_=prm2).then_inc(_sg, 16)
            g.dma_start(out=sid[:, :], in_=ident).then_inc(_sg, 16)
        p.add("gpsimd", loads)
        p.add("sync", lambda s: (
            s.dma_start(out=sqsc[:, :], in_=qsc).then_inc(_sx, 16),
            s.dma_start(out=sksc[:, :], in_=ksc).then_inc(_sx, 16),
            s.dma_start(out=sidf[:, :], in_=ident).then_inc(_sx, 16)))
        s_g.bump(16 * 9); s_x.bump(48)
        p.add("gpsimd", lambda g, w=s_x.v, wg=s_g.v: (
            g.wait_ge(_sx, w), g.wait_ge(_sg, wg),
            g.memset(xspan[:, :, 0:1], 0.0)))
        p.emit()

        def build_span(sp):
            for tl in range(4):
                tt = sp * 4 + tl
                p.add("sync", lambda s, tt=tt: s.dma_start(
                    out=sxt[:, :], in_=xh[tt * 128:(tt + 1) * 128, :]).then_inc(_sx, 16))
                s_x.bump(16)
                p.add("vector", lambda v, w=s_x.v: (v.wait_ge(_sx, w),
                                                    v.tensor_copy(sbt[:, :], sxt[:, :])))
                p.emit()
                for kc in range(8):
                    p.add("tensor", lambda t, kc=kc: t.transpose(
                        psT[:, :], sbt[:, kc * 128:(kc + 1) * 128], sid[:, :]).then_inc(_smm, 1))
                    s_mm.bump(1)
                    p.add("vector", lambda v, tl=tl, kc=kc, w=s_mm.v: (
                        v.wait_ge(_smm, w),
                        v.tensor_copy(xspan[:, kc, 1 + tl * 128:1 + (tl + 1) * 128], psT[:, :])))
                    p.emit()

        def save_tail():
            p.add("gpsimd", lambda g: g.tensor_copy(xspan[:, :, 0:1], xspan[:, :, 512:513]))
            p.emit()

        def shs(kc, lo, hi):
            if kc < 4:
                return xspan[:, kc, 1 + lo:1 + hi]
            return xspan[:, kc, lo:hi]

        p.add("gpsimd", lambda g: g.dma_start(
            out=wbig[:, :, 0:1024],
            in_=wqkv[:, 2048:3072].rearrange("(k p) j -> p k j", p=128)).then_inc(_sg, 16))
        s_g.bump(16)
        vw = s_g.v
        p.emit()
        for sp in range(NSP):
            build_span(sp)
            for tl in range(4):
                for cg in range(2):
                    pst = (psA, psB)[cg]

                    def mmv(t, tl=tl, cg=cg, pst=pst, vw=vw):
                        t.wait_ge(_sg, vw)
                        for kc in range(8):
                            mmi = t.matmul(pst[:, :], shs(kc, tl * 128, (tl + 1) * 128),
                                     wbig[:, kc, cg * 512:(cg + 1) * 512],
                                     start=(kc == 0), stop=(kc == 7))
                        mmi.then_inc(_smm, 1)
                    p.add("tensor", mmv)
                    s_mm.bump(1)
                    p.add("vector", lambda v, cg=cg, pst=pst, w=s_mm.v: (
                        v.wait_ge(_smm, w),
                        v.tensor_copy(sbt[:, cg * 512:(cg + 1) * 512], pst[:, :])))
                p.emit()
                p.add("sync", lambda s, sp=sp, tl=tl, w=s_o.v + 16: (
                    s.dma_start(out=v_hbm[(sp * 4 + tl) * 128:(sp * 4 + tl + 1) * 128, :],
                                in_=sbt[:, :]).then_inc(_so, 16),
                    s.wait_ge(_so, w)))
                s_o.bump(16)
                p.emit()
            save_tail()

        p.add("gpsimd", lambda g: g.dma_start(
            out=wbig[:, :, :],
            in_=wqkv[:, 0:2048].rearrange("(k p) j -> p k j", p=128)).then_inc(_sg, 16))
        s_g.bump(16)
        qkw = s_g.v
        p.add("gpsimd", lambda g: g.memset(xspan[:, :, 0:1], 0.0))
        p.emit()
        for sp in range(NSP):
            build_span(sp)
            for jc in range(16):
                isq = jc < 8
                if isq and sp == 0:
                    continue
                pst = (psA, psB)[jc % 2]

                def mmqk(t, sp=sp, jc=jc, pst=pst, qkw=qkw):
                    t.wait_ge(_sg, qkw)
                    for kc in range(8):
                        mmi = t.matmul(pst[:, :], wbig[:, kc, (jc % 8) * 128 + (0 if jc < 8 else 1024):
                                                 (jc % 8) * 128 + 128 + (0 if jc < 8 else 1024)],
                                 shs(kc, 0, 512), start=(kc == 0), stop=(kc == 7))
                    mmi.then_inc(_smm, 1)
                p.add("tensor", mmqk)
                s_mm.bump(1)

                def evqk(v, sp=sp, jc=jc, pst=pst, w=s_mm.v):
                    v.wait_ge(_smm, w)
                    if jc < 8:
                        v.tensor_copy(sqf[:, jc, (sp - 1) * 512:sp * 512], pst[:, :])
                    else:
                        v.tensor_copy(skf[:, jc - 8, sp * 512:(sp + 1) * 512], pst[:, :])
                p.add("vector", evqk)
                if jc % 2 == 1:
                    p.emit()
            p.emit()
            save_tail()

        for which, dst, scol, t1d, t2d, lo, ntg in (
                (0, sqf, sqsc, tqc, tqs, H0, 4), (1, skf, sksc, tkc, tks, 0, 5)):
            for tg in range(ntg):
                for kc in range(8):
                    p.add("scalar", lambda s, dst=dst, kc=kc, tg=tg: s.activation(
                        srb[:, :], dst[:, kc, tg * 512:(tg + 1) * 512], AF.Square,
                        bias=0.0, scale=1.0))
                    p.emit()
                    p.add("tensor", lambda t, kc=kc: t.matmul(
                        psN[0:16, :], sbind[:, kc, :], srb[:, :],
                        start=(kc == 0), stop=(kc == 7)).then_inc(_smm, 1))
                    s_mm.bump(1)
                    p.emit()
                p.add("vector", lambda v, tg=tg, w=s_mm.v: (
                    v.wait_ge(_smm, w),
                    v.tensor_copy(sqn[:, tg * 512:(tg + 1) * 512], psN[0:16, :])))
                p.emit()
            p.add("scalar", lambda s, ntg=ntg: s.activation(
                sqn[:, 0:ntg * 512], sqn[:, 0:ntg * 512], AF.Sqrt, bias=0.0, scale=1.0))
            p.emit()
            p.add("vector", lambda v, ntg=ntg: (
                v.tensor_scalar_max(sqn[:, 0:ntg * 512], sqn[:, 0:ntg * 512], 1e-12),
                v.reciprocal(sqn[:, 0:ntg * 512], sqn[:, 0:ntg * 512]),
                v.tensor_copy(sqnb[:, 0:ntg * 512], sqn[:, 0:ntg * 512])))
            p.emit()
            for tg in range(ntg):
                p.add("gpsimd", lambda g, t1d=t1d, t2d=t2d, lo=lo, tg=tg: (
                    g.dma_start(out=ttb1[:, :],
                                in_=t1d[:, lo + tg * 512:lo + (tg + 1) * 512]).then_inc(_sg, 16),
                    g.dma_start(out=ttb2[:, :],
                                in_=t2d[:, lo + tg * 512:lo + (tg + 1) * 512]).then_inc(_sg, 16)))
                s_g.bump(32)
                tw = s_g.v
                for kc in range(8):
                    def mmb(t, kc=kc, tg=tg):
                        t.matmul(psW[:, :], sbind2[:, kc, :], sqnb[:, tg * 512:(tg + 1) * 512],
                                 start=True, stop=True).then_inc(_smm, 1)
                    p.add("tensor", mmb)
                    s_mm.bump(1)
                    p.add("vector", lambda v, dst=dst, scol=scol, kc=kc, tg=tg, w=s_mm.v: (
                        v.wait_ge(_smm, w),
                        v.tensor_tensor(dst[:, kc, tg * 512:(tg + 1) * 512],
                                        dst[:, kc, tg * 512:(tg + 1) * 512], psW[:, :], ALU.mult),
                        v.tensor_scalar(dst[:, kc, tg * 512:(tg + 1) * 512],
                                        dst[:, kc, tg * 512:(tg + 1) * 512],
                                        scol[:, kc:kc + 1], None, ALU.mult)))
                    p.emit()

                    def mmr(t, dst=dst, kc=kc, tg=tg):
                        t.matmul(psW[:, :], sprm[:, :], dst[:, kc, tg * 512:(tg + 1) * 512],
                                 start=True, stop=True).then_inc(_smm, 1)
                    p.add("tensor", mmr)
                    s_mm.bump(1)
                    p.add("vector", lambda v, dst=dst, kc=kc, tg=tg, tw=tw, w=s_mm.v: (
                        v.wait_ge(_smm, w), v.wait_ge(_sg, tw),
                        v.tensor_tensor(srb[:, :], psW[:, :], ttb2[:, :], ALU.mult),
                        v.tensor_tensor(dst[:, kc, tg * 512:(tg + 1) * 512],
                                        dst[:, kc, tg * 512:(tg + 1) * 512],
                                        ttb1[:, :], ALU.mult),
                        v.tensor_add(dst[:, kc, tg * 512:(tg + 1) * 512],
                                     dst[:, kc, tg * 512:(tg + 1) * 512], srb[:, :])))
                    p.emit()

        for w in range(8):
            msk = smk0 if w == 0 else smkN
            mskT = smt0 if w == 0 else smtb
            klo = 256 * w + 256
            p.add("sync", lambda s, klo=klo: s.dma_start(
                out=svtw[:, :, :],
                in_=v_hbm[klo:klo + 512, :].rearrange("(kt p) h -> p kt h", p=128))
                .then_inc(_sx, 16))
            s_x.bump(16)
            vww = s_x.v
            p.emit()
            for h in range(16):
                kc = h // 2
                ro = 64 * (h % 2)

                def mms1(t, h=h, kc=kc, ro=ro, klo=klo, w=w):
                    for qt in range(2):
                        mmi = t.matmul(psS[:, qt * 512:(qt + 1) * 512],
                                 sqf[ro:ro + 64, kc, w * 256 + qt * 128:w * 256 + (qt + 1) * 128],
                                 skf[ro:ro + 64, kc, klo:klo + 512],
                                 start=True, stop=True)
                    mmi.then_inc(_smm, 1)
                p.add("tensor", mms1)
                s_mm.bump(1)

                def ev1(v, msk=msk, w2=s_mm.v):
                    v.wait_ge(_smm, w2)
                    v.tensor_add(ssc[:, :], psS[:, :], msk[:, :])
                    v.reduce_max(nm[:, 0:1], ssc[:, 0:512], axis=AX.X, negate=True)
                    v.reduce_max(nm[:, 1:2], ssc[:, 512:1024], axis=AX.X, negate=True)
                p.add("vector", ev1)
                p.emit()

                def ex1(s, h=h):
                    s.activation(se1[:, 0:512], ssc[:, 0:512], AF.Exp, bias=nm[:, 0:1],
                                 scale=1.0, accum_out=ssum[:, 0:1])
                    s.activation(se1[:, 512:1024], ssc[:, 512:1024], AF.Exp, bias=nm[:, 1:2],
                                 scale=1.0, accum_out=ssum[:, 1:2])
                p.add("scalar", ex1)
                p.emit()
                p.add("scalar", lambda s: s.activation(ssum[:, :], ssum[:, :], AF.Ln,
                                                       bias=0.0, scale=1.0))
                p.emit()
                p.add("gpsimd", lambda g, h=h: g.tensor_sub(stw[:, 2 * h:2 * h + 2],
                                                             nm[:, :], ssum[:, :]))
                p.emit()

            p.add("tensor", lambda t: t.transpose(psT2[0:32, 0:128], stw[:, :],
                                                  sidf[:, :]).then_inc(_smm, 1))
            s_mm.bump(1)
            p.add("vector", lambda v, w2=s_mm.v: (
                v.wait_ge(_smm, w2), v.tensor_copy(strb[:, :], psT2[0:32, 0:128])))
            p.emit()

            for h in range(16):
                kc = h // 2
                ro = 64 * (h % 2)

                def mms2(t, h=h, kc=kc, ro=ro, klo=klo, w=w):
                    for ktc in range(4):
                        for qt in range(2):
                            sl2 = slice(ktc * 256 + qt * 128, ktc * 256 + (qt + 1) * 128)
                            t.matmul(psS[:, sl2],
                                     skf[ro:ro + 64, kc, klo + ktc * 128:klo + (ktc + 1) * 128],
                                     sqf[ro:ro + 64, kc,
                                         w * 256 + qt * 128:w * 256 + (qt + 1) * 128],
                                     start=True, stop=False)
                            mmi = t.matmul(psS[:, sl2], serw[:, 2 * h + qt, :], strb[:, :],
                                     start=False, stop=True)
                    mmi.then_inc(_smm, 1)
                p.add("tensor", mms2)
                s_mm.bump(1)
                p.add("scalar", lambda s, w2=s_mm.v: (
                    s.wait_ge(_smm, w2),
                    s.activation(seT[:, :], psS[:, :], AF.Exp, bias=0.0, scale=1.0)))
                p.emit()
                p.add("gpsimd", lambda g, mskT=mskT: g.tensor_mul(seT[:, :], seT[:, :], mskT[:, :]))
                p.emit()

                def mmav(t, h=h):
                    for ktc in range(4):
                        mmi = t.matmul(psY[:, :], svtw[:, ktc, 64 * h:64 * h + 64],
                                 seT[:, ktc * 256:(ktc + 1) * 256],
                                 start=(ktc == 0), stop=(ktc == 3))
                    mmi.then_inc(_smm, 1)
                p.add("tensor", mmav)
                s_mm.bump(1)
                p.add("vector", lambda v, h=h, w2=s_mm.v: (
                    v.wait_ge(_smm, w2),
                    v.tensor_copy(yfw[64 * (h % 2):64 * (h % 2) + 64, h // 2, :], psY[:, :])))
                p.emit()

            p.add("sync", lambda s, w=w, w2=s_o.v + 16: (
                s.dma_start(out=y_hbm.rearrange("p (k t) -> p k t", k=8)[:, :, w * 256:(w + 1) * 256],
                            in_=yfw[:, :, :]).then_inc(_so, 16),
                s.wait_ge(_so, w2)))
            s_o.bump(16)
            p.emit()

        p.add("gpsimd", lambda g: g.dma_start(
            out=wbig[:, :, 0:1024],
            in_=wo.rearrange("(k p) c -> p k c", p=128)).then_inc(_sg, 16))
        s_g.bump(16)
        wow = s_g.v
        p.emit()
        for tt in range(16):
            p.add("sync", lambda s, tt=tt: (
                s.dma_start(out=sxt[:, :], in_=xh[H0 + tt * 128:H0 + (tt + 1) * 128, :])
                .then_inc(_sx, 16),
                s.dma_start(out=ybuf[:, :, :],
                            in_=y_hbm.rearrange("p (k t) -> p k t", k=8)[:, :, tt * 128:(tt + 1) * 128])
                .then_inc(_sx, 16)))
            s_x.bump(32)
            xw = s_x.v
            for cg in range(2):
                pst = (psA, psB)[cg]

                def mmo(t, tt=tt, cg=cg, pst=pst, xw=xw, wow=wow):
                    t.wait_ge(_sx, xw)
                    t.wait_ge(_sg, wow)
                    for kc in range(8):
                        mmi = t.matmul(pst[:, :], ybuf[:, kc, :],
                                 wbig[:, kc, cg * 512:(cg + 1) * 512],
                                 start=(kc == 0), stop=(kc == 7))
                    mmi.then_inc(_smm, 1)
                p.add("tensor", mmo)
                s_mm.bump(1)
                p.add("vector", lambda v, cg=cg, pst=pst, w2=s_mm.v, xw=xw: (
                    v.wait_ge(_smm, w2), v.wait_ge(_sx, xw),
                    v.tensor_add(souo[:, cg * 512:(cg + 1) * 512], pst[:, :],
                                 sxt[:, cg * 512:(cg + 1) * 512])))
            p.emit()
            p.add("sync", lambda s, tt=tt, w2=s_o.v + 16: (
                s.dma_start(out=o_x[tt * 128:(tt + 1) * 128, :], in_=souo[:, :]).then_inc(_so, 16),
                s.wait_ge(_so, w2)))
            s_o.bump(16)
            p.emit()
    return nc


def inputs_l1f_attn(ins, core, xcur, layer):
    start = TLOC * core
    xh = np.zeros((EXT, C), np.float32)
    lo = start - H0
    src_lo = max(0, lo)
    xh[src_lo - lo:] = xcur[src_lo:start + TLOC]
    DH = 64
    half = np.arange(0, DH, 2, dtype=np.float64)
    inv_freq = 1.0 / 10000.0 ** (half / DH)
    tpos = np.arange(lo, start + TLOC, dtype=np.float64)
    freqs = tpos[:, None] * inv_freq[None, :]
    base_scale = (half + 0.4 * DH) / (1.4 * DH)
    power = (tpos - T // 2) / 128.0
    rsc = base_scale[None, :] ** power[:, None]
    cosf = np.cos(freqs); sinf = np.sin(freqs)
    def tab(m):
        m64 = np.concatenate([m, m], axis=1)
        return np.ascontiguousarray(np.tile(m64.T, (2, 1)))
    tqc_ = tab(cosf * rsc); tqs_ = tab(sinf * rsc)
    tkc_ = tab(cosf / rsc); tks_ = tab(sinf / rsc)
    mskA = np.zeros((128, 1024), np.float32)
    msk0 = np.zeros((128, 1024), np.float32)
    mskTb = np.zeros((128, 1024), np.float32)
    mskT0 = np.zeros((128, 1024), np.float32)
    for qt in range(2):
        for pp in range(128):
            i = qt * 128 + pp
            for kk in range(512):
                valid = (kk >= i) and (kk <= i + 256)
                first_invalid = kk < 256
                mskA[pp, qt * 512 + kk] = 0.0 if valid else -1e30
                msk0[pp, qt * 512 + kk] = (0.0 if (valid and not (core == 0 and first_invalid))
                                           else -1e30)
    for ktc in range(4):
        for pp in range(128):
            kk = ktc * 128 + pp
            for qq in range(256):
                valid = (kk >= qq) and (kk <= qq + 256)
                first_invalid = kk < 256
                col = ktc * 256 + qq
                mskTb[pp, col] = 1.0 if valid else 0.0
                mskT0[pp, col] = 1.0 if (valid and not (core == 0 and first_invalid)) else 0.0
    bind = np.zeros((128, 8, 16), np.float32)
    bind2 = np.zeros((16, 8, 128), np.float32)
    for kc in range(8):
        for pp in range(128):
            h = 2 * kc + pp // 64
            bind[pp, kc, h] = 1.0
            bind2[h, kc, pp] = 1.0
    erows = np.zeros((32, 32, 128), np.float32)
    for r in range(32):
        erows[r, r, :] = 1.0
    prm2 = np.zeros((128, 128), np.float32)
    for b in range(2):
        for i in range(32):
            prm2[b * 64 + 32 + i, b * 64 + i] = -1.0
            prm2[b * 64 + i, b * 64 + 32 + i] = 1.0
    return {
        "xh": f32c(xh),
        "wqkv": f32c(ins["L1_ffn_Wqkv"][layer]),
        "wo": f32c(ins["L1_ffn_Wo"][layer]),
        "qsc": f32c(np.tile(ins["L1_ffn_q_scale"][layer], 2)[:, None].repeat(8, 1)),
        "ksc": f32c(np.tile(ins["L1_ffn_k_scale"][layer], 2)[:, None].repeat(8, 1)),
        "tqc": f32c(tqc_), "tqs": f32c(tqs_), "tkc": f32c(tkc_), "tks": f32c(tks_),
        "mk0": msk0, "mkN": mskA, "mtb": mskTb, "mt0": mskT0,
        "bind": f32c(bind.reshape(128, 128)),
        "bind2": f32c(bind2.reshape(16, 8 * 128)),
        "erows": f32c(erows.reshape(32, 32 * 128)),
        "prm2": prm2,
        "ident": f32c(np.eye(128)),
    }



EXT2 = 2176

def build_l1f_ffn(final=False, fu=None):
    nc = fu["nc"] if fu is not None else bass.Bass(
        "TRN2", target_bir_lowering=False, detect_race_conditions=False)
    pfx = fu["pfx"] if fu is not None else ""
    xh = dram_in(nc, fu, "xh", [EXT2, C], F32)
    w1a = dram_in(nc, fu, "w1a", [C, 2730], F32)
    w1g = dram_in(nc, fu, "w1g", [C, 2730], F32)
    w2 = dram_in(nc, fu, "w2", [2816, C], F32)
    b1a = dram_in(nc, fu, "b1a", [128, 22], F32)
    b1g = dram_in(nc, fu, "b1g", [128, 22], F32)
    b2r = dram_in(nc, fu, "b2r", [1, C], F32)
    lng = dram_in(nc, fu, "lng", [128, 8], F32)
    lnb = dram_in(nc, fu, "lnb", [128, 8], F32)
    ident = dram_in(nc, fu, "ident", [128, 128], F32)
    if final:
        s2r = dram_in(nc, fu, "s2r", [TLOC, C], F32)
    o_x = dram_out(nc, fu, "o_x", [TLOC, C], F32)

    with ExitStack() as _st:
        E = _st.enter_context
        xfm = E(nc.sbuf_tensor(pfx + "xfm", [128, 8, EXT2], BF16))
        prodf = E(nc.sbuf_tensor(pfx + "prodf", [128, 22, TLOC], BF16))
        sw2 = E(nc.sbuf_tensor(pfx + "sw2", [128, 22, C], BF16))
        swa = E(nc.sbuf_tensor(pfx + "swa", [128, 8, 128], BF16))
        swb = E(nc.sbuf_tensor(pfx + "swb", [128, 8, 128], BF16))
        sid = E(nc.sbuf_tensor(pfx + "sid", [128, 128], BF16))
        sb1a = E(nc.sbuf_tensor(pfx + "sb1a", [128, 22], F32))
        sb1g = E(nc.sbuf_tensor(pfx + "sb1g", [128, 22], F32))
        sb2 = E(nc.sbuf_tensor(pfx + "sb2", [1, C], BF16))
        slng = E(nc.sbuf_tensor(pfx + "slng", [128, 8], F32))
        slnb = E(nc.sbuf_tensor(pfx + "slnb", [128, 8], F32))
        srow = E(nc.sbuf_tensor(pfx + "srow", [1, TLOC], F32))
        srow2 = E(nc.sbuf_tensor(pfx + "srow2", [1, TLOC], F32))
        srow3 = E(nc.sbuf_tensor(pfx + "srow3", [1, TLOC // 2], F32))
        sxt = E(nc.sbuf_tensor(pfx + "sxt", [128, C], F32))
        sbt = E(nc.sbuf_tensor(pfx + "sbt", [128, 128], BF16))
        souo = E(nc.sbuf_tensor(pfx + "souo", [128, C], F32))
        sst2 = E(nc.sbuf_tensor(pfx + "sst2", [128, C], F32))
        mcol = E(nc.sbuf_tensor(pfx + "mcol", [128, 1], F32))
        vcol = E(nc.sbuf_tensor(pfx + "vcol", [128, 1], F32))
        sgel = E(nc.sbuf_tensor(pfx + "sgel", [128, 512], BF16))
        ssq = sgel
        onesc = E(nc.sbuf_tensor(pfx + "onesc", [128, 1], BF16))
        onesr = E(nc.sbuf_tensor(pfx + "onesr", [1, 128], BF16))
        onesrf = E(nc.sbuf_tensor(pfx + "onesrf", [1, 128], F32))
        psA = E(nc.psum_tensor(pfx + "psA", [128, 512], F32))
        psB = E(nc.psum_tensor(pfx + "psB", [128, 512], F32))
        psR = E(nc.psum_tensor(pfx + "psR", [1, 512], F32))
        psR2 = E(nc.psum_tensor(pfx + "psR2", [1, 512], F32))
        psP1 = E(nc.psum_tensor(pfx + "psP1", [128, 512], F32))
        psP2 = E(nc.psum_tensor(pfx + "psP2", [128, 512], F32))
        psT = E(nc.psum_tensor(pfx + "psT", [128, 128], BF16))
        p, (_sx, _sg, _smm, _sev, _so), (s_x, s_g, s_mm, s_ev, s_o) = fu_start(nc, fu, E)

        def loads(g):
            g.dma_start(out=sw2[:, :, :],
                        in_=w2.rearrange("(k p) c -> p k c", p=128)).then_inc(_sg, 16)
            g.dma_start(out=sid[:, :], in_=ident).then_inc(_sg, 16)
            g.dma_start(out=sb2[:, :], in_=b2r).then_inc(_sg, 16)
        p.add("gpsimd", loads)
        p.add("sync", lambda s: (
            s.dma_start(out=sb1a[:, :], in_=b1a).then_inc(_sx, 16),
            s.dma_start(out=sb1g[:, :], in_=b1g).then_inc(_sx, 16),
            s.dma_start(out=slng[:, :], in_=lng).then_inc(_sx, 16),
            s.dma_start(out=slnb[:, :], in_=lnb).then_inc(_sx, 16)))
        s_g.bump(48); s_x.bump(64)
        p.add("vector", lambda v, w=s_x.v, wg=s_g.v: (
            v.wait_ge(_sx, w), v.wait_ge(_sg, wg),
            v.memset(onesr[:, :], 1.0), v.memset(onesrf[:, :], 1.0)))
        p.add("gpsimd", lambda g: g.memset(onesc[:, :], 1.0))
        p.emit()

        for tt in range(EXT2 // 128):
            p.add("sync", lambda s, tt=tt: s.dma_start(
                out=sxt[:, :], in_=xh[tt * 128:(tt + 1) * 128, :]).then_inc(_sx, 16))
            s_x.bump(16)
            p.add("vector", lambda v, w=s_x.v: v.wait_ge(_sx, w))
            p.emit()
            for kc in range(8):
                p.add("vector", lambda v, kc=kc: v.tensor_copy(
                    sbt[:, :], sxt[:, kc * 128:(kc + 1) * 128]))
                p.emit()
                p.add("tensor", lambda t, kc=kc: t.transpose(
                    psT[:, :], sbt[:, :], sid[:, :]).then_inc(_smm, 1))
                s_mm.bump(1)
                p.add("vector", lambda v, tt=tt, kc=kc, w=s_mm.v: (
                    v.wait_ge(_smm, w),
                    v.tensor_copy(xfm[:, kc, tt * 128:(tt + 1) * 128], psT[:, :])))
                p.emit()

        def shsl(kc, lo, hi):
            base = 128
            if kc < 4:
                return xfm[:, kc, base + lo:base + hi]
            return xfm[:, kc, base + lo - 1:base + hi - 1]

        for tg in range(4):
            def mmst(t, tg=tg):
                for kc in range(8):
                    mmi = t.matmul(psR[:, :], onesc[:, :], shsl(kc, tg * 512, (tg + 1) * 512),
                             start=(kc == 0), stop=(kc == 7))
                mmi.then_inc(_smm, 1)
            p.add("tensor", mmst)
            s_mm.bump(1)
            p.add("vector", lambda v, tg=tg, w=s_mm.v: (
                v.wait_ge(_smm, w),
                v.tensor_copy(srow[0:1, tg * 512:(tg + 1) * 512], psR[:, :])))
            p.emit()
            for kc in range(8):
                p.add("scalar", lambda s, kc=kc, tg=tg: s.activation(
                    ssq[:, :], shsl(kc, tg * 512, (tg + 1) * 512), AF.Square,
                    bias=0.0, scale=1.0))
                p.emit()
                p.add("tensor", lambda t, kc=kc, tg=tg: t.matmul(
                    psR2[:, :], onesc[:, :], ssq[:, :],
                    start=(kc == 0), stop=(kc == 7)).then_inc(_smm, 1))
                s_mm.bump(1)
                p.emit()
            p.add("vector", lambda v, tg=tg, w=s_mm.v: (
                v.wait_ge(_smm, w),
                v.tensor_copy(srow2[0:1, tg * 512:(tg + 1) * 512], psR2[:, :])))
            p.emit()

        def rowops(v):
            v.tensor_scalar(srow[:, :], srow[:, :], 1.0 / C, None, ALU.mult)
            v.tensor_scalar(srow2[:, :], srow2[:, :], 1.0 / C, None, ALU.mult)
            for hh in range(2):
                sl = slice(hh * (TLOC // 2), (hh + 1) * (TLOC // 2))
                v.tensor_tensor(srow3[:, :], srow[:, sl], srow[:, sl], ALU.mult)
                v.tensor_sub(srow2[:, sl], srow2[:, sl], srow3[:, :])
            v.tensor_scalar(srow2[:, :], srow2[:, :], 1.0, 1e-5, ALU.mult, ALU.add)
        p.add("vector", rowops)
        p.emit()
        p.add("scalar", lambda s: s.activation(srow2[:, :], srow2[:, :], AF.Sqrt,
                                               bias=0.0, scale=1.0))
        p.emit()
        p.add("vector", lambda v: (
            v.reciprocal(srow2[:, :], srow2[:, :]),
            v.tensor_mul(srow[:, :], srow[:, :], srow2[:, :]),
            v.tensor_scalar(srow[:, :], srow[:, :], -1.0, None, ALU.mult)))
        p.emit()

        for tg in (3, 2, 1, 0):
            def mmbc(t, tg=tg):
                t.matmul(psP1[:, :], onesrf[0:1, :], srow2[0:1, tg * 512:(tg + 1) * 512],
                         start=True, stop=True)
                t.matmul(psP2[:, :], onesrf[0:1, :], srow[0:1, tg * 512:(tg + 1) * 512],
                         start=True, stop=True).then_inc(_smm, 1)
            p.add("tensor", mmbc)
            s_mm.bump(1)

            def apl(v, tg=tg, w=s_mm.v):
                v.wait_ge(_smm, w)
                for kc in range(8):
                    v.tensor_tensor(sst2[:, 0:512], shsl(kc, tg * 512, (tg + 1) * 512),
                                    psP1[:, :], ALU.mult)
                    v.tensor_tensor(sst2[:, 0:512], sst2[:, 0:512], psP2[:, :], ALU.add)
                    v.tensor_scalar(xfm[:, kc, 128 + tg * 512:128 + (tg + 1) * 512],
                                    sst2[:, 0:512], slng[:, kc:kc + 1], slnb[:, kc:kc + 1],
                                    ALU.mult, ALU.add)
            p.add("vector", apl)
            p.emit()

        wps = ((swa, swb), (swa, swb))
        for jc in range(22):
            wa, wg_ = wps[jc % 2]
            jw = 128 if jc < 21 else 42
            p.add("gpsimd", lambda g, jc=jc, wa=wa, wg_=wg_, jw=jw: (
                g.dma_start(out=wa[:, :, 0:jw],
                            in_=w1a[:, jc * 128:jc * 128 + jw].rearrange("(k p) j -> p k j", p=128))
                .then_inc(_sg, 16),
                g.dma_start(out=wg_[:, :, 0:jw],
                            in_=w1g[:, jc * 128:jc * 128 + jw].rearrange("(k p) j -> p k j", p=128))
                .then_inc(_sg, 16)))
            s_g.bump(32)
            gw = s_g.v
            for tg in range(4):
                def mmh(t, jc=jc, tg=tg, wa=wa, wg_=wg_, jw=jw, gw=gw):
                    t.wait_ge(_sg, gw)
                    for kc in range(8):
                        t.matmul(psA[0:jw, :], wa[:, kc, 0:jw],
                                 xfm[:, kc, 128 + tg * 512:128 + (tg + 1) * 512],
                                 start=(kc == 0), stop=(kc == 7))
                    for kc in range(8):
                        mmi = t.matmul(psB[0:jw, :], wg_[:, kc, 0:jw],
                                 xfm[:, kc, 128 + tg * 512:128 + (tg + 1) * 512],
                                 start=(kc == 0), stop=(kc == 7))
                    mmi.then_inc(_smm, 1)
                p.add("tensor", mmh)
                s_mm.bump(1)
                p.add("scalar", lambda s, jc=jc, jw=jw, w=s_mm.v: (
                    s.wait_ge(_smm, w),
                    s.activation(sgel[0:jw, :], psB[0:jw, :], AF.Gelu,
                                 bias=sb1g[0:jw, jc:jc + 1], scale=1.0)))
                p.emit()
                p.add("vector", lambda v, jc=jc, tg=tg, jw=jw: (
                    v.scalar_tensor_tensor(prodf[0:jw, jc, tg * 512:(tg + 1) * 512],
                                           psA[0:jw, :], sb1a[0:jw, jc:jc + 1],
                                           sgel[0:jw, :], ALU.add, ALU.mult)))
                p.emit()

        for tt in range(16):
            p.add("sync", lambda s, tt=tt: s.dma_start(
                out=sxt[:, :], in_=xh[128 + tt * 128:128 + (tt + 1) * 128, :]).then_inc(_sx, 16))
            s_x.bump(16)
            xw = s_x.v
            for cg in range(2):
                pst = (psA, psB)[cg]

                def mmo(t, tt=tt, cg=cg, pst=pst):
                    for kc in range(22):
                        jw = 128 if kc < 21 else 42
                        t.matmul(pst[:, :], prodf[0:jw, kc, tt * 128:(tt + 1) * 128],
                                 sw2[0:jw, kc, cg * 512:(cg + 1) * 512],
                                 start=(kc == 0), stop=False)
                    t.matmul(pst[:, :], onesr[0:1, 0:128],
                             sb2[0:1, cg * 512:(cg + 1) * 512],
                             start=False, stop=True).then_inc(_smm, 1)
                p.add("tensor", mmo)
                s_mm.bump(1)
                p.add("vector", lambda v, cg=cg, pst=pst, w=s_mm.v, xw=xw: (
                    v.wait_ge(_smm, w), v.wait_ge(_sx, xw),
                    v.tensor_add(souo[:, cg * 512:(cg + 1) * 512], pst[:, :],
                                 sxt[:, cg * 512:(cg + 1) * 512])))
            p.emit()
            if final:
                p.add("sync", lambda s, tt=tt: s.dma_start(
                    out=sxt[:, :], in_=s2r[tt * 128:(tt + 1) * 128, :]).then_inc(_sx, 16))
                s_x.bump(16)
                p.add("vector", lambda v, w=s_x.v: (
                    v.wait_ge(_sx, w),
                    v.tensor_add(souo[:, :], souo[:, :], sxt[:, :])))
                p.emit()
                emit_ln_tm(p, nc, souo, sst2, mcol, vcol)
            p.add("sync", lambda s, tt=tt, w=s_o.v + 16: (
                s.dma_start(out=o_x[tt * 128:(tt + 1) * 128, :], in_=souo[:, :]).then_inc(_so, 16),
                s.wait_ge(_so, w)))
            s_o.bump(16)
            p.emit()
    return nc


def inputs_l1f_ffn(ins, core, xcur, layer, s2=None):
    start = TLOC * core
    xh = np.zeros((EXT2, C), np.float32)
    lo = start - 128
    src_lo = max(0, lo)
    xh[src_lo - lo:] = xcur[src_lo:start + TLOC]
    W1 = ins["L1_ffn_W1"][layer]
    b1 = ins["L1_ffn_b1"][layer]
    DIN = 2730

    def pack22(vec):
        out = np.zeros((128, 22), np.float32)
        for jc in range(22):
            jw = 128 if jc < 21 else 42
            out[0:jw, jc] = vec[jc * 128:jc * 128 + jw]
        return out
    d = {
        "xh": f32c(xh),
        "w1a": f32c(W1[:, :DIN]),
        "w1g": f32c(W1[:, DIN:]),
        "w2": f32c(np.concatenate([ins["L1_ffn_W2"][layer],
                                   np.zeros((2816 - 2730, C), np.float32)], axis=0)),
        "b1a": pack22(b1[:DIN]),
        "b1g": pack22(b1[DIN:]),
        "b2r": f32c(ins["L1_ffn_b2"][layer][None, :]),
        "lng": f32c(ins["L1_ffn_ln_g"][layer].reshape(8, 128).T),
        "lnb": f32c(ins["L1_ffn_ln_b"][layer].reshape(8, 128).T),
        "ident": f32c(np.eye(128)),
    }
    if s2 is not None:
        d["s2r"] = f32c(s2[start:start + TLOC])
    return d




def build_fused():
    nc = bass.Bass("TRN2", target_bir_lowering=False, detect_race_conditions=False)
    ALLC = [list(range(NCORE))]

    def D(nm, sh, dt):
        return nc.dram_tensor(nm, sh, dt).ap()

    t_u = D("t_u", [DSL, T], BF16)
    t_z = D("t_z", [DSL, T], BF16)
    t_xdblp = D("t_xdblp", [97, T], F32)
    t_xdar = D("t_xdar", [1, 97, T], F32)
    t_ycat = D("t_ycat", [512, T], BF16)
    t_ycin8 = D("t_ycin8", [NCORE, 512, TLOC], BF16)
    t_ycown = D("t_ycown", [4096, TLOC], BF16)
    t_svec = D("t_svec", [1, T], F32)
    t_svr = D("t_svr", [1, TLOC], F32)
    t_s0 = D("t_s0", [TLOC, C], F32)
    t_qf = D("t_qf", [128, TLOC], BF16)
    t_kf = D("t_kf", [128, TLOC], BF16)
    t_kfag = D("t_kfag", [NCORE, 128, TLOC], BF16)
    t_vt = D("t_vt", [TLOC, 128], BF16)
    t_vtag = D("t_vtag", [T, 128], BF16)
    t_w = D("t_w", [TLOC, 16], F32)
    t_qn = D("t_qn", [16, 128], F32)
    t_km = D("t_km", [1, 1], F32)
    t_kmag = D("t_kmag", [8, 1], F32)
    t_s1 = D("t_s1", [TLOC, C], F32)
    t_s1ft = D("t_s1ft", [C, TLOC], BF16)
    t_sfstg = D("t_sfstg", [NCORE, 128, TLOC], BF16)
    t_yn = D("t_yn", [T, 128], BF16)
    t_ynstage = D("t_ynstage", [NCORE, TLOC, 128], BF16)
    t_ynown = D("t_ynown", [TLOC, C], BF16)
    t_s2 = D("t_s2", [TLOC, C], F32)
    t_dx = [D(f"t_dx{k}", [TLOC, C], F32) for k in range(11)]
    t_rsin = D("t_rsin", [4096, C], BF16)
    t_hxout = D("t_hxout", [512, C], BF16)
    t_rsinf = D("t_rsinf", [1024, C], BF16)
    t_hxoutf = D("t_hxoutf", [128, C], BF16)
    t_xha = D("t_xha", [EXT, C], F32)
    t_xhf = D("t_xhf", [EXT2, C], F32)

    g_dofs = nc.dram_tensor("g_dofs", [1, 4], I32, kind="ExternalInput").ap()
    g_m8 = nc.dram_tensor("g_m8", [128, 8], F32, kind="ExternalInput").ap()
    g_idf = nc.dram_tensor("g_idf", [128, 128], F32, kind="ExternalInput").ap()
    o_x = nc.dram_tensor("o_x", [TLOC, C], F32, kind="ExternalOutput").ap()

    with ExitStack() as _st:
        E = _st.enter_context
        hs = tuple(E(nc.semaphore(f"fs{i}")) for i in range(5))
        _sx, _sg, _smm, _sev, _so = hs
        ctrs = tuple(Sem(h) for h in hs)
        s_x, s_g, s_mm, s_ev, s_o = ctrs
        p = Plan(nc)
        sofs = E(nc.sbuf_tensor("g_sofs", [1, 4], I32))
        sm8 = E(nc.sbuf_tensor("g_sm8", [128, 8], F32))

        p.add("sync", lambda s: (
            s.dma_start(out=sofs[:, :], in_=g_dofs).then_inc(_sx, 16),
            s.dma_start(out=sm8[:, :], in_=g_m8).then_inc(_sx, 16)))
        s_x.bump(32)
        p.add("gpsimd", lambda g, w=s_x.v: g.wait_ge(_sx, w))
        p.emit()
        nc.all_engine_barrier(sem_only=True)

        def FU(px, ext):
            return {"nc": nc, "pfx": px, "p": p, "sems": hs, "ctrs": ctrs, "ext": ext}

        def bar():
            p.emit()
            nc.all_engine_barrier(sem_only=True)

        def ag(src_ap, dst_ap):
            s_ev.bump(1)

            def f(g, w=s_ev.v):
                g.collective_compute("AllGather", ALU.bypass, replica_groups=ALLC,
                                     ins=[src_ap.opt()], outs=[dst_ap.opt()]).then_inc(_sev, 1)
                g.wait_ge(_sev, w)
            p.add("gpsimd", f)
            bar()

        _gctr = [0]

        def a2a(src_ap, dst_ap):
            s_ev.bump(1)

            def f(g, w=s_ev.v):
                with allow_ici_all_to_all():
                    g.collective_compute("AllToAll", ALU.bypass, replica_groups=ALLC,
                                         ins=[src_ap.opt()], outs=[dst_ap.opt()]).then_inc(_sev, 1)
                g.wait_ge(_sev, w)
            p.add("gpsimd", f)
            bar()

        def dyn_glue(fn, ndma, oidx):
            s_g.bump(16 * ndma)
            _gctr[0] += 1

            def f(g, w=s_g.v, nm=f"gdr{_gctr[0]}"):
                with g.register(nm) as r0:
                    g.reg_load(r0, sofs[0:1, oidx:oidx + 1])
                    fn(g, g.snap(r0))
                g.wait_ge(_sg, w)
            p.add("gpsimd", f)
            bar()

        build_l0a_pre(fu=FU("a1", {"o_u": t_u, "o_z": t_z, "o_xdblp": t_xdblp}))
        bar()
        s_ev.bump(1)

        def far(g, w=s_ev.v):
            g.collective_compute("AllReduce", ALU.add, replica_groups=ALLC,
                                 ins=[t_xdblp.opt()], outs=[t_xdar.opt()]).then_inc(_sev, 1)
            g.wait_ge(_sev, w)
        p.add("gpsimd", far)
        bar()
        build_l0a_scan(fu=FU("a3", {"xdblp": t_xdar, "u_in": t_u, "z_in": t_z,
                                    "o_ycat": t_ycat, "o_svec": t_svec}))
        bar()
        s_g.bump(16 * NCORE)
        p.add("gpsimd", lambda g, w=s_g.v: (
            [g.dma_start(out=t_ycin8[j],
                         in_=t_ycat[:, j * TLOC:(j + 1) * TLOC]).then_inc(_sg, 16)
             for j in range(NCORE)],
            g.wait_ge(_sg, w)))
        bar()
        a2a(t_ycin8, t_ycown)

        def g2b(g, tok):
            g.dma_start(out=t_svr, in_=t_svec[0:1, bass.ds(tok, TLOC)]).then_inc(_sg, 16)
        dyn_glue(g2b, 1, 0)
        build_l0a_out(fu=FU("a4", {"ycat": t_ycown, "svr": t_svr, "o_s": t_s0}))
        bar()

        build_l0f_qkv(fu=FU("b1", {"ss": t_s0, "o_qf": t_qf, "o_kf": t_kf, "o_vt": t_vt,
                                   "o_w": t_w, "o_qn": t_qn, "o_km": t_km}))
        bar()
        ag(t_kf, t_kfag)
        ag(t_vt, t_vtag)
        ag(t_km, t_kmag)
        build_l0f_attn(fu=FU("b2", {
            "kf": t_kfag.rearrange("c p t -> p c t"),
            "vt": t_vtag, "qf": t_qf,
            "qn": t_qn.rearrange("(o a) b -> o (a b)", o=1),
            "kms": t_kmag, "wrt": t_w, "ss": t_s0, "o_s": t_s1}))
        bar()

        with ExitStack() as _gs:
            gE = _gs.enter_context
            stile = gE(nc.sbuf_tensor("g_trt", [128, C], F32))
            sid = gE(nc.sbuf_tensor("g_trid", [128, 128], F32))
            sout = gE(nc.sbuf_tensor("g_tro", [128, 8, 128], BF16))
            pstr = gE(nc.psum_tensor("g_trp", [128, 128], F32))
            s_x.bump(16)
            p.add("sync", lambda s: s.dma_start(out=sid[:, :], in_=g_idf).then_inc(_sx, 16))
            p.emit()
            for tt in range(16):
                s_x.bump(16)
                xw = s_x.v
                p.add("sync", lambda s, tt=tt, w=s_o.v: (
                    s.wait_ge(_so, w),
                    s.dma_start(out=stile[:, :],
                                in_=t_s1[tt * 128:(tt + 1) * 128, :]).then_inc(_sx, 16)))
                for f in range(8):
                    s_mm.bump(1)

                    def mmt(t, f=f, xw=xw, ew=s_ev.v):
                        t.wait_ge(_sx, xw)
                        t.wait_ge(_sev, ew)
                        t.transpose(pstr[:, :], stile[:, f * 128:(f + 1) * 128],
                                    sid[:, :]).then_inc(_smm, 1)
                    p.add("tensor", mmt)
                    s_ev.bump(1)
                    p.add("vector", lambda v, f=f, w=s_mm.v: (
                        v.wait_ge(_smm, w),
                        v.tensor_copy(sout[:, f, :], pstr[:, :]).then_inc(_sev, 1)))
                    p.emit()
                s_o.bump(16)
                p.add("sync", lambda s, tt=tt, w=s_ev.v, wo=s_o.v: (
                    s.wait_ge(_sev, w),
                    s.dma_start(
                        out=t_s1ft.rearrange("(f p) t -> p f t", p=128)[:, :, tt * 128:(tt + 1) * 128],
                        in_=sout[:, :, :]).then_inc(_so, 16),
                    s.wait_ge(_so, wo)))
                p.emit()
        bar()
        a2a(t_s1ft, t_sfstg)
        build_l1a_ret(fu=FU("c1", {"sfm": t_sfstg.rearrange("r p t -> p r t"), "o_yn": t_yn}))
        bar()
        a2a(t_yn, t_ynstage)
        s_g.bump(16 * NCORE)
        p.add("gpsimd", lambda g, w=s_g.v: (
            [g.dma_start(out=t_ynown[:, 128 * h:128 * (h + 1)],
                         in_=t_ynstage[h]).then_inc(_sg, 16) for h in range(NCORE)],
            g.wait_ge(_sg, w)))
        bar()
        build_l1a_out(fu=FU("c2", {"sfT": t_s1ft, "ss": t_s1, "yn": t_ynown, "o_s": t_s2}))
        bar()

        def halo_glue(xp, sub, gi):
            HR = 4 if sub == 0 else 1
            rsin = t_rsin if sub == 0 else t_rsinf
            rsout = t_hxout if sub == 0 else t_hxoutf
            with ExitStack() as _hs:
                hE = _hs.enter_context
                tb = hE(nc.sbuf_tensor(f"g_tb{gi}", [128, HR * C], F32))
                tb8 = hE(nc.sbuf_tensor(f"g_tc{gi}", [128, NCORE, HR * C], BF16))
                s_x.bump(16)
                p.add("sync", lambda s: s.dma_start(
                    out=tb[:, :],
                    in_=xp[TLOC - 128 * HR:TLOC, :].rearrange("(k p) c -> p k c", p=128))
                    .then_inc(_sx, 16))
                p.emit()
                s_ev.bump(NCORE)
                p.add("vector", lambda v, w=s_x.v: (
                    v.wait_ge(_sx, w),
                    [v.tensor_tensor(tb8[:, k, :], tb[:, :],
                                     sm8[:, k:k + 1].to_broadcast([128, HR * C]),
                                     ALU.mult).then_inc(_sev, 1) for k in range(NCORE)]))
                s_o.bump(16 * NCORE)
                p.add("sync", lambda s, w=s_ev.v, wo=s_o.v: (
                    s.wait_ge(_sev, w),
                    [s.dma_start(
                        out=rsin[128 * HR * k:128 * HR * (k + 1), :]
                        .rearrange("(j p) c -> p j c", p=128),
                        in_=tb8[:, k, :]).then_inc(_so, 16) for k in range(NCORE)],
                    s.wait_ge(_so, wo)))
                p.emit()
                bar()
            s_ev.bump(1)

            def frs(g, w=s_ev.v):
                g.collective_compute("ReduceScatter", ALU.add, replica_groups=ALLC,
                                     ins=[rsin.opt()], outs=[rsout.opt()]).then_inc(_sev, 1)
                g.wait_ge(_sev, w)
            p.add("gpsimd", frs)
            bar()
            HAL, xh = (H0, t_xha) if sub == 0 else (128, t_xhf)
            s_g.bump(32)

            def fst(g, w=s_g.v, HAL=HAL, xh=xh, rsout=rsout):
                g.dma_start(out=xh[0:HAL, :], in_=rsout).then_inc(_sg, 16)
                g.dma_start(out=xh[HAL:HAL + TLOC, :], in_=xp).then_inc(_sg, 16)
                g.wait_ge(_sg, w)
            p.add("gpsimd", fst)
            bar()

        xprev = t_s2
        for L in range(6):
            for sub in range(2):
                k = 2 * L + sub
                last = (k == 11)
                halo_glue(xprev, sub, k)
                xout = o_x if last else t_dx[k]
                if sub == 0:
                    build_l1f_attn(fu=FU(f"da{L}", {"xh": t_xha, "o_x": xout}))
                else:
                    ext = {"xh": t_xhf, "o_x": xout}
                    if last:
                        ext["s2r"] = t_s2
                    build_l1f_ffn(final=last, fu=FU(f"df{L}", ext))
                bar()
                xprev = xout
    return nc


def inputs_fused(ins, core):
    m = {}

    def put(px, d, drop=()):
        for kk, vv in d.items():
            if kk not in drop:
                m[px + kk] = vv
    zf = np.zeros((NCORE, 97, T), np.float32)
    zu = np.zeros((2048, T), BFNP)
    zyc = np.zeros((4096, T), BFNP)
    zsv = np.zeros((1, T), np.float32)
    zs = np.zeros((T, C), np.float32)
    zb1 = [{"o_qf": np.zeros((128, TLOC), BFNP), "o_qn": np.zeros((16, 128), np.float32),
            "o_km": np.zeros((1, 1), np.float32), "o_w": np.zeros((TLOC, 16), np.float32)}
           for _ in range(NCORE)]
    put("a1", inputs_l0a_pre(ins, core))
    put("a3", inputs_l0a_scan(ins, core, zf, zu, zu), drop=("xdblp", "u_in", "z_in"))
    d4 = inputs_l0a_out(ins, core, zyc, zsv)
    perm = np.concatenate([np.concatenate([r * 256 + np.arange(256),
                                           2048 + r * 256 + np.arange(256)])
                           for r in range(NCORE)])
    d4["w1T"] = np.ascontiguousarray(d4["w1T"][perm])
    put("a4", d4, drop=("ycat", "svr"))
    put("b1", inputs_l0f_qkv(ins, core, zs), drop=("ss",))
    put("b2", inputs_l0f_attn(ins, core, np.zeros((128, T), BFNP),
                              np.zeros((T, 128), BFNP), zb1, zs),
        drop=("kf", "vt", "qf", "qn", "kms", "wrt", "ss"))
    put("c1", inputs_l1a_ret(ins, core, zs), drop=("sfm",))
    put("c2", inputs_l1a_out(ins, core, zs, np.zeros((T, C), BFNP)),
        drop=("sfT", "ss", "yn"))
    for L in range(6):
        put(f"da{L}", inputs_l1f_attn(ins, core, zs, L), drop=("xh",))
        put(f"df{L}", inputs_l1f_ffn(ins, core, zs, L, s2=zs if L == 5 else None),
            drop=("xh", "s2r"))
    prev = (core - 1) % NCORE
    m["g_dofs"] = np.array([[core * TLOC, core * 4096 * TLOC,
                             core * NCORE * 128 * TLOC, core * TLOC * 128]], np.int32)
    m8 = np.zeros((128, 8), np.float32)
    if core != NCORE - 1:
        m8[:, (core + 1) % NCORE] = 1.0
    m["g_m8"] = m8
    m["g_idf"] = np.eye(128, dtype=np.float32)
    return m


def kernel(**inputs):
    ins = inputs
    nc = prog("fused", build_fused)
    maps = [inputs_fused(ins, c) for c in range(NCORE)]
    res = run_bass_kernel_spmd(nc, maps, core_ids=list(range(NCORE))).results
    xcur = np.concatenate([np.asarray(r["o_x"]) for r in res], axis=0)
    return np.ascontiguousarray(xcur, dtype=np.float32).reshape(1, T, C)
